# version 14
# baseline (speedup 1.0000x reference)
import numpy as np
import ml_dtypes
import concourse.bass as bass
import concourse.mybir as mybir
from concourse.bass_utils import run_bass_kernel_spmd

F32 = mybir.dt.float32
F16 = mybir.dt.float16
BF16 = mybir.dt.bfloat16
AF = mybir.ActivationFunctionType
ALU = mybir.AluOpType
AX = mybir.AxisListType

D = 1024
S = 4096
TO = 2048
LATD = 256
HID = 4096
NI = 512
MAGIC = 1152.0
EPS = 1e-5
KB = 1024
ENGS = ("pe", "dv", "ac", "sp", "gp")

SEMS = ("gWA", "gWL", "gWB", "gX0", "gX1", "sB", "sWS0", "sWS1", "sWL0", "sWL1",
        "sWQ0", "sWQ1", "sXT0", "sXT1", "sWT0", "sWT1", "sX320", "sX321", "sXS0", "sXS1",
        "sH0", "sH1", "sH2", "sW10", "sW11", "sW2", "sXL0", "sXL1",
        "sHR0", "sHR1", "sHR2", "sHR3", "sY0", "sY1", "sCC", "sC2", "pe", "dv",
        "ac", "gc")


class Sch:
    def __init__(self, nc):
        self.nc = nc
        self.q = {e: [] for e in ENGS}
        self.sems = {}
        self.wm = {}

    def eng(self, e):
        nc = self.nc
        return {"pe": nc.tensor, "dv": nc.vector, "ac": nc.scalar,
                "sp": nc.sync, "gp": nc.gpsimd}[e]

    def mksem(self, name, es):
        self.sems[name] = [es.enter_context(self.nc.semaphore(name)), 0]

    def n(self, name):
        return self.sems[name][1]

    def wait(self, e, name, val):
        if val is None or val <= 0:
            return
        key = (e, name)
        if self.wm.get(key, 0) >= val:
            return
        self.wm[key] = val
        h = self.sems[name][0]
        eng = self.eng(e)
        self.q[e].append(lambda eng=eng, h=h, val=val: eng.wait_ge(h, val))

    def do(self, e, fn, inc=None, k=1):
        if inc is None:
            self.q[e].append(fn)
            return None
        rec = self.sems[inc]
        rec[1] += k
        h = rec[0]
        self.q[e].append(lambda fn=fn, h=h, k=k: fn().then_inc(h, k))
        return rec[1]

    def emit(self, block):
        for e in ENGS:
            dec = {"pe": block.tensor, "dv": block.vector, "ac": block.scalar,
                   "sp": block.sync, "gp": block.gpsimd}[e]
            ops = self.q[e]

            def body(_eng, ops=ops):
                for f in ops:
                    f()
            dec(body)


def build_nc(r0, cc_ranks=8):
    nc = bass.Bass("TRN2", target_bir_lowering=False)

    d_x = nc.dram_tensor("xfull", [S, D], F32, kind="ExternalInput").ap()
    d_Wq = nc.dram_tensor("Wq", [D, D], F32, kind="ExternalInput").ap()
    d_Wl = nc.dram_tensor("Wl", [D, LATD], F32, kind="ExternalInput").ap()
    d_Wk = nc.dram_tensor("Wk", [LATD, D], F32, kind="ExternalInput").ap()
    d_Wv = nc.dram_tensor("Wv", [LATD, D], F32, kind="ExternalInput").ap()
    d_Wo = nc.dram_tensor("Wo", [D, D], F32, kind="ExternalInput").ap()
    d_bcat = nc.dram_tensor("bcat", [32, 128], F32, kind="ExternalInput").ap()
    d_bo = nc.dram_tensor("bo", [1, D], F32, kind="ExternalInput").ap()
    d_w1 = nc.dram_tensor("w1", [HID, D], F32, kind="ExternalInput").ap()
    d_b1 = nc.dram_tensor("b1", [32, 128], F32, kind="ExternalInput").ap()
    d_w2 = nc.dram_tensor("w2", [D, HID], F32, kind="ExternalInput").ap()
    d_b2 = nc.dram_tensor("b2", [1, D], F32, kind="ExternalInput").ap()
    d_idb = nc.dram_tensor("identb", [128, 128], BF16, kind="ExternalInput").ap()
    d_idf = nc.dram_tensor("identf", [128, 128], F32, kind="ExternalInput").ap()
    d_y = nc.dram_tensor("y", [TO, D], F32, kind="ExternalOutput").ap()

    d_hst = nc.dram_tensor("h_stash", [32, 128, TO], F32).ap()
    d_xln = nc.dram_tensor("xln_stash", [TO, D], F32).ap()
    d_w1q = nc.dram_tensor("w1q_dram", [HID, D], BF16).ap()
    d_w2q = nc.dram_tensor("w2q_dram", [D, HID], BF16).ap()
    d_w1qT = nc.dram_tensor("w1qT_dram", [32, 128, 8, 128], BF16).ap()
    d_w2qT = nc.dram_tensor("w2qT_dram", [32, 128, D], BF16).ap()
    d_cc1i = nc.dram_tensor("cc1i", [1, 8], F32).ap()
    d_cc1o = nc.dram_tensor("cc1o", [1, 8], F32).ap()
    d_cc2i = nc.dram_tensor("cc2i", [1, 8], F32).ap()
    d_cc2o = nc.dram_tensor("cc2o", [1, 8], F32).ap()

    AT = nc.alloc_sbuf_tensor_at
    po = [(int(nc.sbuf_base) + 63) & ~63]

    def perm(name, shape, dtype):
        nbytes = int(np.prod(shape[1:])) * mybir.dt.size(dtype)
        t = AT(name, shape, dtype, offset=po[0])
        po[0] = (po[0] + nbytes + 31) & ~31
        return t

    c_idb = perm("c_idb", [128, 128], BF16)
    c_idf = perm("c_idf", [128, 128], F32)
    c_ball = perm("c_ball", [128, 32], F32)
    c_bvb = perm("c_bvb", [128, 8], BF16)
    c_b1r = perm("c_b1r", [128, 32], F32)
    c_b1s = perm("c_b1s", [128, 32], F32)
    c_bob = perm("c_bob", [1, D], BF16)
    c_b2b = perm("c_b2b", [1, D], BF16)
    c_1f = perm("c_1f", [1, 128], F32)
    c_sel = perm("c_sel", [128, 128], F32)
    c_1b = perm("c_1b", [1, 128], BF16)
    c_sc = perm("c_sc", [1, 16], F32)
    c_s8 = perm("c_s8", [1, 8], F32)
    c_wred = perm("c_wred", [128, 2], F32)
    c_wr1 = perm("c_wr1", [128, 1], F32)
    c_wr2 = perm("c_wr2", [128, 1], F32)
    c_sxr = perm("c_sxr", [128, 1], F32)
    c_s1 = perm("c_s1", [128, 1], F32)
    c_shr = perm("c_shr", [128, 1], F32)
    c_s2 = perm("c_s2", [128, 1], F32)
    c_eps = perm("c_eps", [128, 1], F32)
    c_mag = perm("c_mag", [128, 1], F32)
    c_rc = perm("c_rc", [1, 4], F32)
    c_stage = perm("c_stage", [32, 128], F32)

    PB = (po[0] + 63) & ~63

    def at(name, shape, dtype, off_kb):
        return AT(name, shape, dtype, offset=PB + (int(off_kb * KB) & ~31))

    t_lat = at("t_lat", [128, 2, S], BF16, 0)
    t_q = at("t_q", [128, 8, TO], BF16, 16)
    t_o = at("t_o", [128, 8, TO], BF16, 48)
    t_P = at("t_P", [128, 8, 2, NI], BF16, 80)
    t_kg = at("t_kg", [128, 2, S], BF16, 112)
    t_vg = at("t_vg", [128, 32, 4, 65], BF16, 128)
    t_Wk = at("t_Wk", [128, 2, D], BF16, 145)
    t_Wv = at("t_Wv", [128, 2, D], BF16, 149)
    t_Wo = at("t_Wo", [128, 8, D], BF16, 153)
    t_wld = [at(f"t_wld{i}", [128, D], F32, 169 + 4 * i) for i in range(2)]
    t_wf16 = at("t_wf16", [128, D], F16, 177)
    t_wqb = [at(f"t_wqb{i}", [128, D], BF16, 179 + 2 * i) for i in range(2)]
    t_wtr = [at(f"t_wtr{i}", [128, 2048], BF16, 183 + 4 * i) for i in range(2)]
    t_slr = at("t_slr", [128, NI], F32, 191)
    t_slb = at("t_slb", [128, NI], F32, 195)
    t_sll = at("t_sll", [128, NI], F32, 96)
    t_slx = at("t_slx", [128, NI], F32, 98)
    t_bo32 = at("t_bo32", [1, D], F32, 112)
    t_b2r = at("t_b2r", [1, D], F32, 116)

    t_Wq = at("t_Wq", [128, 8, D], BF16, 48)
    t_Wl = at("t_Wl", [128, 8, LATD], BF16, 64)
    t_x32b = [at(f"t_x32b{i}", [128, D], F32, 68 + 4 * i) for i in range(2)]
    t_xb = [at(f"t_xb{i}", [128, D], BF16, 76 + 2 * i) for i in range(2)]
    t_xf = [at(f"t_xf{i}", [128, 8, 128], BF16, 80 + 2 * i) for i in range(2)]
    t_wm = at("t_wm", [128, 64], F32, 84)

    t_x32 = [at(f"t_x32{i}", [128, D], F32, 112 + 4 * i) for i in range(2)]
    t_sum = [at(f"t_sum{i}", [128, D], F32, 120 + 4 * i) for i in range(2)]
    t_bn = [at(f"t_bn{i}", [128, 12], F32, 128 + 0.1 * i) for i in range(2)]
    t_bnx = [at(f"t_bnx{i}", [128, 16], F32, 128.2 + 0.1 * i) for i in range(2)]
    t_rm = at("t_rm", [128, 16], F32, 128.75)

    t_xq = at("t_xq", [128, 8, TO], BF16, 0)
    t_w2s = at("t_w2s", [128, 32, D], BF16, 32)
    t_w1s = [at(f"t_w1s{i}", [128, 8, 128], BF16, 96 + 2 * i) for i in range(2)]
    t_ht = [at(f"t_ht{i}", [128, NI], F32, 100 + 2 * i) for i in range(3)]
    t_hm = at("t_hm", [128, 128], F32, 106)
    t_xl5 = [at(f"t_xl5{i}", [128, D], F32, 107 + 4 * i) for i in range(2)]
    t_qf16 = at("t_qf16", [128, D], F16, 115)
    t_f16b = [at(f"t_f16b{i}", [128, HID], F16, [117, 107][i]) for i in range(2)]

    t_xf5 = [at(f"t_xf5{i}", [128, 8, 128], F32, 125 + 4 * i) for i in range(16)]
    t_w2l = [at(f"t_w2l{i}", [128, HID], BF16, 32 + 8 * i) for i in range(2)]
    t_w2x = [at(f"t_w2x{i}", [128, 32, 128], BF16, [183, 191][i]) for i in range(2)]
    t_xr = [at(f"t_xr{i}", [128, D], F32, 129 + 4 * i) for i in range(2)]
    t_hq = [at(f"t_hq{i}", [128, 32, 128], BF16, 125 + 8 * i) for i in range(2)]
    t_hf = ([at(f"t_hf{i}", [128, 32, 128], F32, 141 + 16 * i) for i in range(2)]
            + [at(f"t_hfb{i}", [128, 32, 128], F32, 0 + 16 * i) for i in range(2)])
    t_xl6 = [at(f"t_xl6{i}", [128, D], F32, 173 + 4 * i) for i in range(2)]
    t_out = [at(f"t_out{i}", [128, D], F32, 181 + 4 * i) for i in range(2)]
    t_bn6 = [at(f"t_bn6{i}", [128, 12], F32, 189 + 0.1 * i) for i in range(2)]
    t_bx6 = [at(f"t_bx6{i}", [128, 16], F32, 189.2 + 0.1 * i) for i in range(2)]

    pA = nc.alloc_psum_tensor("pA", [128, 1536], F32)
    pB = nc.alloc_psum_tensor("pB", [128, 1536], F32)
    pC = nc.alloc_psum_tensor("pC", [128, 512], F32)
    pD = nc.alloc_psum_tensor("pD", [128, 512], F32)

    sch = Sch(nc)
    from contextlib import ExitStack
    es = ExitStack()
    for s in SEMS:
        sch.mksem(s, es)
    block = es.enter_context(nc.Block())

    W = sch.wait
    DO = sch.do
    T = nc.tensor
    V = nc.vector
    A = nc.scalar
    SP = nc.sync
    GP = nc.gpsimd

    DO("sp", lambda: SP.dma_start(out=c_idb[:, :], in_=d_idb), "sB", 16)
    DO("sp", lambda: SP.dma_start(out=c_idf[:, :], in_=d_idf), "sB", 16)
    DO("sp", lambda: SP.dma_start(out=c_stage[:, :], in_=d_bcat), "sB", 16)
    DO("sp", lambda: SP.dma_start(out=t_bo32[0:1, :], in_=d_bo), "sB", 16)
    tok_b0 = DO("sp", lambda: SP.dma_start(out=t_b2r[0:1, :], in_=d_b2), "sB", 16)

    DO("dv", lambda: V.memset(c_eps[:, :], EPS))
    DO("dv", lambda: V.memset(c_mag[:, :], MAGIC))
    DO("dv", lambda: V.memset(c_1f[0:1, :], 1.0))
    t_sel0 = DO("dv", lambda: V.memset(c_sel[:, :], 0.0), "dv")
    W("dv", "dv", t_sel0)
    DO("dv", lambda: V.memset(c_sel[0:1, 0:64], 1.0))
    DO("dv", lambda: V.memset(c_sel[64:65, 64:128], 1.0))
    DO("dv", lambda: V.memset(t_sll[:, :], 1.0))
    DO("dv", lambda: V.memset(c_1b[0:1, :], 1.0))
    tok_ones = DO("dv", lambda: V.memset(t_vg[:, :, :, 64], 1.0), "dv")

    tok_wL = DO("gp", lambda: GP.dma_start(
        out=t_Wl[:, :, :], in_=d_Wl.rearrange("(kc p) m -> p kc m", p=128)), "gWL", 16)
    tok_wA = DO("gp", lambda: GP.dma_start(
        out=t_Wq[:, :, :], in_=d_Wq.rearrange("(kc p) m -> p kc m", p=128)), "gWA", 16)
    DO("gp", lambda: GP.dma_start(
        out=t_Wk[:, :, :], in_=d_Wk.rearrange("(kc p) m -> p kc m", p=128)), "gWB", 16)
    DO("gp", lambda: GP.dma_start(
        out=t_Wv[:, :, :], in_=d_Wv.rearrange("(kc p) m -> p kc m", p=128)), "gWB", 16)
    tok_wall = DO("gp", lambda: GP.dma_start(
        out=t_Wo[:, :, :], in_=d_Wo.rearrange("(kc p) m -> p kc m", p=128)), "gWB", 16)

    W("pe", "sB", tok_b0)
    tok_bT = DO("pe", lambda: T.transpose(pC[:, 0:32], c_stage[0:32, :],
                                          c_idf[0:32, 0:32]), "pe")
    W("dv", "pe", tok_bT)
    tok_ballc = DO("dv", lambda: V.tensor_copy(c_ball[:, :], pC[:, 0:32]), "dv")
    W("dv", "dv", tok_ballc)
    tok_ball = DO("dv", lambda: V.tensor_copy(c_bvb[:, :], c_ball[:, 18:26]), "dv")

    W("sp", "pe", tok_bT)
    tok_b1ld = DO("sp", lambda: SP.dma_start(out=c_stage[:, :], in_=d_b1), "sB", 16)
    W("pe", "sB", tok_b1ld)
    W("pe", "dv", tok_ball)
    tok_b1T = DO("pe", lambda: T.transpose(pC[:, 32:64], c_stage[0:32, :],
                                           c_idf[0:32, 0:32]), "pe")
    W("ac", "pe", tok_b1T)
    tok_b1r = DO("ac", lambda: A.activation(c_b1r[:, :], pC[:, 32:64],
                                            AF.Identity, bias=0.0, scale=1.0),
                 "ac")

    W("ac", "dv", tok_ball)
    tok_tr = {}
    tok_xf = {(-1, 1): tok_b1r}
    tok_latmm = {}
    tok_latev = {}
    tok_qmm = {}
    tok_qev = {}
    qcnt = 0
    tok_wsred = {}

    def emit_scan_job(i):
        sl = i % 2
        ssem = "sWS0" if sl == 0 else "sWS1"
        q = "gp" if sl == 0 else "sp"
        ENQ = GP if sl == 0 else SP
        if i < 32:
            src = d_w1[i * 128:(i + 1) * 128, :]
        else:
            j = i - 32
            rc, cs = j // 4, j % 4
            src = d_w2[rc * 128:(rc + 1) * 128, cs * 1024:(cs + 1) * 1024]
        W(q, "dv", tok_wsred.get(i - 2))
        tok_ld = DO(q, lambda src=src, sl=sl, ENQ=ENQ: ENQ.dma_start(
            out=t_wld[sl][:, :], in_=src), ssem, 16)
        W("dv", ssem, tok_ld)
        tok_wsred[i] = DO("dv", lambda sl=sl, i=i: V.tensor_reduce(
            t_wm[:, i:i + 1], t_wld[sl][:, :], AX.X, ALU.max,
            apply_absolute_value=True), "dv")

    for tit in range(33):
        if tit < 32:
            ti = tit
            sl = ti % 2
            xsem = "gX0" if sl == 0 else "gX1"
            W("sp", "pe", tok_tr.get(ti - 2))
            tok_xld = DO("sp", lambda ti=ti, sl=sl: SP.dma_start(
                out=t_x32b[sl][:, :], in_=d_x[ti * 128:(ti + 1) * 128, :]), xsem, 16)
            W("pe", xsem, tok_xld)
            for half, pbank in ((0, pD), (1, pC)):
                W("pe", "ac", tok_xf.get((ti - 1, half)))
                for kc4 in range(4):
                    kc = half * 4 + kc4
                    last = DO("pe", lambda sl=sl, kc=kc, kc4=kc4, pbank=pbank:
                              T.transpose(pbank[:, kc4 * 128:(kc4 + 1) * 128],
                                          t_x32b[sl][:, kc * 128:(kc + 1) * 128],
                                          c_idf[:, :]), "pe" if kc4 == 3 else None)
                tok_tr[(ti, half)] = last
                W("ac", "pe", tok_tr[(ti, half)])
                tok_xf[(ti, half)] = DO("ac", lambda sl=sl, half=half, pbank=pbank:
                                        A.activation(
                                            t_xf[sl][:, half * 4:half * 4 + 4, :]
                                            .rearrange("p a b -> p (a b)"),
                                            pbank[:, :], AF.Identity, bias=0.0,
                                            scale=1.0), "ac")
            tok_tr[ti] = tok_tr[(ti, 1)]
            tok_xf[ti] = tok_xf[(ti, 1)]
        if tit < 1:
            continue
        ti = tit - 1
        sl = ti % 2
        W("pe", "ac", tok_xf[(ti, 0)])
        W("pe", "ac", tok_xf[(ti, 1)])
        W("pe", "gWL", tok_wL)
        W("pe", "ac", tok_latev.get(ti - 2))
        lo = 0 if ti % 2 == 0 else 512
        for mc in range(2):
            for kc in range(8):
                tk = DO("pe", lambda sl=sl, mc=mc, kc=kc, lo=lo: T.matmul(
                    pA[:, lo + mc * 128: lo + (mc + 1) * 128],
                    t_Wl[:, kc, mc * 128:(mc + 1) * 128],
                    t_xf[sl][:, kc, :], start=(kc == 0), stop=(kc == 7)),
                    "pe" if (mc == 1 and kc == 7) else None)
        tok_latmm[ti] = tk
        W("ac", "pe", tok_latmm[ti])
        for mc in range(2):
            tk = DO("ac", lambda ti=ti, mc=mc, lo=lo: A.activation(
                t_lat[:, mc, ti * 128:(ti + 1) * 128],
                pA[:, lo + mc * 128: lo + (mc + 1) * 128],
                AF.Identity, bias=c_ball[:, 8 + mc:9 + mc], scale=1.0), "ac")
        tok_latev[ti] = tk
        if ti < 16:
            tloc = ti * 128
            for half in range(2):
                qs = (qcnt % 2) * 512
                W("pe", "gWA", tok_wA)
                W("pe", "ac", tok_qev.get(qcnt - 2))
                for mc in range(4):
                    for kc in range(8):
                        tk = DO("pe", lambda sl=sl, half=half, mc=mc, kc=kc, qs=qs:
                                T.matmul(pB[:, qs + mc * 128: qs + (mc + 1) * 128],
                                         t_Wq[:, kc, (half * 4 + mc) * 128:
                                              (half * 4 + mc + 1) * 128],
                                         t_xf[sl][:, kc, :],
                                         start=(kc == 0), stop=(kc == 7)),
                                "pe" if (mc == 3 and kc == 7) else None)
                tok_qmm[qcnt] = tk
                W("ac", "pe", tok_qmm[qcnt])
                for mc in range(4):
                    tk = DO("ac", lambda half=half, mc=mc, qs=qs, tloc=tloc:
                            A.activation(
                                t_q[:, half * 4 + mc, tloc:tloc + 128],
                                pB[:, qs + mc * 128: qs + (mc + 1) * 128],
                                AF.Identity,
                                bias=c_ball[:, half * 4 + mc: half * 4 + mc + 1],
                                scale=1.0), "ac")
                tok_qev[qcnt] = tk
                qcnt += 1
        emit_scan_job(2 * ti)
        emit_scan_job(2 * ti + 1)

    tok_q_all = tok_qev[qcnt - 1]
    tok_lat_all = tok_latev[31]

    W("dv", "dv", tok_wsred[63])
    DO("dv", lambda: V.tensor_reduce(c_wred[:, 0:1], t_wm[:, 0:32], AX.X, ALU.max))
    tok_wred = DO("dv", lambda: V.tensor_reduce(
        c_wred[:, 1:2], t_wm[:, 32:64], AX.X, ALU.max), "dv")
    W("pe", "dv", tok_wred)
    W("pe", "ac", tok_b1r)
    DO("pe", lambda: T.transpose(pC[0:1, 64:192], c_wred[:, 0:1], c_idf[:, :]))
    tok_wT = DO("pe", lambda: T.transpose(pC[0:1, 192:320], c_wred[:, 1:2],
                                          c_idf[:, :]), "pe")
    W("dv", "pe", tok_wT)
    DO("dv", lambda: V.tensor_reduce(c_sc[0:1, 0:1], pC[0:1, 64:192], AX.X, ALU.max))
    tok_w12 = DO("dv", lambda: V.tensor_reduce(
        c_sc[0:1, 1:2], pC[0:1, 192:320], AX.X, ALU.max), "dv")
    W("dv", "dv", tok_w12)
    DO("dv", lambda: V.tensor_single_scalar(c_sc[0:1, 2:3], c_sc[0:1, 0:1], 1e-6, ALU.max))
    tok_m = DO("dv", lambda: V.tensor_single_scalar(
        c_sc[0:1, 3:4], c_sc[0:1, 1:2], 1e-6, ALU.max), "dv")
    W("dv", "dv", tok_m)
    DO("dv", lambda: V.reciprocal(c_sc[0:1, 4:5], c_sc[0:1, 2:3]))
    tok_r = DO("dv", lambda: V.reciprocal(c_sc[0:1, 5:6], c_sc[0:1, 3:4]), "dv")
    W("dv", "dv", tok_r)
    DO("dv", lambda: V.tensor_single_scalar(c_sc[0:1, 6:7], c_sc[0:1, 4:5], 127.0, ALU.mult))
    DO("dv", lambda: V.tensor_single_scalar(c_sc[0:1, 7:8], c_sc[0:1, 5:6], 127.0, ALU.mult))
    DO("dv", lambda: V.tensor_single_scalar(c_sc[0:1, 8:9], c_sc[0:1, 2:3], 1.0 / 127.0, ALU.mult))
    tok_sw = DO("dv", lambda: V.tensor_single_scalar(
        c_sc[0:1, 9:10], c_sc[0:1, 3:4], 1.0 / 127.0, ALU.mult), "dv")
    W("pe", "dv", tok_sw)
    DO("pe", lambda: T.matmul(pC[:, 320:321], c_1f[0:1, :], c_sc[0:1, 6:7],
                              start=True, stop=True))
    tok_wbc = DO("pe", lambda: T.matmul(pC[:, 321:322], c_1f[0:1, :], c_sc[0:1, 7:8],
                                        start=True, stop=True), "pe")
    W("dv", "pe", tok_wbc)
    DO("dv", lambda: V.tensor_copy(c_wr1[:, :], pC[:, 320:321]))
    tok_wrb = DO("dv", lambda: V.tensor_copy(c_wr2[:, :], pC[:, 321:322]), "dv")

    W("pe", "gWB", tok_wall)
    W("pe", "dv", tok_wrb)
    tok_bvE = None
    for part in range(2):
        W("pe", "dv", tok_bvE)
        for kc in range(8):
            tk = DO("pe", lambda part=part, kc=kc: T.matmul(
                pC[0:1, 0:512],
                c_bvb[:, kc:kc + 1], t_Wo[:, kc, part * 512:(part + 1) * 512],
                start=(kc == 0), stop=(kc == 7)), "pe" if kc == 7 else None)
        W("dv", "pe", tk)
        tok_bvE = DO("dv", lambda part=part: V.tensor_tensor(
            c_bob[0:1, part * 512:(part + 1) * 512], pC[0:1, 0:512],
            t_bo32[0:1, part * 512:(part + 1) * 512], ALU.add), "dv")
    tok_b2b = DO("dv", lambda: V.tensor_copy(c_b2b[0:1, :], t_b2r[0:1, :]), "dv")

    wprep_quant = [("w1", i) for i in range(32)] + [("w2", j) for j in range(32)]
    tok_wq2 = {}
    tok_wqst = {}
    tok_wld_d = {}

    def emit_wquant_load(idx):
        kind, i = wprep_quant[idx]
        sl = idx % 2
        lsem = "sWL0" if sl == 0 else "sWL1"
        if kind == "w1":
            src_ = d_w1[i * 128:(i + 1) * 128, :]
        else:
            rc, cs = i // 4, i % 4
            src_ = d_w2[rc * 128:(rc + 1) * 128, cs * 1024:(cs + 1) * 1024]
        if idx < 2:
            W("sp", "dv", tok_wsred[62 + idx])
        W("sp", "dv", tok_wq2.get(idx - 2))
        tok_wld_d[idx] = DO("sp", lambda src_=src_, sl=sl: SP.dma_start(
            out=t_wld[sl][:, :], in_=src_), lsem, 16)

    def emit_wquant_comp(idx):
        kind, i = wprep_quant[idx]
        sl = idx % 2
        lsem = "sWL0" if sl == 0 else "sWL1"
        qsem = "sWQ0" if sl == 0 else "sWQ1"
        if kind == "w1":
            dst = d_w1q[i * 128:(i + 1) * 128, :]
            wr = c_wr1
        else:
            rc, cs = i // 4, i % 4
            dst = d_w2q[rc * 128:(rc + 1) * 128, cs * 1024:(cs + 1) * 1024]
            wr = c_wr2
        W("dv", lsem, tok_wld_d[idx])
        W("dv", "dv", tok_wrb)
        W("dv", "dv", tok_wq2.get(idx - 1))
        W("dv", qsem, tok_wqst.get(idx - 2))
        st1 = DO("dv", lambda sl=sl, wr=wr: V.tensor_scalar(
            t_wf16[:, :], t_wld[sl][:, :], wr[:, :], MAGIC, ALU.mult, ALU.add), "dv")
        W("dv", "dv", st1)
        tok_wq2[idx] = DO("dv", lambda sl=sl: V.tensor_scalar(
            t_wqb[sl][:, :], t_wf16[:, :], -MAGIC, None, ALU.add), "dv")
        W("sp", "dv", tok_wq2[idx])
        tok_wqst[idx] = DO("sp", lambda dst=dst, sl=sl: SP.dma_start(
            out=dst, in_=t_wqb[sl][:, :]), qsem, 16)

    xbar_jobs = [("w1", c, half) for c in range(8) for half in range(2)]
    tok_wt = {}
    xbar_i = [0]

    def emit_xbar_job(idx):
        kind, a, b = xbar_jobs[idx]
        sl = idx % 2
        wt = t_wtr[sl]
        xtsem = "sXT0" if sl == 0 else "sXT1"
        wtsem = "sWT0" if sl == 0 else "sWT1"
        W("sp", wtsem, tok_wt.get(idx - 2))
        if kind == "w1":
            W("sp", "sWQ0", tok_wqst[30])
            W("sp", "sWQ1", tok_wqst[31])
            src_ = d_w1q[b * 2048:(b + 1) * 2048, a * 128:(a + 1) * 128]
            tok_x = DO("sp", lambda src_=src_, wt=wt: SP.dma_start(
                out=wt[:, :], in_=src_, transpose=True), xtsem, 16)
            W("sp", xtsem, tok_x)
            tok_wt[idx] = DO("sp", lambda a=a, b=b, wt=wt: SP.dma_start(
                out=d_w1qT[b * 16:(b + 1) * 16, :, a, :].rearrange(
                    "hc p m -> p hc m"),
                in_=wt[:, :].rearrange("p (hc m) -> p hc m", m=128)),
                wtsem, 16)
        else:
            W("sp", "sWQ0", tok_wqst[62])
            W("sp", "sWQ1", tok_wqst[63])
            src_ = d_w2q[:, a * 128:(a + 1) * 128]
            tok_x = DO("sp", lambda src_=src_, wt=wt: SP.dma_start(
                out=wt[:, 0:1024], in_=src_, transpose=True), xtsem, 16)
            W("sp", xtsem, tok_x)
            tok_wt[idx] = DO("sp", lambda a=a, wt=wt: SP.dma_start(
                out=d_w2qT[a, :, :], in_=wt[:, 0:1024]), wtsem, 16)

    tok_stt_prev = None
    tok_copy_prev = None
    tok_av_final_prev = None
    pend_ln = [None]
    tok_stt_pair = {0: None, 1: None}
    pend = [None]
    tok_bc_last = [None]
    tok_ln_last = [None]
    tok_exp = {}
    tok_av = {}
    last_scores_tok = None
    pit = 0
    wprep_i = 0

    def emit_pend_ln():
        if pend_ln[0] is None:
            return
        tok_lcp = pend_ln[0]
        pend_ln[0] = None
        W("ac", "dv", tok_lcp)
        tok_ln = DO("ac", lambda: A.activation(
            t_slx[:, :], t_sll[:, :], AF.Ln, bias=0.0, scale=1.0), "ac")
        W("ac", "ac", tok_ln)
        W("ac", "pe", tok_bc_last[0])
        tok_recip = DO("ac", lambda: A.activation(
            t_slr[:, :], t_slx[:, :], AF.Exp, bias=0.0, scale=-1.0), "ac")
        tok_ln_last[0] = tok_recip
        if pend[0] is not None:
            pend[0]["tok_recip"] = tok_recip

    def flush_tail(tok_pa_free):
        nonlocal tok_stt_prev, tok_copy_prev
        p = pend[0]
        if p is None:
            return
        pend[0] = None
        W("pe", "ac", p["tok_recip"])
        W("pe", "ac", tok_pa_free)
        W("pe", "dv", tok_copy_prev)
        tok_bc = DO("pe", lambda: T.matmul(pA[:, 0:512], c_sel[:, :],
                                           t_slr[:, :], start=True, stop=True),
                    "pe")
        tok_bc_last[0] = tok_bc
        W("dv", "pe", tok_bc)
        W("dv", "dv", tok_stt_prev)
        tok_copy = DO("dv", lambda: V.tensor_copy(t_slb[:, :], pA[:, 0:512]), "dv")
        W("dv", "dv", tok_copy)
        avA, avB = p["avA"], p["avB"]
        DO("dv", lambda avA=avA, p=p: V.scalar_tensor_tensor(
            t_o[0:64, p["qsub"], p["it"] * 512:(p["it"] + 1) * 512],
            avA[0:64, :], 1.0, t_slb[0:64, :], ALU.mult, ALU.mult), "dv")
        tok_stt = DO("dv", lambda avB=avB, p=p: V.scalar_tensor_tensor(
            t_o[64:128, p["qsub"], p["it"] * 512:(p["it"] + 1) * 512],
            avB[0:64, :], 1.0, t_slb[64:128, :], ALU.mult, ALU.mult), "dv")
        tok_stt_pair[p["parity"]] = ("dv", tok_stt)
        tok_stt_prev = tok_stt
        tok_copy_prev = tok_copy

    tok_stt_pair[0] = ("ac", tok_xf[(31, 1)])
    tok_stt_pair[1] = ("ac", tok_xf[(31, 0)])
    for g in range(4):
        emit_pend_ln()
        flush_tail(tok_exp.get((pit - 1, 30)))
        kv_par = 0
        tok_kv_ev = {}
        for r in range(2):
            for jt in range(8):
                ps = pC[:, 0:512] if kv_par % 2 == 0 else pB[:, 0:512]
                W("pe", "dv", tok_stt_prev)
                W("pe", "ac", tok_exp.get((pit - 1, 31)))
                W("pe", "dv", tok_copy_prev)
                W("pe", "gWB", tok_wall)
                W("pe", "ac", tok_lat_all)
                W("pe", "dv", tok_b2b)
                W("pe", "dv", tok_kv_ev.get(kv_par - 2))
                for lc in range(2):
                    tk = DO("pe", lambda g=g, r=r, jt=jt, lc=lc, ps=ps: T.matmul(
                        ps, t_Wk[:, lc, g * 256 + r * 128: g * 256 + r * 128 + 128],
                        t_lat[:, lc, jt * 512:(jt + 1) * 512],
                        start=(lc == 0), stop=(lc == 1)), "pe" if lc == 1 else None)
                W("dv", "pe", tk)
                W("dv", "pe", last_scores_tok)
                tok_kv_ev[kv_par] = DO("dv", lambda g=g, r=r, jt=jt, ps=ps:
                                       V.tensor_scalar(
                                           t_kg[:, r, jt * 512:(jt + 1) * 512], ps,
                                           c_ball[:, 10 + 2 * g + r: 11 + 2 * g + r],
                                           None, ALU.add), "dv")
                kv_par += 1
        for jc in range(32):
            ps = pC[:, 0:256] if kv_par % 2 == 0 else pB[:, 0:256]
            W("pe", "dv", tok_kv_ev.get(kv_par - 2))
            for lc in range(2):
                tk = DO("pe", lambda g=g, jc=jc, lc=lc, ps=ps: T.matmul(
                    ps, t_lat[:, lc, jc * 128:(jc + 1) * 128],
                    t_Wv[:, lc, g * 256:(g + 1) * 256],
                    start=(lc == 0), stop=(lc == 1)), "pe" if lc == 1 else None)
            W("dv", "pe", tk)
            W("dv", "pe", tok_av_final_prev)
            W("dv", "dv", tok_ones)
            tok_kv_ev[kv_par] = DO("dv", lambda jc=jc, ps=ps: V.tensor_copy(
                t_vg[:, jc, :, 0:64],
                ps.rearrange("p (hh d) -> p hh d", d=64)), "dv")
            kv_par += 1
        tok_kv_all = tok_kv_ev[kv_par - 1]

        for hp in range(2):
            ksub = hp
            qsub = 2 * g + hp
            hhA, hhB = 2 * hp, 2 * hp + 1
            for it in range(4):
                tok_sc = {}
                parity = pit % 2
                avA = pC[:, 0:512] if parity == 0 else pB[:, 1024:1536]
                avB = pA[:, 1024:1536] if parity == 0 else pD[:, 0:512]
                for gr in range(34):
                    if gr == 2:
                        emit_pend_ln()
                    if gr == 4:
                        flush_tail(tok_exp[(pit, 2)])
                    if gr < 32:
                        sb = pA if gr % 2 == 0 else pB
                        W("pe", "ac", tok_exp.get((pit, gr - 2)))
                        if gr == 4:
                            W("pe", "dv", tok_copy_prev)
                        if gr <= 1:
                            W("pe", "ac", tok_exp.get((pit - 1, 30 + gr)))
                            W("pe", "dv", tok_kv_all)
                            W("pe", "ac", tok_q_all)
                        DO("pe", lambda sb=sb, gr=gr, ksub=ksub, qsub=qsub, it=it:
                           T.matmul(sb[:, 0:512],
                                    t_kg[0:64, ksub, gr * 128:(gr + 1) * 128],
                                    t_q[0:64, qsub, it * 512:(it + 1) * 512],
                                    start=True, stop=True))
                        tok_sc[gr] = DO("pe", lambda sb=sb, gr=gr, ksub=ksub,
                                        qsub=qsub, it=it: T.matmul(
                                            sb[:, 512:1024],
                                            t_kg[64:128, ksub, gr * 128:(gr + 1) * 128],
                                            t_q[64:128, qsub, it * 512:(it + 1) * 512],
                                            start=True, stop=True), "pe")
                        W("ac", "pe", tok_sc[gr])
                        ring = tok_av.get((pit, gr - 8 + ((gr - 8) % 2 == 0))) \
                            if gr >= 8 else tok_av.get((pit - 1, gr + 24 + ((gr + 24) % 2 == 0)))
                        W("ac", "pe", ring)
                        tok_exp[(pit, gr)] = DO(
                            "ac", lambda sb=sb, gr=gr: A.activation(
                                t_P[:, gr % 8, :, :].rearrange("p c n -> p (c n)"),
                                sb[:, 0:1024], AF.Exp, bias=0.0, scale=0.125), "ac")
                    if gr >= 2:
                        ga = gr - 2
                        W("pe", "ac", tok_exp[(pit, ga)])
                        if ga == 0:
                            sp_e, sp_t = tok_stt_pair[parity]
                            W("pe", sp_e, sp_t)
                        DO("pe", lambda ga=ga, hhA=hhA, avA=avA: T.matmul(
                            avA[0:65, :], t_vg[:, ga, hhA, :], t_P[:, ga % 8, 0, :],
                            start=(ga == 0), stop=(ga == 31)))
                        tk = DO("pe", lambda ga=ga, hhB=hhB, avB=avB: T.matmul(
                            avB[0:65, :], t_vg[:, ga, hhB, :], t_P[:, ga % 8, 1, :],
                            start=(ga == 0), stop=(ga == 31)),
                            "pe" if ga % 2 == 1 else None)
                        if ga % 2 == 1:
                            tok_av[(pit, ga)] = tk
                tok_av_final = tok_av[(pit, 31)]
                last_scores_tok = tok_sc[31]
                W("dv", "pe", tok_av_final)
                W("dv", "ac", tok_ln_last[0])
                DO("dv", lambda avA=avA: V.tensor_copy(t_sll[0:1, :],
                                                       avA[64:65, :]))
                tok_lcp = DO("dv", lambda avB=avB: V.tensor_copy(
                    t_sll[64:65, :], avB[64:65, :]), "dv")
                pend_ln[0] = tok_lcp
                pend[0] = {"avA": avA, "avB": avB, "parity": parity,
                           "qsub": qsub, "it": it, "tok_recip": None}
                tok_av_final_prev = tok_av_final
                for j in (2 * wprep_i - 2, 2 * wprep_i - 1):
                    if 0 <= j < 64:
                        emit_wquant_comp(j)
                for j in (2 * wprep_i, 2 * wprep_i + 1):
                    if j < 64:
                        emit_wquant_load(j)
                if wprep_i >= 16 and xbar_i[0] < 16:
                    emit_xbar_job(xbar_i[0])
                    xbar_i[0] += 1
                wprep_i += 1
                pit += 1

    tok_womm = {}
    tok_sumev = {}
    tok_rmred = {}
    tok_x32 = {}
    tok_xst = {}

    def ln_pre(sum_ap, bn, bnx):
        bn_m = bn[:, 0:12].rearrange("p (a b) -> p a b", b=3)[:, :, 1]
        bn_cv = bn[:, 0:12].rearrange("p (a b) -> p a b", b=3)[:, :, 2]
        DO("dv", lambda: V.bn_stats(bn[:, 0:6], sum_ap[:, 0:512]))
        t1 = DO("dv", lambda: V.bn_stats(bn[:, 6:12], sum_ap[:, 512:1024]), "dv")
        W("dv", "dv", t1)
        DO("dv", lambda: V.tensor_reduce(bnx[:, 0:1], bn_m, AX.X, ALU.add))
        t2 = DO("dv", lambda: V.tensor_tensor(bnx[:, 8:12], bn_m, bn_m,
                                              ALU.mult), "dv")
        W("dv", "dv", t2)
        DO("dv", lambda: V.scalar_tensor_tensor(bnx[:, 12:16], bnx[:, 8:12], 256.0,
                                                bn_cv, ALU.mult, ALU.add))
        t3 = DO("dv", lambda: V.tensor_single_scalar(bnx[:, 2:3], bnx[:, 0:1],
                                                     0.25, ALU.mult), "dv")
        W("dv", "dv", t3)
        DO("dv", lambda: V.tensor_reduce(bnx[:, 1:2], bnx[:, 12:16], AX.X, ALU.add))
        t4 = DO("dv", lambda: V.tensor_tensor(bnx[:, 3:4], bnx[:, 2:3], bnx[:, 2:3],
                                              ALU.mult), "dv")
        W("dv", "dv", t4)
        t5 = DO("dv", lambda: V.scalar_tensor_tensor(bnx[:, 4:5], bnx[:, 1:2],
                                                     1.0 / 1024.0, bnx[:, 3:4],
                                                     ALU.mult, ALU.subtract), "dv")
        W("ac", "dv", t5)
        t6 = DO("ac", lambda: A.activation(bnx[:, 5:6], bnx[:, 4:5], AF.Ln,
                                           bias=c_eps[:, :], scale=1.0), "ac")
        W("ac", "ac", t6)
        t7 = DO("ac", lambda: A.activation(bnx[:, 6:7], bnx[:, 5:6], AF.Exp,
                                           bias=0.0, scale=-0.5), "ac")
        return t7, t5

    def ln_post(sum_ap, out_ap, bnx, tok_rstd):
        W("dv", "ac", tok_rstd)
        t8 = DO("dv", lambda: V.scalar_tensor_tensor(bnx[:, 7:8], bnx[:, 2:3], -1.0,
                                                     bnx[:, 6:7], ALU.mult,
                                                     ALU.mult), "dv")
        W("dv", "dv", t8)
        t9 = DO("dv", lambda: V.tensor_scalar(out_ap, sum_ap, bnx[:, 6:7],
                                              bnx[:, 7:8], ALU.mult, ALU.add), "dv")
        return t9

    emit_pend_ln()
    flush_tail(tok_exp.get((pit - 1, 30)))
    emit_wquant_comp(62)
    emit_wquant_comp(63)
    W("sp", "pe", tok_av_final_prev)
    W("sp", "dv", tok_stt_prev)
    W("gp", "pe", tok_av_final_prev)
    W("gp", "dv", tok_stt_prev)
    tok_pre = {}

    def p3_post(tc):
        sl = tc % 2
        W("dv", "sXS0" if sl == 0 else "sXS1", tok_xst.get(tc - 2))
        tok_x32[tc] = ln_post(t_sum[sl][:, :], t_x32[sl][:, :], t_bnx[sl],
                              tok_pre[tc][0])
        W("dv", "dv", tok_x32[tc])
        tok_rmred[tc] = DO("dv", lambda sl=sl, tc=tc: V.tensor_reduce(
            t_rm[:, tc:tc + 1], t_x32[sl][:, :], AX.X, ALU.max,
            apply_absolute_value=True), "dv")
        W("gp", "dv", tok_x32[tc])
        xssem = "sXS0" if sl == 0 else "sXS1"
        tok_xst[tc] = DO("gp", lambda sl=sl, tc=tc: GP.dma_start(
            out=d_xln[tc * 128:(tc + 1) * 128, :], in_=t_x32[sl][:, :]), xssem, 16)

    w2x_tr = {}
    w2x_ev = {}
    w2x_st = {}

    w2x_ld = {}

    def emit_w2x_load(rc):
        sl = rc % 2
        xtsem = "sXT0" if sl == 0 else "sXT1"
        if rc == 0:
            W("sp", "pe", tok_av_final_prev)
        if rc >= 2:
            W("sp", "pe", w2x_tr[(rc - 2, 7)])
        W("sp", "sWQ0", tok_wqst[34 + 4 * rc])
        W("sp", "sWQ1", tok_wqst[35 + 4 * rc])
        w2x_ld[rc] = DO("sp", lambda rc=rc, sl=sl: SP.dma_start(
            out=t_w2l[sl][:, :], in_=d_w2q[rc * 128:(rc + 1) * 128, :]), xtsem, 16)

    def emit_w2x_comp(rc):
        sl = rc % 2
        xtsem = "sXT0" if sl == 0 else "sXT1"
        wtsem = "sWT0" if sl == 0 else "sWT1"
        if rc == 0:
            W("ac", "sWT0", tok_wt[14])
            W("ac", "sWT1", tok_wt[15])
        for g in range(8):
            pbank = pD if g % 2 == 0 else pC
            W("pe", xtsem, w2x_ld[rc])
            W("pe", "ac", w2x_ev.get((rc, g - 2)))
            if rc > 0 and g < 2:
                W("pe", "ac", w2x_ev.get((rc - 1, 6 + g)))
            for hh in range(4):
                h = g * 4 + hh
                tk = DO("pe", lambda sl=sl, h=h, hh=hh, pbank=pbank: T.matmul(
                    pbank[:, hh * 128:(hh + 1) * 128],
                    t_w2l[sl][:, h * 128:(h + 1) * 128],
                    c_idb[:, :], start=True, stop=True), "pe" if hh == 3 else None)
            w2x_tr[(rc, g)] = tk
            W("ac", "pe", tk)
            W("ac", wtsem, w2x_st.get(rc - 2))
            w2x_ev[(rc, g)] = DO("ac", lambda sl=sl, g=g, pbank=pbank:
                                 A.activation(
                                     t_w2x[sl][:, g * 4:(g + 1) * 4, :].rearrange(
                                         "p a b -> p (a b)"),
                                     pbank[:, :], AF.Identity, bias=0.0,
                                     scale=1.0), "ac")
        W("sp", "ac", w2x_ev[(rc, 7)])
        w2x_st[rc] = DO("sp", lambda rc=rc, sl=sl: SP.dma_start(
            out=d_w2qT[:, :, rc * 128:(rc + 1) * 128].rearrange("a p b -> p a b"),
            in_=t_w2x[sl][:, :, :]), wtsem, 16)

    tok_x3 = {}
    for pf in range(2):
        tok_x3[pf] = DO("gp", lambda pf=pf: GP.dma_start(
            out=t_xr[pf][:, :], in_=d_x[pf * 128:(pf + 1) * 128, :]),
            "sX320" if pf == 0 else "sX321", 16)
    for tc in range(16):
        sl = tc % 2
        buf = pA if tc % 2 == 0 else pB
        x3sem = "sX320" if sl == 0 else "sX321"
        xssem = "sXS0" if sl == 0 else "sXS1"
        W("pe", "dv", tok_stt_prev)
        W("pe", "dv", tok_b2b)
        W("pe", "dv", tok_sumev.get(tc - 2))
        for dn in range(2):
            for kc in range(8):
                DO("pe", lambda buf=buf, dn=dn, kc=kc, tc=tc: T.matmul(
                    buf[:, dn * 512:(dn + 1) * 512],
                    t_o[:, kc, tc * 128:(tc + 1) * 128],
                    t_Wo[:, kc, dn * 512:(dn + 1) * 512],
                    start=(kc == 0), stop=False))
            tk = DO("pe", lambda buf=buf, dn=dn: T.matmul(
                buf[:, dn * 512:(dn + 1) * 512], c_1b[0:1, :],
                c_bob[0:1, dn * 512:(dn + 1) * 512], start=False, stop=True),
                "pe" if dn == 1 else None)
        tok_womm[tc] = tk
        W("dv", "pe", tok_womm[tc])
        W("dv", x3sem, tok_x3[tc])
        W("dv", "dv", tok_x32.get(tc - 2))
        tok_sumev[tc] = DO("dv", lambda buf=buf, sl=sl: V.tensor_tensor(
            t_sum[sl][:, :], buf[:, 0:1024], t_xr[sl][:, :], ALU.add), "dv")
        W("dv", "dv", tok_sumev[tc])
        if tc + 2 < 16:
            W("gp", "dv", tok_sumev[tc])
            tok_x3[tc + 2] = DO("gp", lambda tc=tc, sl=sl: GP.dma_start(
                out=t_xr[sl][:, :], in_=d_x[(tc + 2) * 128:(tc + 3) * 128, :]),
                x3sem, 16)
        if tc % 2 == 1:
            k = (tc - 1) // 2
            if k == 0:
                emit_w2x_load(0)
            if k + 1 < 8:
                emit_w2x_load(k + 1)
            emit_w2x_comp(k)
        tok_pre[tc] = ln_pre(t_sum[sl][:, :], t_bn[sl], t_bnx[sl])
        if tc >= 1:
            p3_post(tc - 1)
        if xbar_i[0] < len(xbar_jobs):
            emit_xbar_job(xbar_i[0])
            xbar_i[0] += 1
    p3_post(15)

    W("dv", "dv", tok_rmred[15])
    tok_rmc = DO("dv", lambda: V.tensor_reduce(c_wred[:, 0:1], t_rm[:, 0:16],
                                               AX.X, ALU.max), "dv")
    W("pe", "dv", tok_rmc)
    tok_rmT = DO("pe", lambda: T.transpose(pB[0:1, 64:192], c_wred[:, 0:1],
                                           c_idf[:, :]), "pe")
    W("dv", "pe", tok_rmT)
    tok_lmax = DO("dv", lambda: V.tensor_reduce(c_sc[0:1, 10:11], pB[0:1, 64:192],
                                                AX.X, ALU.max), "dv")
    W("dv", "dv", tok_lmax)
    tok_s8 = DO("dv", lambda: V.tensor_scalar(c_s8[0:1, :], c_1f[0:1, 0:8],
                                              c_sc[0:1, 10:11], None, ALU.mult),
                "dv")
    W("gp", "dv", tok_s8)
    tok_c1a = DO("gp", lambda: GP.dma_start(out=d_cc1i, in_=c_s8[0:1, :]), "sCC", 16)
    W("gp", "sCC", tok_c1a)
    tok_cc1 = DO("gp", lambda: GP.collective_compute(
        "AllReduce", ALU.max, replica_groups=[list(range(cc_ranks))],
        ins=[d_cc1i], outs=[d_cc1o]), "gc")

    p3_pe = tok_womm[15]
    W("sp", "pe", p3_pe)
    W("sp", "dv", tok_rmred[15])
    W("sp", "sXS0", tok_xst[14])
    W("sp", "sXS1", tok_xst[15])
    W("dv", "sWQ0", tok_wqst[62])
    W("dv", "sWQ1", tok_wqst[63])
    W("pe", "sWQ0", tok_wqst[62])
    W("pe", "sWQ1", tok_wqst[63])
    tok_tr = {}
    tok_tev = {(-1, 0): w2x_ev[(7, 6)], (-1, 1): w2x_ev[(7, 7)]}
    tok_xl5t = {}
    for tc in range(16):
        sl = tc % 2
        xlsem = "sXL0" if sl == 0 else "sXL1"
        xssem = "sXS0" if tc % 2 == 0 else "sXS1"
        if tc >= 2:
            W("sp", "pe", tok_tr[(tc - 2, 1)])
        W("sp", xssem, tok_xst[tc])
        tok_xl5t[tc] = DO("sp", lambda tc=tc, sl=sl: SP.dma_start(
            out=t_xl5[sl][:, :], in_=d_xln[tc * 128:(tc + 1) * 128, :]), xlsem, 16)
        if tc == 14:
            W("ac", "sWT0", w2x_st[6])
            W("ac", "sWT1", w2x_st[7])
        for half, pbank in ((0, pD), (1, pC)):
            W("pe", "ac", tok_tev.get((tc - 1, half)))
            W("pe", xlsem, tok_xl5t[tc])
            for kc4 in range(4):
                kc = half * 4 + kc4
                tk = DO("pe", lambda sl=sl, kc=kc, kc4=kc4, pbank=pbank: T.transpose(
                    pbank[:, kc4 * 128:(kc4 + 1) * 128],
                    t_xl5[sl][:, kc * 128:(kc + 1) * 128],
                    c_idf[:, :]), "pe" if kc4 == 3 else None)
            tok_tr[(tc, half)] = tk
            W("ac", "pe", tok_tr[(tc, half)])
            tok_tev[(tc, half)] = DO("ac", lambda tc=tc, half=half, pbank=pbank:
                                     A.activation(
                                         t_xf5[tc][:, half * 4:half * 4 + 4, :]
                                         .rearrange("p a b -> p (a b)"),
                                         pbank[:, :], AF.Identity, bias=0.0,
                                         scale=1.0), "ac")

    W("gp", "gc", tok_cc1)
    tok_c1b = DO("gp", lambda: GP.dma_start(out=c_sc[0:1, 10:11],
                                            in_=d_cc1o[0:1, 0:1]), "sCC", 16)
    W("dv", "sCC", tok_c1b)
    t = DO("dv", lambda: V.tensor_single_scalar(c_sc[0:1, 11:12], c_sc[0:1, 10:11],
                                                1e-6, ALU.max), "dv")
    W("ac", "dv", t)
    t = DO("ac", lambda: A.activation(c_rc[0:1, 0:1], c_sc[0:1, 11:12], AF.Ln,
                                      bias=0.0, scale=1.0), "ac")
    W("ac", "ac", t)
    t = DO("ac", lambda: A.activation(c_sc[0:1, 12:13], c_rc[0:1, 0:1], AF.Exp,
                                      bias=0.0, scale=-1.0), "ac")
    W("dv", "ac", t)
    DO("dv", lambda: V.tensor_single_scalar(c_sc[0:1, 13:14], c_sc[0:1, 12:13],
                                            127.0, ALU.mult))
    t = DO("dv", lambda: V.tensor_single_scalar(c_sc[0:1, 14:15], c_sc[0:1, 11:12],
                                                1.0 / 127.0, ALU.mult), "dv")
    W("dv", "dv", t)
    tok_s1s = DO("dv", lambda: V.tensor_tensor(c_sc[0:1, 15:16], c_sc[0:1, 14:15],
                                               c_sc[0:1, 8:9], ALU.mult), "dv")
    W("pe", "dv", tok_s1s)
    W("pe", "ac", tok_tev[(15, 1)])
    DO("pe", lambda: T.matmul(pC[:, 322:323], c_1f[0:1, :], c_sc[0:1, 13:14],
                              start=True, stop=True))
    tok_sbc = DO("pe", lambda: T.matmul(pC[:, 323:324], c_1f[0:1, :],
                                        c_sc[0:1, 15:16], start=True, stop=True),
                 "pe")
    W("dv", "pe", tok_sbc)
    DO("dv", lambda: V.tensor_copy(c_sxr[:, :], pC[:, 322:323]))
    tok_scb = DO("dv", lambda: V.tensor_copy(c_s1[:, :], pC[:, 323:324]), "dv")
    W("dv", "dv", tok_scb)
    tok_b1s = DO("dv", lambda: V.tensor_scalar(c_b1s[:, :], c_b1r[:, :],
                                               c_s1[:, :], None, ALU.mult), "dv")

    tok_xqq = {}
    for tc in range(16):
        W("dv", "dv", tok_scb)
        W("dv", "dv", tok_xqq.get(tc - 1))
        W("dv", "ac", tok_tev[(tc, 1)])
        t1 = DO("dv", lambda tc=tc: V.tensor_scalar(
            t_qf16[:, :], t_xf5[tc][:, :, :].rearrange("p a b -> p (a b)"),
            c_sxr[:, :], MAGIC, ALU.mult, ALU.add), "dv")
        W("dv", "dv", t1)
        tok_xqq[tc] = DO("dv", lambda tc=tc: V.tensor_scalar(
            t_xq[:, :, tc * 128:(tc + 1) * 128],
            t_qf16[:, :].rearrange("p (kc n) -> p kc n", n=128),
            -MAGIC, None, ALU.add), "dv")
    tok_xq_all = tok_xqq[15]

    tok_gelu = {}
    tok_hst = {}
    tok_hred = {}
    tok_ffmm = {}
    fidx = 0
    for Hc in range(32):
        sl = Hc % 2
        w1sem = "sW10" if sl == 0 else "sW11"
        W("sp", "pe", tok_ffmm.get((Hc - 2, 3)))
        W("sp", "sWT0", tok_wt[14])
        W("sp", "sWT1", tok_wt[15])
        tok_w1 = DO("sp", lambda Hc=Hc, sl=sl: SP.dma_start(
            out=t_w1s[sl][:, :, :], in_=d_w1qT[Hc]), w1sem, 16)
        for tt in range(4):
            slot = fidx % 4
            ps = [pA[:, 0:512], pA[:, 512:1024], pA[:, 1024:1536], pB[:, 0:512]][slot]
            W("pe", w1sem, tok_w1)
            W("pe", "dv", tok_xqq[4 * tt + 3])
            W("pe", "ac", tok_gelu.get(fidx - 4))
            for kc in range(8):
                tk = DO("pe", lambda ps=ps, sl=sl, kc=kc, tt=tt: T.matmul(
                    ps, t_w1s[sl][:, kc, :], t_xq[:, kc, tt * 512:(tt + 1) * 512],
                    start=(kc == 0), stop=(kc == 7)), "pe" if kc == 7 else None)
            tok_ffmm[(Hc, tt)] = tk
            hsl = fidx % 3
            hsem = ("sH0", "sH1", "sH2")[hsl]
            W("ac", "pe", tok_ffmm[(Hc, tt)])
            W("ac", "dv", tok_b1s)
            W("ac", hsem, tok_hst.get(fidx - 3))
            W("ac", "dv", tok_hred.get(fidx - 3))
            tok_gelu[fidx] = DO("ac", lambda ps=ps, hsl=hsl, Hc=Hc: A.activation(
                t_ht[hsl][:, :], ps, AF.Gelu, bias=c_b1s[:, Hc:Hc + 1],
                scale=c_s1[:, :]), "ac")
            W("dv", "ac", tok_gelu[fidx])
            tok_hred[fidx] = DO("dv", lambda hsl=hsl, fidx=fidx: V.tensor_reduce(
                t_hm[:, fidx:fidx + 1], t_ht[hsl][:, :], AX.X, ALU.max,
                apply_absolute_value=True), "dv")
            W("gp", "ac", tok_gelu[fidx])
            tok_hst[fidx] = DO("gp", lambda Hc=Hc, tt=tt, hsl=hsl: GP.dma_start(
                out=d_hst[Hc, :, tt * 512:(tt + 1) * 512], in_=t_ht[hsl][:, :]),
                hsem, 16)
            fidx += 1

    W("dv", "dv", tok_hred[127])
    tok_hmc = DO("dv", lambda: V.tensor_reduce(c_wred[:, 1:2], t_hm[:, :],
                                               AX.X, ALU.max), "dv")
    W("pe", "dv", tok_hmc)
    W("pe", "dv", tok_scb)
    tok_hmT = DO("pe", lambda: T.transpose(pC[0:1, 64:192], c_wred[:, 1:2],
                                           c_idf[:, :]), "pe")
    W("dv", "pe", tok_hmT)
    tok_hmax = DO("dv", lambda: V.tensor_reduce(c_sc[0:1, 10:11], pC[0:1, 64:192],
                                                AX.X, ALU.max), "dv")
    W("dv", "dv", tok_hmax)
    tok_s8b = DO("dv", lambda: V.tensor_scalar(c_s8[0:1, :], c_1f[0:1, 0:8],
                                               c_sc[0:1, 10:11], None, ALU.mult),
                 "dv")
    W("gp", "dv", tok_s8b)
    tok_c2a = DO("gp", lambda: GP.dma_start(out=d_cc2i, in_=c_s8[0:1, :]), "sCC", 16)
    W("gp", "sCC", tok_c2a)
    tok_cc2 = DO("gp", lambda: GP.collective_compute(
        "AllReduce", ALU.max, replica_groups=[list(range(cc_ranks))],
        ins=[d_cc2i], outs=[d_cc2o]), "gc")

    HRSEM = ("sHR0", "sHR1", "sHR2", "sHR3")
    tok_hr = {}
    tok_hq1 = {}
    W("sp", "pe", tok_ffmm[(31, 3)])
    W("sp", "sWT0", w2x_st[6])
    W("sp", "sWT1", w2x_st[7])
    tok_w2sld = DO("sp", lambda: SP.dma_start(
        out=t_w2s[:, :, :], in_=d_w2qT.rearrange("hc p m -> p hc m")), "sW2", 16)
    W("sp", "sH0", tok_hst[126])
    W("sp", "sH1", tok_hst[127])
    W("sp", "sH2", tok_hst[125])
    W("gp", "pe", tok_ffmm[(31, 3)])
    for tc in range(4):
        q = "gp" if tc % 2 == 0 else "sp"
        ENQ = GP if tc % 2 == 0 else SP
        tok_hr[tc] = DO(q, lambda tc=tc, ENQ=ENQ: ENQ.dma_start(
            out=t_hf[tc][:, :, :],
            in_=d_hst[:, :, tc * 128:(tc + 1) * 128].rearrange("hc p d -> p hc d")),
            HRSEM[tc], 16)

    W("sp", "gc", tok_cc2)
    tok_c2b = DO("sp", lambda: SP.dma_start(out=c_sc[0:1, 10:11],
                                            in_=d_cc2o[0:1, 0:1]), "sC2", 16)
    W("dv", "sC2", tok_c2b)
    t = DO("dv", lambda: V.tensor_single_scalar(c_sc[0:1, 11:12], c_sc[0:1, 10:11],
                                                1e-6, ALU.max), "dv")
    W("ac", "dv", t)
    t = DO("ac", lambda: A.activation(c_rc[0:1, 1:2], c_sc[0:1, 11:12], AF.Ln,
                                      bias=0.0, scale=1.0), "ac")
    W("ac", "ac", t)
    t = DO("ac", lambda: A.activation(c_sc[0:1, 12:13], c_rc[0:1, 1:2], AF.Exp,
                                      bias=0.0, scale=-1.0), "ac")
    W("dv", "ac", t)
    DO("dv", lambda: V.tensor_single_scalar(c_sc[0:1, 13:14], c_sc[0:1, 12:13],
                                            127.0, ALU.mult))
    t = DO("dv", lambda: V.tensor_single_scalar(c_sc[0:1, 14:15], c_sc[0:1, 11:12],
                                                1.0 / 127.0, ALU.mult), "dv")
    W("dv", "dv", t)
    tok_s2s = DO("dv", lambda: V.tensor_tensor(c_sc[0:1, 15:16], c_sc[0:1, 14:15],
                                               c_sc[0:1, 9:10], ALU.mult), "dv")
    W("pe", "dv", tok_s2s)
    DO("pe", lambda: T.matmul(pC[:, 324:325], c_1f[0:1, :], c_sc[0:1, 13:14],
                              start=True, stop=True))
    tok_sbc2 = DO("pe", lambda: T.matmul(pC[:, 325:326], c_1f[0:1, :],
                                         c_sc[0:1, 15:16], start=True, stop=True),
                  "pe")
    W("dv", "pe", tok_sbc2)
    DO("dv", lambda: V.tensor_copy(c_shr[:, :], pC[:, 324:325]))
    tok_scb2 = DO("dv", lambda: V.tensor_copy(c_s2[:, :], pC[:, 325:326]), "dv")

    W("sp", "pe", tok_tr[(15, 1)])
    W("dv", "pe", tok_tr[(15, 1)])
    W("pe", "ac", tok_gelu[127])
    tok_hq = {}
    tok_ff2 = {}
    tok_sum6 = {}
    tok_yst = {}
    tok_pre6 = {}
    tok_post6 = {}

    def p6_post(tc):
        sl = tc % 2
        ysem = "sY0" if sl == 0 else "sY1"
        tok_post6[tc] = ln_post(t_out[sl][:, :], t_out[sl][:, :], t_bx6[sl],
                                tok_pre6[tc][0])
        W("sp", "dv", tok_post6[tc])
        tok_yst[tc] = DO("sp", lambda tc=tc, sl=sl: SP.dma_start(
            out=d_y[tc * 128:(tc + 1) * 128, :], in_=t_out[sl][:, :]), ysem, 16)

    def p6_quant(q1):
        sl4q = q1 % 4
        slq = q1 % 2
        W("ac", HRSEM[sl4q], tok_hr[q1])
        W("ac", "dv", tok_scb2)
        W("ac", "dv", tok_hq.get(q1 - 2))
        t1 = DO("ac", lambda sl4q=sl4q, slq=slq: A.activation(
            t_f16b[slq][:, 0:4096],
            t_hf[sl4q][:, :, :].rearrange("p a b -> p (a b)"),
            AF.Identity, bias=c_mag[:, :], scale=c_shr[:, :]), "ac")
        tok_hq1[q1] = t1
        W("dv", "ac", t1)
        W("dv", "pe", tok_ff2.get(q1 - 2))
        tok_hq[q1] = DO("dv", lambda slq=slq: V.tensor_scalar(
            t_hq[slq][:, :, :].rearrange("p a b -> p (a b)"),
            t_f16b[slq][:, 0:4096], -MAGIC, None, ALU.add), "dv")

    p6_quant(0)
    for tc in range(16):
        sl = tc % 2
        sl4 = tc % 4
        buf = pA if tc % 2 == 0 else pB
        xlsem = "sXL0" if sl == 0 else "sXL1"
        ysem = "sY0" if sl == 0 else "sY1"
        W("sp", "dv", tok_sum6.get(tc - 2))
        tok_xl6t = DO("sp", lambda tc=tc, sl=sl: SP.dma_start(
            out=t_xl6[sl][:, :], in_=d_xln[tc * 128:(tc + 1) * 128, :]), xlsem, 16)
        W("pe", "dv", tok_hq[tc])
        W("pe", "sW2", tok_w2sld)
        W("pe", "dv", tok_sum6.get(tc - 2))
        for dn in range(2):
            for hc in range(32):
                DO("pe", lambda buf=buf, dn=dn, hc=hc, sl=sl: T.matmul(
                    buf[:, dn * 512:(dn + 1) * 512], t_hq[sl][:, hc, :],
                    t_w2s[:, hc, dn * 512:(dn + 1) * 512],
                    start=(hc == 0), stop=False))
            tk = DO("pe", lambda buf=buf, dn=dn: T.matmul(
                buf[:, dn * 512:(dn + 1) * 512], c_1b[0:1, :],
                c_b2b[0:1, dn * 512:(dn + 1) * 512], start=False, stop=True),
                "pe" if dn == 1 else None)
        tok_ff2[tc] = tk
        if tc + 1 < 16:
            p6_quant(tc + 1)
        if tc + 4 < 16:
            q = "gp" if (tc + 4) % 2 == 0 else "sp"
            ENQ = GP if (tc + 4) % 2 == 0 else SP
            W(q, "ac", tok_hq1[tc])
            tok_hr[tc + 4] = DO(q, lambda tc=tc, sl4=sl4, ENQ=ENQ: ENQ.dma_start(
                out=t_hf[sl4][:, :, :],
                in_=d_hst[:, :, (tc + 4) * 128:(tc + 5) * 128].rearrange(
                    "hc p d -> p hc d")), HRSEM[sl4], 16)
        W("dv", "pe", tok_ff2[tc])
        W("dv", xlsem, tok_xl6t)
        W("dv", ysem, tok_yst.get(tc - 2))
        W("dv", "dv", tok_post6.get(tc - 2))
        tok_sum6[tc] = DO("dv", lambda buf=buf, sl=sl: V.scalar_tensor_tensor(
            t_out[sl][:, :], buf[:, 0:1024], c_s2[:, :], t_xl6[sl][:, :],
            ALU.mult, ALU.add), "dv")
        W("dv", "dv", tok_sum6[tc])
        tok_pre6[tc] = ln_pre(t_out[sl][:, :], t_bn6[sl], t_bx6[sl])
        if tc >= 1:
            p6_post(tc - 1)
    p6_post(15)

    W("sp", "sY0", tok_yst[14])
    W("sp", "sY1", tok_yst[15])

    sch.emit(block)
    es.close()
    return nc


_NC_CACHE = {}


def _get_nc(r0, cc_ranks=8):
    key = (r0, cc_ranks)
    if key not in _NC_CACHE:
        _NC_CACHE[key] = build_nc(r0, cc_ranks)
    return _NC_CACHE[key]


def kernel(L0_attn_x, L0_attn_Wq, L0_attn_bq, L0_attn_Wl, L0_attn_bl,
           L0_attn_Wk, L0_attn_bk, L0_attn_Wv, L0_attn_bv, L0_attn_Wo,
           L0_attn_bo, L0_ffn_w1, L0_ffn_b1, L0_ffn_w2, L0_ffn_b2):
    x = np.ascontiguousarray(np.asarray(L0_attn_x, dtype=np.float32))
    bcat = np.zeros((32, 128), np.float32)
    bcat[0:8] = np.asarray(L0_attn_bq, np.float32).reshape(8, 128)
    bcat[8:10] = np.asarray(L0_attn_bl, np.float32).reshape(2, 128)
    bcat[10:18] = np.asarray(L0_attn_bk, np.float32).reshape(8, 128)
    bcat[18:26] = np.asarray(L0_attn_bv, np.float32).reshape(8, 128)
    common = {
        "Wq": np.ascontiguousarray(np.asarray(L0_attn_Wq, np.float32)),
        "Wl": np.ascontiguousarray(np.asarray(L0_attn_Wl, np.float32)),
        "Wk": np.ascontiguousarray(np.asarray(L0_attn_Wk, np.float32)),
        "Wv": np.ascontiguousarray(np.asarray(L0_attn_Wv, np.float32)),
        "Wo": np.ascontiguousarray(np.asarray(L0_attn_Wo, np.float32)),
        "bcat": bcat,
        "bo": np.asarray(L0_attn_bo, np.float32).reshape(1, D),
        "w1": np.ascontiguousarray(np.asarray(L0_ffn_w1, np.float32)),
        "b1": np.asarray(L0_ffn_b1, np.float32).reshape(32, 128),
        "w2": np.ascontiguousarray(np.asarray(L0_ffn_w2, np.float32)),
        "b2": np.asarray(L0_ffn_b2, np.float32).reshape(1, D),
        "identb": np.eye(128, dtype=ml_dtypes.bfloat16),
        "identf": np.eye(128, dtype=np.float32),
    }
    in_maps = []
    for c in range(8):
        b, half = c // 2, c % 2
        m = dict(common)
        xb = x[b] if half == 0 else np.roll(x[b], -TO, axis=0)
        m["xfull"] = np.ascontiguousarray(xb)
        in_maps.append(m)
    nc = _get_nc(0, 8)
    res = run_bass_kernel_spmd(nc, in_maps, core_ids=list(range(8)))
    out = np.empty((4, S, D), np.float32)
    for c in range(8):
        b, half = c // 2, c % 2
        out[b, half * TO:(half + 1) * TO] = res.results[c]["y"]
    return out


# version 15
# speedup vs baseline: 1.0306x; 1.0306x over previous
import numpy as np
import ml_dtypes
import concourse.bass as bass
import concourse.mybir as mybir
from concourse.bass_utils import run_bass_kernel_spmd

F32 = mybir.dt.float32
F16 = mybir.dt.float16
BF16 = mybir.dt.bfloat16
AF = mybir.ActivationFunctionType
ALU = mybir.AluOpType
AX = mybir.AxisListType

D = 1024
S = 4096
TO = 2048
LATD = 256
HID = 4096
NI = 512
MAGIC = 1152.0
EPS = 1e-5
KB = 1024
ENGS = ("pe", "dv", "ac", "sp", "gp")

SEMS = ("gWA", "gWL", "gWB", "gX0", "gX1", "sB", "sWS0", "sWS1", "sWS2", "sWS3", "sWL0", "sWL1",
        "sWQ0", "sWQ1", "sXT0", "sXT1", "sWT0", "sWT1", "sX320", "sX321", "sXS0", "sXS1",
        "sH0", "sH1", "sH2", "sW10", "sW11", "sW2", "sXL0", "sXL1",
        "sHR0", "sHR1", "sHR2", "sHR3", "sY0", "sY1", "sCC", "sC2", "pe", "dv",
        "ac", "gc")


class Sch:
    def __init__(self, nc):
        self.nc = nc
        self.q = {e: [] for e in ENGS}
        self.sems = {}
        self.wm = {}

    def eng(self, e):
        nc = self.nc
        return {"pe": nc.tensor, "dv": nc.vector, "ac": nc.scalar,
                "sp": nc.sync, "gp": nc.gpsimd}[e]

    def mksem(self, name, es):
        self.sems[name] = [es.enter_context(self.nc.semaphore(name)), 0]

    def n(self, name):
        return self.sems[name][1]

    def wait(self, e, name, val):
        if val is None or val <= 0:
            return
        key = (e, name)
        if self.wm.get(key, 0) >= val:
            return
        self.wm[key] = val
        h = self.sems[name][0]
        eng = self.eng(e)
        self.q[e].append(lambda eng=eng, h=h, val=val: eng.wait_ge(h, val))

    def do(self, e, fn, inc=None, k=1):
        if inc is None:
            self.q[e].append(fn)
            return None
        rec = self.sems[inc]
        rec[1] += k
        h = rec[0]
        self.q[e].append(lambda fn=fn, h=h, k=k: fn().then_inc(h, k))
        return rec[1]

    def emit(self, block):
        for e in ENGS:
            dec = {"pe": block.tensor, "dv": block.vector, "ac": block.scalar,
                   "sp": block.sync, "gp": block.gpsimd}[e]
            ops = self.q[e]

            def body(_eng, ops=ops):
                for f in ops:
                    f()
            dec(body)


def build_nc(r0, cc_ranks=8):
    nc = bass.Bass("TRN2", target_bir_lowering=False)

    d_x = nc.dram_tensor("xfull", [S, D], F32, kind="ExternalInput").ap()
    d_Wq = nc.dram_tensor("Wq", [D, D], F32, kind="ExternalInput").ap()
    d_Wl = nc.dram_tensor("Wl", [D, LATD], F32, kind="ExternalInput").ap()
    d_Wk = nc.dram_tensor("Wk", [LATD, D], F32, kind="ExternalInput").ap()
    d_Wv = nc.dram_tensor("Wv", [LATD, D], F32, kind="ExternalInput").ap()
    d_Wo = nc.dram_tensor("Wo", [D, D], F32, kind="ExternalInput").ap()
    d_bcat = nc.dram_tensor("bcat", [32, 128], F32, kind="ExternalInput").ap()
    d_bo = nc.dram_tensor("bo", [1, D], F32, kind="ExternalInput").ap()
    d_w1 = nc.dram_tensor("w1", [HID, D], F32, kind="ExternalInput").ap()
    d_b1 = nc.dram_tensor("b1", [32, 128], F32, kind="ExternalInput").ap()
    d_w2 = nc.dram_tensor("w2", [D, HID], F32, kind="ExternalInput").ap()
    d_b2 = nc.dram_tensor("b2", [1, D], F32, kind="ExternalInput").ap()
    d_idb = nc.dram_tensor("identb", [128, 128], BF16, kind="ExternalInput").ap()
    d_idf = nc.dram_tensor("identf", [128, 128], F32, kind="ExternalInput").ap()
    d_y = nc.dram_tensor("y", [TO, D], F32, kind="ExternalOutput").ap()

    d_hst = nc.dram_tensor("h_stash", [32, 128, TO], F32).ap()
    d_xln = nc.dram_tensor("xln_stash", [TO, D], F32).ap()
    d_w1q = nc.dram_tensor("w1q_dram", [HID, D], BF16).ap()
    d_w2q = nc.dram_tensor("w2q_dram", [D, HID], BF16).ap()
    d_w1qT = nc.dram_tensor("w1qT_dram", [32, 128, 8, 128], BF16).ap()
    d_w2qT = nc.dram_tensor("w2qT_dram", [32, 128, D], BF16).ap()
    d_cc1i = nc.dram_tensor("cc1i", [1, 8], F32).ap()
    d_cc1o = nc.dram_tensor("cc1o", [1, 8], F32).ap()
    d_cc2i = nc.dram_tensor("cc2i", [1, 8], F32).ap()
    d_cc2o = nc.dram_tensor("cc2o", [1, 8], F32).ap()

    AT = nc.alloc_sbuf_tensor_at
    po = [(int(nc.sbuf_base) + 63) & ~63]

    def perm(name, shape, dtype):
        nbytes = int(np.prod(shape[1:])) * mybir.dt.size(dtype)
        t = AT(name, shape, dtype, offset=po[0])
        po[0] = (po[0] + nbytes + 31) & ~31
        return t

    c_idb = perm("c_idb", [128, 128], BF16)
    c_idf = perm("c_idf", [128, 128], F32)
    c_ball = perm("c_ball", [128, 32], F32)
    c_bvb = perm("c_bvb", [128, 8], BF16)
    c_b1r = perm("c_b1r", [128, 32], F32)
    c_b1s = perm("c_b1s", [128, 32], F32)
    c_bob = perm("c_bob", [1, D], BF16)
    c_b2b = perm("c_b2b", [1, D], BF16)
    c_1f = perm("c_1f", [1, 128], F32)
    c_sel = perm("c_sel", [128, 128], F32)
    c_1b = perm("c_1b", [1, 128], BF16)
    c_sc = perm("c_sc", [1, 16], F32)
    c_s8 = perm("c_s8", [1, 8], F32)
    c_wred = perm("c_wred", [128, 2], F32)
    c_wr1 = perm("c_wr1", [128, 1], F32)
    c_wr2 = perm("c_wr2", [128, 1], F32)
    c_sxr = perm("c_sxr", [128, 1], F32)
    c_s1 = perm("c_s1", [128, 1], F32)
    c_shr = perm("c_shr", [128, 1], F32)
    c_s2 = perm("c_s2", [128, 1], F32)
    c_eps = perm("c_eps", [128, 1], F32)
    c_mag = perm("c_mag", [128, 1], F32)
    c_rc = perm("c_rc", [1, 4], F32)
    c_stage = perm("c_stage", [32, 128], F32)

    PB = (po[0] + 63) & ~63

    def at(name, shape, dtype, off_kb):
        return AT(name, shape, dtype, offset=PB + (int(off_kb * KB) & ~31))

    t_lat = at("t_lat", [128, 2, S], BF16, 0)
    t_q = at("t_q", [128, 8, TO], BF16, 16)
    t_o = at("t_o", [128, 8, TO], BF16, 48)
    t_P = at("t_P", [128, 8, 2, NI], BF16, 80)
    t_kg = at("t_kg", [128, 2, S], BF16, 112)
    t_vg = at("t_vg", [128, 32, 4, 65], BF16, 128)
    t_Wk = at("t_Wk", [128, 2, D], BF16, 145)
    t_Wv = at("t_Wv", [128, 2, D], BF16, 149)
    t_Wo = at("t_Wo", [128, 8, D], BF16, 153)
    t_wld = [at(f"t_wld{i}", [128, D], F32, [169, 173, 179, 183][i])
             for i in range(4)]
    t_wf16 = at("t_wf16", [128, D], F16, 177)
    t_wqb = [at(f"t_wqb{i}", [128, D], BF16, 179 + 2 * i) for i in range(2)]
    t_wtr = [at(f"t_wtr{i}", [128, 2048], BF16, 183 + 4 * i) for i in range(2)]
    t_slr = at("t_slr", [128, NI], F32, 191)
    t_slb = at("t_slb", [128, NI], F32, 195)
    t_sll = at("t_sll", [128, NI], F32, 96)
    t_slx = at("t_slx", [128, NI], F32, 98)
    t_bo32 = at("t_bo32", [1, D], F32, 112)
    t_b2r = at("t_b2r", [1, D], F32, 116)

    t_Wq = at("t_Wq", [128, 8, D], BF16, 48)
    t_Wl = at("t_Wl", [128, 8, LATD], BF16, 64)
    t_x32b = [at(f"t_x32b{i}", [128, D], BF16, 68 + 4 * i) for i in range(2)]
    t_xb = [at(f"t_xb{i}", [128, D], BF16, 76 + 2 * i) for i in range(2)]
    t_xf = [at(f"t_xf{i}", [128, 8, 128], BF16, 80 + 2 * i) for i in range(2)]
    t_wm = at("t_wm", [128, 64], F32, 84)

    t_x32 = [at(f"t_x32{i}", [128, D], F32, 112 + 4 * i) for i in range(2)]
    t_sum = [at(f"t_sum{i}", [128, D], F32, 120 + 4 * i) for i in range(2)]
    t_bn = [at(f"t_bn{i}", [128, 12], F32, 128 + 0.1 * i) for i in range(2)]
    t_bnx = [at(f"t_bnx{i}", [128, 16], F32, 128.2 + 0.1 * i) for i in range(2)]
    t_rm = at("t_rm", [128, 16], F32, 128.75)

    t_xq = at("t_xq", [128, 8, TO], BF16, 0)
    t_w2s = at("t_w2s", [128, 32, D], BF16, 32)
    t_w1s = [at(f"t_w1s{i}", [128, 8, 128], BF16, 96 + 2 * i) for i in range(2)]
    t_ht = [at(f"t_ht{i}", [128, NI], F32, 100 + 2 * i) for i in range(3)]
    t_hm = at("t_hm", [128, 128], F32, 106)
    t_xl5 = [at(f"t_xl5{i}", [128, D], F32, 84 + 4 * i) for i in range(2)]
    t_qf16 = at("t_qf16", [128, D], F16, 115)
    t_f16b = [at(f"t_f16b{i}", [128, HID], F16, [117, 107][i]) for i in range(2)]

    t_xf5 = [at(f"t_xf5{i}", [128, 8, 128], F32, 125 + 4 * i) for i in range(16)]
    t_w2l = [at(f"t_w2l{i}", [128, HID], BF16, 32 + 8 * i) for i in range(2)]
    t_w2x = [at(f"t_w2x{i}", [128, 32, 128], BF16, [183, 191][i]) for i in range(2)]
    t_xr = [at(f"t_xr{i}", [128, D], F32, 129 + 4 * i) for i in range(2)]
    t_hq = [at(f"t_hq{i}", [128, 32, 128], BF16, 125 + 8 * i) for i in range(2)]
    t_hf = ([at(f"t_hf{i}", [128, 32, 128], F32, 141 + 16 * i) for i in range(2)]
            + [at(f"t_hfb{i}", [128, 32, 128], F32, 0 + 16 * i) for i in range(2)])
    t_xl6 = [at(f"t_xl6{i}", [128, D], F32, 173 + 4 * i) for i in range(2)]
    t_out = [at(f"t_out{i}", [128, D], F32, 181 + 4 * i) for i in range(2)]
    t_bn6 = [at(f"t_bn6{i}", [128, 12], F32, 189 + 0.1 * i) for i in range(2)]
    t_bx6 = [at(f"t_bx6{i}", [128, 16], F32, 189.2 + 0.1 * i) for i in range(2)]

    pA = nc.alloc_psum_tensor("pA", [128, 1536], F32)
    pB = nc.alloc_psum_tensor("pB", [128, 1536], F32)
    pC = nc.alloc_psum_tensor("pC", [128, 512], F32)
    pD = nc.alloc_psum_tensor("pD", [128, 512], F32)

    sch = Sch(nc)
    from contextlib import ExitStack
    es = ExitStack()
    for s in SEMS:
        sch.mksem(s, es)
    block = es.enter_context(nc.Block())

    W = sch.wait
    DO = sch.do
    T = nc.tensor
    V = nc.vector
    A = nc.scalar
    SP = nc.sync
    GP = nc.gpsimd

    DO("sp", lambda: SP.dma_start(out=c_idb[:, :], in_=d_idb), "sB", 16)
    DO("sp", lambda: SP.dma_start(out=c_idf[:, :], in_=d_idf), "sB", 16)
    DO("sp", lambda: SP.dma_start(out=c_stage[:, :], in_=d_bcat), "sB", 16)
    DO("sp", lambda: SP.dma_start(out=t_bo32[0:1, :], in_=d_bo), "sB", 16)
    tok_b0 = DO("sp", lambda: SP.dma_start(out=t_b2r[0:1, :], in_=d_b2), "sB", 16)

    DO("dv", lambda: V.memset(c_eps[:, :], EPS))
    DO("dv", lambda: V.memset(c_mag[:, :], MAGIC))
    DO("dv", lambda: V.memset(c_1f[0:1, :], 1.0))
    t_sel0 = DO("dv", lambda: V.memset(c_sel[:, :], 0.0), "dv")
    W("dv", "dv", t_sel0)
    DO("dv", lambda: V.memset(c_sel[0:1, 0:64], 1.0))
    DO("dv", lambda: V.memset(c_sel[64:65, 64:128], 1.0))
    DO("dv", lambda: V.memset(t_sll[:, :], 1.0))
    DO("dv", lambda: V.memset(c_1b[0:1, :], 1.0))
    tok_ones = DO("dv", lambda: V.memset(t_vg[:, :, :, 64], 1.0), "dv")

    tok_wL = DO("gp", lambda: GP.dma_start(
        out=t_Wl[:, :, :], in_=d_Wl.rearrange("(kc p) m -> p kc m", p=128)), "gWL", 16)
    tok_wA = DO("gp", lambda: GP.dma_start(
        out=t_Wq[:, :, :], in_=d_Wq.rearrange("(kc p) m -> p kc m", p=128)), "gWA", 16)
    DO("gp", lambda: GP.dma_start(
        out=t_Wk[:, :, :], in_=d_Wk.rearrange("(kc p) m -> p kc m", p=128)), "gWB", 16)
    DO("gp", lambda: GP.dma_start(
        out=t_Wv[:, :, :], in_=d_Wv.rearrange("(kc p) m -> p kc m", p=128)), "gWB", 16)
    tok_wall = DO("gp", lambda: GP.dma_start(
        out=t_Wo[:, :, :], in_=d_Wo.rearrange("(kc p) m -> p kc m", p=128)), "gWB", 16)

    W("pe", "sB", tok_b0)
    tok_bT = DO("pe", lambda: T.transpose(pC[:, 0:32], c_stage[0:32, :],
                                          c_idf[0:32, 0:32]), "pe")
    W("dv", "pe", tok_bT)
    tok_ballc = DO("dv", lambda: V.tensor_copy(c_ball[:, :], pC[:, 0:32]), "dv")
    W("dv", "dv", tok_ballc)
    tok_ball = DO("dv", lambda: V.tensor_copy(c_bvb[:, :], c_ball[:, 18:26]), "dv")

    W("sp", "pe", tok_bT)
    tok_b1ld = DO("sp", lambda: SP.dma_start(out=c_stage[:, :], in_=d_b1), "sB", 16)
    W("pe", "sB", tok_b1ld)
    W("pe", "dv", tok_ball)
    tok_b1T = DO("pe", lambda: T.transpose(pC[:, 32:64], c_stage[0:32, :],
                                           c_idf[0:32, 0:32]), "pe")
    W("ac", "pe", tok_b1T)
    tok_b1r = DO("ac", lambda: A.activation(c_b1r[:, :], pC[:, 32:64],
                                            AF.Identity, bias=0.0, scale=1.0),
                 "ac")

    W("ac", "dv", tok_ball)
    tok_tr = {}
    tok_xf = {(-1, 1): tok_b1r}
    tok_latmm = {}
    tok_latev = {}
    tok_qmm = {}
    tok_qev = {}
    qcnt = 0
    tok_wsred = {}

    def emit_scan_job(i):
        sl = i % 4
        ssem = ("sWS0", "sWS1", "sWS2", "sWS3")[sl]
        q = "gp" if sl == 0 else "sp"
        ENQ = GP if sl == 0 else SP
        if i < 32:
            src = d_w1[i * 128:(i + 1) * 128, :]
        else:
            j = i - 32
            rc, cs = j // 4, j % 4
            src = d_w2[rc * 128:(rc + 1) * 128, cs * 1024:(cs + 1) * 1024]
        W(q, "dv", tok_wsred.get(i - 4))
        tok_ld = DO(q, lambda src=src, sl=sl, ENQ=ENQ: ENQ.dma_start(
            out=t_wld[sl][:, :], in_=src), ssem, 16)
        W("dv", ssem, tok_ld)
        tok_wsred[i] = DO("dv", lambda sl=sl, i=i: V.tensor_reduce(
            t_wm[:, i:i + 1], t_wld[sl][:, :], AX.X, ALU.max,
            apply_absolute_value=True), "dv")

    for tit in range(33):
        if tit < 32:
            ti = tit
            sl = ti % 2
            xsem = "gX0" if sl == 0 else "gX1"
            W("gp", "pe", tok_tr.get(ti - 2))
            tok_xld = DO("gp", lambda ti=ti, sl=sl: GP.dma_start(
                out=t_x32b[sl][:, :], in_=d_x[ti * 128:(ti + 1) * 128, :]), xsem, 16)
            W("pe", xsem, tok_xld)
            for half, pbank in ((0, pD), (1, pC)):
                W("pe", "ac", tok_xf.get((ti - 1, half)))
                for kc4 in range(4):
                    kc = half * 4 + kc4
                    last = DO("pe", lambda sl=sl, kc=kc, kc4=kc4, pbank=pbank:
                              T.matmul(pbank[:, kc4 * 128:(kc4 + 1) * 128],
                                       t_x32b[sl][:, kc * 128:(kc + 1) * 128],
                                       c_idb[:, :], start=True, stop=True),
                              "pe" if kc4 == 3 else None)
                tok_tr[(ti, half)] = last
                W("ac", "pe", tok_tr[(ti, half)])
                tok_xf[(ti, half)] = DO("ac", lambda sl=sl, half=half, pbank=pbank:
                                        A.activation(
                                            t_xf[sl][:, half * 4:half * 4 + 4, :]
                                            .rearrange("p a b -> p (a b)"),
                                            pbank[:, :], AF.Identity, bias=0.0,
                                            scale=1.0), "ac")
            tok_tr[ti] = tok_tr[(ti, 1)]
            tok_xf[ti] = tok_xf[(ti, 1)]
        if tit < 1:
            continue
        ti = tit - 1
        sl = ti % 2
        W("pe", "ac", tok_xf[(ti, 0)])
        W("pe", "ac", tok_xf[(ti, 1)])
        W("pe", "gWL", tok_wL)
        W("pe", "ac", tok_latev.get(ti - 2))
        lo = 0 if ti % 2 == 0 else 512
        for mc in range(2):
            for kc in range(8):
                tk = DO("pe", lambda sl=sl, mc=mc, kc=kc, lo=lo: T.matmul(
                    pA[:, lo + mc * 128: lo + (mc + 1) * 128],
                    t_Wl[:, kc, mc * 128:(mc + 1) * 128],
                    t_xf[sl][:, kc, :], start=(kc == 0), stop=(kc == 7)),
                    "pe" if (mc == 1 and kc == 7) else None)
        tok_latmm[ti] = tk
        W("ac", "pe", tok_latmm[ti])
        for mc in range(2):
            tk = DO("ac", lambda ti=ti, mc=mc, lo=lo: A.activation(
                t_lat[:, mc, ti * 128:(ti + 1) * 128],
                pA[:, lo + mc * 128: lo + (mc + 1) * 128],
                AF.Identity, bias=c_ball[:, 8 + mc:9 + mc], scale=1.0), "ac")
        tok_latev[ti] = tk
        if ti < 16:
            tloc = ti * 128
            for half in range(2):
                qs = (qcnt % 2) * 512
                W("pe", "gWA", tok_wA)
                W("pe", "ac", tok_qev.get(qcnt - 2))
                for mc in range(4):
                    for kc in range(8):
                        tk = DO("pe", lambda sl=sl, half=half, mc=mc, kc=kc, qs=qs:
                                T.matmul(pB[:, qs + mc * 128: qs + (mc + 1) * 128],
                                         t_Wq[:, kc, (half * 4 + mc) * 128:
                                              (half * 4 + mc + 1) * 128],
                                         t_xf[sl][:, kc, :],
                                         start=(kc == 0), stop=(kc == 7)),
                                "pe" if (mc == 3 and kc == 7) else None)
                tok_qmm[qcnt] = tk
                W("ac", "pe", tok_qmm[qcnt])
                for mc in range(4):
                    tk = DO("ac", lambda half=half, mc=mc, qs=qs, tloc=tloc:
                            A.activation(
                                t_q[:, half * 4 + mc, tloc:tloc + 128],
                                pB[:, qs + mc * 128: qs + (mc + 1) * 128],
                                AF.Identity,
                                bias=c_ball[:, half * 4 + mc: half * 4 + mc + 1],
                                scale=1.0), "ac")
                tok_qev[qcnt] = tk
                qcnt += 1
        emit_scan_job(2 * ti)
        emit_scan_job(2 * ti + 1)

    tok_q_all = tok_qev[qcnt - 1]
    tok_lat_all = tok_latev[31]

    W("dv", "dv", tok_wsred[63])
    DO("dv", lambda: V.tensor_reduce(c_wred[:, 0:1], t_wm[:, 0:32], AX.X, ALU.max))
    tok_wred = DO("dv", lambda: V.tensor_reduce(
        c_wred[:, 1:2], t_wm[:, 32:64], AX.X, ALU.max), "dv")
    W("pe", "dv", tok_wred)
    W("pe", "ac", tok_b1r)
    DO("pe", lambda: T.transpose(pC[0:1, 64:192], c_wred[:, 0:1], c_idf[:, :]))
    tok_wT = DO("pe", lambda: T.transpose(pC[0:1, 192:320], c_wred[:, 1:2],
                                          c_idf[:, :]), "pe")
    W("dv", "pe", tok_wT)
    DO("dv", lambda: V.tensor_reduce(c_sc[0:1, 0:1], pC[0:1, 64:192], AX.X, ALU.max))
    tok_w12 = DO("dv", lambda: V.tensor_reduce(
        c_sc[0:1, 1:2], pC[0:1, 192:320], AX.X, ALU.max), "dv")
    W("dv", "dv", tok_w12)
    DO("dv", lambda: V.tensor_single_scalar(c_sc[0:1, 2:3], c_sc[0:1, 0:1], 1e-6, ALU.max))
    tok_m = DO("dv", lambda: V.tensor_single_scalar(
        c_sc[0:1, 3:4], c_sc[0:1, 1:2], 1e-6, ALU.max), "dv")
    W("dv", "dv", tok_m)
    DO("dv", lambda: V.reciprocal(c_sc[0:1, 4:5], c_sc[0:1, 2:3]))
    tok_r = DO("dv", lambda: V.reciprocal(c_sc[0:1, 5:6], c_sc[0:1, 3:4]), "dv")
    W("dv", "dv", tok_r)
    DO("dv", lambda: V.tensor_single_scalar(c_sc[0:1, 6:7], c_sc[0:1, 4:5], 127.0, ALU.mult))
    DO("dv", lambda: V.tensor_single_scalar(c_sc[0:1, 7:8], c_sc[0:1, 5:6], 127.0, ALU.mult))
    DO("dv", lambda: V.tensor_single_scalar(c_sc[0:1, 8:9], c_sc[0:1, 2:3], 1.0 / 127.0, ALU.mult))
    tok_sw = DO("dv", lambda: V.tensor_single_scalar(
        c_sc[0:1, 9:10], c_sc[0:1, 3:4], 1.0 / 127.0, ALU.mult), "dv")
    W("pe", "dv", tok_sw)
    DO("pe", lambda: T.matmul(pC[:, 320:321], c_1f[0:1, :], c_sc[0:1, 6:7],
                              start=True, stop=True))
    tok_wbc = DO("pe", lambda: T.matmul(pC[:, 321:322], c_1f[0:1, :], c_sc[0:1, 7:8],
                                        start=True, stop=True), "pe")
    W("dv", "pe", tok_wbc)
    DO("dv", lambda: V.tensor_copy(c_wr1[:, :], pC[:, 320:321]))
    tok_wrb = DO("dv", lambda: V.tensor_copy(c_wr2[:, :], pC[:, 321:322]), "dv")

    W("pe", "gWB", tok_wall)
    W("pe", "dv", tok_wrb)
    tok_bvE = None
    for part in range(2):
        W("pe", "dv", tok_bvE)
        for kc in range(8):
            tk = DO("pe", lambda part=part, kc=kc: T.matmul(
                pC[0:1, 0:512],
                c_bvb[:, kc:kc + 1], t_Wo[:, kc, part * 512:(part + 1) * 512],
                start=(kc == 0), stop=(kc == 7)), "pe" if kc == 7 else None)
        W("dv", "pe", tk)
        tok_bvE = DO("dv", lambda part=part: V.tensor_tensor(
            c_bob[0:1, part * 512:(part + 1) * 512], pC[0:1, 0:512],
            t_bo32[0:1, part * 512:(part + 1) * 512], ALU.add), "dv")
    tok_b2b = DO("dv", lambda: V.tensor_copy(c_b2b[0:1, :], t_b2r[0:1, :]), "dv")

    wprep_quant = [("w1", i) for i in range(32)] + [("w2", j) for j in range(32)]
    tok_wq2 = {}
    tok_wqst = {}
    tok_wld_d = {}

    def emit_wquant_load(idx):
        kind, i = wprep_quant[idx]
        sl = idx % 2
        lsem = "sWL0" if sl == 0 else "sWL1"
        if kind == "w1":
            src_ = d_w1[i * 128:(i + 1) * 128, :]
        else:
            rc, cs = i // 4, i % 4
            src_ = d_w2[rc * 128:(rc + 1) * 128, cs * 1024:(cs + 1) * 1024]
        if idx < 2:
            W("sp", "dv", tok_wsred[60 + idx])
        W("sp", "dv", tok_wq2.get(idx - 2))
        tok_wld_d[idx] = DO("sp", lambda src_=src_, sl=sl: SP.dma_start(
            out=t_wld[sl][:, :], in_=src_), lsem, 16)

    def emit_wquant_comp(idx):
        kind, i = wprep_quant[idx]
        sl = idx % 2
        lsem = "sWL0" if sl == 0 else "sWL1"
        qsem = "sWQ0" if sl == 0 else "sWQ1"
        if kind == "w1":
            dst = d_w1q[i * 128:(i + 1) * 128, :]
            wr = c_wr1
        else:
            rc, cs = i // 4, i % 4
            dst = d_w2q[rc * 128:(rc + 1) * 128, cs * 1024:(cs + 1) * 1024]
            wr = c_wr2
        W("dv", lsem, tok_wld_d[idx])
        W("dv", "dv", tok_wrb)
        W("dv", "dv", tok_wq2.get(idx - 1))
        W("dv", qsem, tok_wqst.get(idx - 2))
        st1 = DO("dv", lambda sl=sl, wr=wr: V.tensor_scalar(
            t_wf16[:, :], t_wld[sl][:, :], wr[:, :], MAGIC, ALU.mult, ALU.add), "dv")
        W("dv", "dv", st1)
        tok_wq2[idx] = DO("dv", lambda sl=sl: V.tensor_scalar(
            t_wqb[sl][:, :], t_wf16[:, :], -MAGIC, None, ALU.add), "dv")
        W("sp", "dv", tok_wq2[idx])
        tok_wqst[idx] = DO("sp", lambda dst=dst, sl=sl: SP.dma_start(
            out=dst, in_=t_wqb[sl][:, :]), qsem, 16)

    xbar_jobs = [("w1", c, half) for c in range(8) for half in range(2)]
    tok_wt = {}
    xbar_i = [0]

    def emit_xbar_job(idx):
        kind, a, b = xbar_jobs[idx]
        sl = idx % 2
        wt = t_wtr[sl]
        xtsem = "sXT0" if sl == 0 else "sXT1"
        wtsem = "sWT0" if sl == 0 else "sWT1"
        W("sp", wtsem, tok_wt.get(idx - 2))
        if kind == "w1":
            W("sp", "sWQ0", tok_wqst[30])
            W("sp", "sWQ1", tok_wqst[31])
            src_ = d_w1q[b * 2048:(b + 1) * 2048, a * 128:(a + 1) * 128]
            tok_x = DO("sp", lambda src_=src_, wt=wt: SP.dma_start(
                out=wt[:, :], in_=src_, transpose=True), xtsem, 16)
            W("sp", xtsem, tok_x)
            tok_wt[idx] = DO("sp", lambda a=a, b=b, wt=wt: SP.dma_start(
                out=d_w1qT[b * 16:(b + 1) * 16, :, a, :].rearrange(
                    "hc p m -> p hc m"),
                in_=wt[:, :].rearrange("p (hc m) -> p hc m", m=128)),
                wtsem, 16)
        else:
            W("sp", "sWQ0", tok_wqst[62])
            W("sp", "sWQ1", tok_wqst[63])
            src_ = d_w2q[:, a * 128:(a + 1) * 128]
            tok_x = DO("sp", lambda src_=src_, wt=wt: SP.dma_start(
                out=wt[:, 0:1024], in_=src_, transpose=True), xtsem, 16)
            W("sp", xtsem, tok_x)
            tok_wt[idx] = DO("sp", lambda a=a, wt=wt: SP.dma_start(
                out=d_w2qT[a, :, :], in_=wt[:, 0:1024]), wtsem, 16)

    tok_stt_prev = None
    tok_copy_prev = None
    tok_av_final_prev = None
    pend_ln = [None]
    tok_stt_pair = {0: None, 1: None}
    pend = [None]
    tok_bc_last = [None]
    tok_ln_last = [None]
    tok_exp = {}
    tok_av = {}
    last_scores_tok = None
    pit = 0
    wprep_i = 0

    def emit_pend_ln():
        if pend_ln[0] is None:
            return
        tok_lcp = pend_ln[0]
        pend_ln[0] = None
        W("ac", "dv", tok_lcp)
        tok_ln = DO("ac", lambda: A.activation(
            t_slx[:, :], t_sll[:, :], AF.Ln, bias=0.0, scale=1.0), "ac")
        W("ac", "ac", tok_ln)
        W("ac", "pe", tok_bc_last[0])
        tok_recip = DO("ac", lambda: A.activation(
            t_slr[:, :], t_slx[:, :], AF.Exp, bias=0.0, scale=-1.0), "ac")
        tok_ln_last[0] = tok_recip
        if pend[0] is not None:
            pend[0]["tok_recip"] = tok_recip

    def flush_tail(tok_pa_free):
        nonlocal tok_stt_prev, tok_copy_prev
        p = pend[0]
        if p is None:
            return
        pend[0] = None
        W("pe", "ac", p["tok_recip"])
        W("pe", "ac", tok_pa_free)
        W("pe", "dv", tok_copy_prev)
        tok_bc = DO("pe", lambda: T.matmul(pA[:, 0:512], c_sel[:, :],
                                           t_slr[:, :], start=True, stop=True),
                    "pe")
        tok_bc_last[0] = tok_bc
        W("dv", "pe", tok_bc)
        W("dv", "dv", tok_stt_prev)
        tok_copy = DO("dv", lambda: V.tensor_copy(t_slb[:, :], pA[:, 0:512]), "dv")
        W("dv", "dv", tok_copy)
        avA, avB = p["avA"], p["avB"]
        DO("dv", lambda avA=avA, p=p: V.scalar_tensor_tensor(
            t_o[0:64, p["qsub"], p["it"] * 512:(p["it"] + 1) * 512],
            avA[0:64, :], 1.0, t_slb[0:64, :], ALU.mult, ALU.mult), "dv")
        tok_stt = DO("dv", lambda avB=avB, p=p: V.scalar_tensor_tensor(
            t_o[64:128, p["qsub"], p["it"] * 512:(p["it"] + 1) * 512],
            avB[0:64, :], 1.0, t_slb[64:128, :], ALU.mult, ALU.mult), "dv")
        tok_stt_pair[p["parity"]] = ("dv", tok_stt)
        tok_stt_prev = tok_stt
        tok_copy_prev = tok_copy

    tok_stt_pair[0] = ("ac", tok_xf[(31, 1)])
    tok_stt_pair[1] = ("ac", tok_xf[(31, 0)])
    for g in range(4):
        emit_pend_ln()
        flush_tail(tok_exp.get((pit - 1, 30)))
        kv_par = 0
        tok_kv_ev = {}
        for r in range(2):
            for jt in range(8):
                ps = pC[:, 0:512] if kv_par % 2 == 0 else pB[:, 0:512]
                W("pe", "dv", tok_stt_prev)
                W("pe", "ac", tok_exp.get((pit - 1, 31)))
                W("pe", "dv", tok_copy_prev)
                W("pe", "gWB", tok_wall)
                W("pe", "ac", tok_lat_all)
                W("pe", "dv", tok_b2b)
                W("pe", "dv", tok_kv_ev.get(kv_par - 2))
                for lc in range(2):
                    tk = DO("pe", lambda g=g, r=r, jt=jt, lc=lc, ps=ps: T.matmul(
                        ps, t_Wk[:, lc, g * 256 + r * 128: g * 256 + r * 128 + 128],
                        t_lat[:, lc, jt * 512:(jt + 1) * 512],
                        start=(lc == 0), stop=(lc == 1)), "pe" if lc == 1 else None)
                W("dv", "pe", tk)
                W("dv", "pe", last_scores_tok)
                tok_kv_ev[kv_par] = DO("dv", lambda g=g, r=r, jt=jt, ps=ps:
                                       V.tensor_scalar(
                                           t_kg[:, r, jt * 512:(jt + 1) * 512], ps,
                                           c_ball[:, 10 + 2 * g + r: 11 + 2 * g + r],
                                           None, ALU.add), "dv")
                kv_par += 1
        for jc in range(32):
            ps = pC[:, 0:256] if kv_par % 2 == 0 else pB[:, 0:256]
            W("pe", "dv", tok_kv_ev.get(kv_par - 2))
            for lc in range(2):
                tk = DO("pe", lambda g=g, jc=jc, lc=lc, ps=ps: T.matmul(
                    ps, t_lat[:, lc, jc * 128:(jc + 1) * 128],
                    t_Wv[:, lc, g * 256:(g + 1) * 256],
                    start=(lc == 0), stop=(lc == 1)), "pe" if lc == 1 else None)
            W("dv", "pe", tk)
            W("dv", "pe", tok_av_final_prev)
            W("dv", "dv", tok_ones)
            tok_kv_ev[kv_par] = DO("dv", lambda jc=jc, ps=ps: V.tensor_copy(
                t_vg[:, jc, :, 0:64],
                ps.rearrange("p (hh d) -> p hh d", d=64)), "dv")
            kv_par += 1
        tok_kv_all = tok_kv_ev[kv_par - 1]

        for hp in range(2):
            ksub = hp
            qsub = 2 * g + hp
            hhA, hhB = 2 * hp, 2 * hp + 1
            for it in range(4):
                tok_sc = {}
                parity = pit % 2
                avA = pC[:, 0:512] if parity == 0 else pB[:, 1024:1536]
                avB = pA[:, 1024:1536] if parity == 0 else pD[:, 0:512]
                for gr in range(34):
                    if gr == 2:
                        emit_pend_ln()
                    if gr == 4:
                        flush_tail(tok_exp[(pit, 2)])
                    if gr < 32:
                        sb = pA if gr % 2 == 0 else pB
                        W("pe", "ac", tok_exp.get((pit, gr - 2)))
                        if gr == 4:
                            W("pe", "dv", tok_copy_prev)
                        if gr <= 1:
                            W("pe", "ac", tok_exp.get((pit - 1, 30 + gr)))
                            W("pe", "dv", tok_kv_all)
                            W("pe", "ac", tok_q_all)
                        DO("pe", lambda sb=sb, gr=gr, ksub=ksub, qsub=qsub, it=it:
                           T.matmul(sb[:, 0:512],
                                    t_kg[0:64, ksub, gr * 128:(gr + 1) * 128],
                                    t_q[0:64, qsub, it * 512:(it + 1) * 512],
                                    start=True, stop=True))
                        tok_sc[gr] = DO("pe", lambda sb=sb, gr=gr, ksub=ksub,
                                        qsub=qsub, it=it: T.matmul(
                                            sb[:, 512:1024],
                                            t_kg[64:128, ksub, gr * 128:(gr + 1) * 128],
                                            t_q[64:128, qsub, it * 512:(it + 1) * 512],
                                            start=True, stop=True), "pe")
                        W("ac", "pe", tok_sc[gr])
                        ring = tok_av.get((pit, gr - 8 + ((gr - 8) % 2 == 0))) \
                            if gr >= 8 else tok_av.get((pit - 1, gr + 24 + ((gr + 24) % 2 == 0)))
                        W("ac", "pe", ring)
                        tok_exp[(pit, gr)] = DO(
                            "ac", lambda sb=sb, gr=gr: A.activation(
                                t_P[:, gr % 8, :, :].rearrange("p c n -> p (c n)"),
                                sb[:, 0:1024], AF.Exp, bias=0.0, scale=0.125), "ac")
                    if gr >= 2:
                        ga = gr - 2
                        W("pe", "ac", tok_exp[(pit, ga)])
                        if ga == 0:
                            sp_e, sp_t = tok_stt_pair[parity]
                            W("pe", sp_e, sp_t)
                        DO("pe", lambda ga=ga, hhA=hhA, avA=avA: T.matmul(
                            avA[0:65, :], t_vg[:, ga, hhA, :], t_P[:, ga % 8, 0, :],
                            start=(ga == 0), stop=(ga == 31)))
                        tk = DO("pe", lambda ga=ga, hhB=hhB, avB=avB: T.matmul(
                            avB[0:65, :], t_vg[:, ga, hhB, :], t_P[:, ga % 8, 1, :],
                            start=(ga == 0), stop=(ga == 31)),
                            "pe" if ga % 2 == 1 else None)
                        if ga % 2 == 1:
                            tok_av[(pit, ga)] = tk
                tok_av_final = tok_av[(pit, 31)]
                last_scores_tok = tok_sc[31]
                W("dv", "pe", tok_av_final)
                W("dv", "ac", tok_ln_last[0])
                DO("dv", lambda avA=avA: V.tensor_copy(t_sll[0:1, :],
                                                       avA[64:65, :]))
                tok_lcp = DO("dv", lambda avB=avB: V.tensor_copy(
                    t_sll[64:65, :], avB[64:65, :]), "dv")
                pend_ln[0] = tok_lcp
                pend[0] = {"avA": avA, "avB": avB, "parity": parity,
                           "qsub": qsub, "it": it, "tok_recip": None}
                tok_av_final_prev = tok_av_final
                for j in (2 * wprep_i - 2, 2 * wprep_i - 1):
                    if 0 <= j < 64:
                        emit_wquant_comp(j)
                for j in (2 * wprep_i, 2 * wprep_i + 1):
                    if j < 64:
                        emit_wquant_load(j)
                if wprep_i >= 16:
                    for _ in range(2):
                        if xbar_i[0] < 16:
                            emit_xbar_job(xbar_i[0])
                            xbar_i[0] += 1
                wprep_i += 1
                pit += 1

    tok_womm = {}
    tok_sumev = {}
    tok_rmred = {}
    tok_x32 = {}
    tok_xst = {}

    def ln_pre(sum_ap, bn, bnx):
        bn_m = bn[:, 0:12].rearrange("p (a b) -> p a b", b=3)[:, :, 1]
        bn_cv = bn[:, 0:12].rearrange("p (a b) -> p a b", b=3)[:, :, 2]
        DO("dv", lambda: V.bn_stats(bn[:, 0:6], sum_ap[:, 0:512]))
        t1 = DO("dv", lambda: V.bn_stats(bn[:, 6:12], sum_ap[:, 512:1024]), "dv")
        W("dv", "dv", t1)
        DO("dv", lambda: V.tensor_reduce(bnx[:, 0:1], bn_m, AX.X, ALU.add))
        t2 = DO("dv", lambda: V.tensor_tensor(bnx[:, 8:12], bn_m, bn_m,
                                              ALU.mult), "dv")
        W("dv", "dv", t2)
        DO("dv", lambda: V.scalar_tensor_tensor(bnx[:, 12:16], bnx[:, 8:12], 256.0,
                                                bn_cv, ALU.mult, ALU.add))
        t3 = DO("dv", lambda: V.tensor_single_scalar(bnx[:, 2:3], bnx[:, 0:1],
                                                     0.25, ALU.mult), "dv")
        W("dv", "dv", t3)
        DO("dv", lambda: V.tensor_reduce(bnx[:, 1:2], bnx[:, 12:16], AX.X, ALU.add))
        t4 = DO("dv", lambda: V.tensor_tensor(bnx[:, 3:4], bnx[:, 2:3], bnx[:, 2:3],
                                              ALU.mult), "dv")
        W("dv", "dv", t4)
        t5 = DO("dv", lambda: V.scalar_tensor_tensor(bnx[:, 4:5], bnx[:, 1:2],
                                                     1.0 / 1024.0, bnx[:, 3:4],
                                                     ALU.mult, ALU.subtract), "dv")
        W("ac", "dv", t5)
        t6 = DO("ac", lambda: A.activation(bnx[:, 5:6], bnx[:, 4:5], AF.Ln,
                                           bias=c_eps[:, :], scale=1.0), "ac")
        W("ac", "ac", t6)
        t7 = DO("ac", lambda: A.activation(bnx[:, 6:7], bnx[:, 5:6], AF.Exp,
                                           bias=0.0, scale=-0.5), "ac")
        return t7, t5

    def ln_post(sum_ap, out_ap, bnx, tok_rstd):
        W("dv", "ac", tok_rstd)
        t8 = DO("dv", lambda: V.scalar_tensor_tensor(bnx[:, 7:8], bnx[:, 2:3], -1.0,
                                                     bnx[:, 6:7], ALU.mult,
                                                     ALU.mult), "dv")
        W("dv", "dv", t8)
        t9 = DO("dv", lambda: V.tensor_scalar(out_ap, sum_ap, bnx[:, 6:7],
                                              bnx[:, 7:8], ALU.mult, ALU.add), "dv")
        return t9

    emit_pend_ln()
    flush_tail(tok_exp.get((pit - 1, 30)))
    emit_wquant_comp(62)
    emit_wquant_comp(63)
    W("sp", "pe", tok_av_final_prev)
    W("sp", "dv", tok_stt_prev)
    W("gp", "pe", tok_av_final_prev)
    W("gp", "dv", tok_stt_prev)
    tok_pre = {}

    def p3_post(tc):
        sl = tc % 2
        W("dv", "sXS0" if sl == 0 else "sXS1", tok_xst.get(tc - 2))
        tok_x32[tc] = ln_post(t_sum[sl][:, :], t_x32[sl][:, :], t_bnx[sl],
                              tok_pre[tc][0])
        W("dv", "dv", tok_x32[tc])
        tok_rmred[tc] = DO("dv", lambda sl=sl, tc=tc: V.tensor_reduce(
            t_rm[:, tc:tc + 1], t_x32[sl][:, :], AX.X, ALU.max,
            apply_absolute_value=True), "dv")
        W("gp", "dv", tok_x32[tc])
        xssem = "sXS0" if sl == 0 else "sXS1"
        tok_xst[tc] = DO("gp", lambda sl=sl, tc=tc: GP.dma_start(
            out=d_xln[tc * 128:(tc + 1) * 128, :], in_=t_x32[sl][:, :]), xssem, 16)

    w2x_tr = {}
    w2x_ev = {}
    w2x_st = {}

    w2x_ld = {}

    def emit_w2x_load(rc):
        sl = rc % 2
        xtsem = "sXT0" if sl == 0 else "sXT1"
        if rc == 0:
            W("sp", "pe", tok_av_final_prev)
        if rc >= 2:
            W("sp", "pe", w2x_tr[(rc - 2, 7)])
        W("sp", "sWQ0", tok_wqst[34 + 4 * rc])
        W("sp", "sWQ1", tok_wqst[35 + 4 * rc])
        w2x_ld[rc] = DO("sp", lambda rc=rc, sl=sl: SP.dma_start(
            out=t_w2l[sl][:, :], in_=d_w2q[rc * 128:(rc + 1) * 128, :]), xtsem, 16)

    def emit_w2x_comp(rc):
        sl = rc % 2
        xtsem = "sXT0" if sl == 0 else "sXT1"
        wtsem = "sWT0" if sl == 0 else "sWT1"
        if rc == 0:
            W("ac", "sWT0", tok_wt[14])
            W("ac", "sWT1", tok_wt[15])
        for g in range(8):
            pbank = pD if g % 2 == 0 else pC
            W("pe", xtsem, w2x_ld[rc])
            W("pe", "ac", w2x_ev.get((rc, g - 2)))
            if rc > 0 and g < 2:
                W("pe", "ac", w2x_ev.get((rc - 1, 6 + g)))
            for hh in range(4):
                h = g * 4 + hh
                tk = DO("pe", lambda sl=sl, h=h, hh=hh, pbank=pbank: T.matmul(
                    pbank[:, hh * 128:(hh + 1) * 128],
                    t_w2l[sl][:, h * 128:(h + 1) * 128],
                    c_idb[:, :], start=True, stop=True), "pe" if hh == 3 else None)
            w2x_tr[(rc, g)] = tk
            W("ac", "pe", tk)
            W("ac", wtsem, w2x_st.get(rc - 2))
            w2x_ev[(rc, g)] = DO("ac", lambda sl=sl, g=g, pbank=pbank:
                                 A.activation(
                                     t_w2x[sl][:, g * 4:(g + 1) * 4, :].rearrange(
                                         "p a b -> p (a b)"),
                                     pbank[:, :], AF.Identity, bias=0.0,
                                     scale=1.0), "ac")
        W("sp", "ac", w2x_ev[(rc, 7)])
        w2x_st[rc] = DO("sp", lambda rc=rc, sl=sl: SP.dma_start(
            out=d_w2qT[:, :, rc * 128:(rc + 1) * 128].rearrange("a p b -> p a b"),
            in_=t_w2x[sl][:, :, :]), wtsem, 16)

    tok_x3 = {}
    for pf in range(2):
        tok_x3[pf] = DO("gp", lambda pf=pf: GP.dma_start(
            out=t_xr[pf][:, :], in_=d_x[pf * 128:(pf + 1) * 128, :]),
            "sX320" if pf == 0 else "sX321", 16)
    for tc in range(16):
        sl = tc % 2
        buf = pA if tc % 2 == 0 else pB
        x3sem = "sX320" if sl == 0 else "sX321"
        xssem = "sXS0" if sl == 0 else "sXS1"
        W("pe", "dv", tok_stt_prev)
        W("pe", "dv", tok_b2b)
        W("pe", "dv", tok_sumev.get(tc - 2))
        for dn in range(2):
            for kc in range(8):
                DO("pe", lambda buf=buf, dn=dn, kc=kc, tc=tc: T.matmul(
                    buf[:, dn * 512:(dn + 1) * 512],
                    t_o[:, kc, tc * 128:(tc + 1) * 128],
                    t_Wo[:, kc, dn * 512:(dn + 1) * 512],
                    start=(kc == 0), stop=False))
            tk = DO("pe", lambda buf=buf, dn=dn: T.matmul(
                buf[:, dn * 512:(dn + 1) * 512], c_1b[0:1, :],
                c_bob[0:1, dn * 512:(dn + 1) * 512], start=False, stop=True),
                "pe" if dn == 1 else None)
        tok_womm[tc] = tk
        W("dv", "pe", tok_womm[tc])
        W("dv", x3sem, tok_x3[tc])
        W("dv", "dv", tok_x32.get(tc - 2))
        tok_sumev[tc] = DO("dv", lambda buf=buf, sl=sl: V.tensor_tensor(
            t_sum[sl][:, :], buf[:, 0:1024], t_xr[sl][:, :], ALU.add), "dv")
        W("dv", "dv", tok_sumev[tc])
        if tc + 2 < 16:
            W("gp", "dv", tok_sumev[tc])
            tok_x3[tc + 2] = DO("gp", lambda tc=tc, sl=sl: GP.dma_start(
                out=t_xr[sl][:, :], in_=d_x[(tc + 2) * 128:(tc + 3) * 128, :]),
                x3sem, 16)
        if tc % 2 == 1:
            k = (tc - 1) // 2
            if k == 0:
                emit_w2x_load(0)
            if k + 1 < 8:
                emit_w2x_load(k + 1)
            emit_w2x_comp(k)
        tok_pre[tc] = ln_pre(t_sum[sl][:, :], t_bn[sl], t_bnx[sl])
        if tc >= 1:
            p3_post(tc - 1)
        if xbar_i[0] < len(xbar_jobs):
            emit_xbar_job(xbar_i[0])
            xbar_i[0] += 1
    p3_post(15)

    W("dv", "dv", tok_rmred[15])
    tok_rmc = DO("dv", lambda: V.tensor_reduce(c_wred[:, 0:1], t_rm[:, 0:16],
                                               AX.X, ALU.max), "dv")
    W("pe", "dv", tok_rmc)
    tok_rmT = DO("pe", lambda: T.transpose(pB[0:1, 64:192], c_wred[:, 0:1],
                                           c_idf[:, :]), "pe")
    W("dv", "pe", tok_rmT)
    tok_lmax = DO("dv", lambda: V.tensor_reduce(c_sc[0:1, 10:11], pB[0:1, 64:192],
                                                AX.X, ALU.max), "dv")
    W("dv", "dv", tok_lmax)
    tok_s8 = DO("dv", lambda: V.tensor_scalar(c_s8[0:1, :], c_1f[0:1, 0:8],
                                              c_sc[0:1, 10:11], None, ALU.mult),
                "dv")
    W("gp", "dv", tok_s8)
    tok_c1a = DO("gp", lambda: GP.dma_start(out=d_cc1i, in_=c_s8[0:1, :]), "sCC", 16)
    W("gp", "sCC", tok_c1a)
    tok_cc1 = DO("gp", lambda: GP.collective_compute(
        "AllReduce", ALU.max, replica_groups=[list(range(cc_ranks))],
        ins=[d_cc1i], outs=[d_cc1o]), "gc")

    p3_pe = tok_womm[15]
    W("dv", "sWQ0", tok_wqst[62])
    W("dv", "sWQ1", tok_wqst[63])
    W("pe", "sWQ0", tok_wqst[62])
    W("pe", "sWQ1", tok_wqst[63])
    tok_tr = {}
    tok_tev = {(-1, 0): w2x_ev[(7, 6)], (-1, 1): w2x_ev[(7, 7)]}
    tok_xl5t = {}
    for tc in range(16):
        sl = tc % 2
        xlsem = "sXL0" if sl == 0 else "sXL1"
        xssem = "sXS0" if tc % 2 == 0 else "sXS1"
        if tc >= 2:
            W("sp", "pe", tok_tr[(tc - 2, 1)])
        W("sp", xssem, tok_xst[tc])
        tok_xl5t[tc] = DO("sp", lambda tc=tc, sl=sl: SP.dma_start(
            out=t_xl5[sl][:, :], in_=d_xln[tc * 128:(tc + 1) * 128, :]), xlsem, 16)
        if tc == 14:
            W("ac", "sWT0", w2x_st[6])
            W("ac", "sWT1", w2x_st[7])
        for half, pbank in ((0, pD), (1, pC)):
            W("pe", "ac", tok_tev.get((tc - 1, half)))
            W("pe", xlsem, tok_xl5t[tc])
            for kc4 in range(4):
                kc = half * 4 + kc4
                tk = DO("pe", lambda sl=sl, kc=kc, kc4=kc4, pbank=pbank: T.transpose(
                    pbank[:, kc4 * 128:(kc4 + 1) * 128],
                    t_xl5[sl][:, kc * 128:(kc + 1) * 128],
                    c_idf[:, :]), "pe" if kc4 == 3 else None)
            tok_tr[(tc, half)] = tk
            W("ac", "pe", tok_tr[(tc, half)])
            tok_tev[(tc, half)] = DO("ac", lambda tc=tc, half=half, pbank=pbank:
                                     A.activation(
                                         t_xf5[tc][:, half * 4:half * 4 + 4, :]
                                         .rearrange("p a b -> p (a b)"),
                                         pbank[:, :], AF.Identity, bias=0.0,
                                         scale=1.0), "ac")

    W("gp", "gc", tok_cc1)
    tok_c1b = DO("gp", lambda: GP.dma_start(out=c_sc[0:1, 10:11],
                                            in_=d_cc1o[0:1, 0:1]), "sCC", 16)
    W("dv", "sCC", tok_c1b)
    t = DO("dv", lambda: V.tensor_single_scalar(c_sc[0:1, 11:12], c_sc[0:1, 10:11],
                                                1e-6, ALU.max), "dv")
    W("ac", "dv", t)
    t = DO("ac", lambda: A.activation(c_rc[0:1, 0:1], c_sc[0:1, 11:12], AF.Ln,
                                      bias=0.0, scale=1.0), "ac")
    W("ac", "ac", t)
    t = DO("ac", lambda: A.activation(c_sc[0:1, 12:13], c_rc[0:1, 0:1], AF.Exp,
                                      bias=0.0, scale=-1.0), "ac")
    W("dv", "ac", t)
    DO("dv", lambda: V.tensor_single_scalar(c_sc[0:1, 13:14], c_sc[0:1, 12:13],
                                            127.0, ALU.mult))
    t = DO("dv", lambda: V.tensor_single_scalar(c_sc[0:1, 14:15], c_sc[0:1, 11:12],
                                                1.0 / 127.0, ALU.mult), "dv")
    W("dv", "dv", t)
    tok_s1s = DO("dv", lambda: V.tensor_tensor(c_sc[0:1, 15:16], c_sc[0:1, 14:15],
                                               c_sc[0:1, 8:9], ALU.mult), "dv")
    W("pe", "dv", tok_s1s)
    W("pe", "ac", tok_tev[(15, 1)])
    DO("pe", lambda: T.matmul(pC[:, 322:323], c_1f[0:1, :], c_sc[0:1, 13:14],
                              start=True, stop=True))
    tok_sbc = DO("pe", lambda: T.matmul(pC[:, 323:324], c_1f[0:1, :],
                                        c_sc[0:1, 15:16], start=True, stop=True),
                 "pe")
    W("dv", "pe", tok_sbc)
    DO("dv", lambda: V.tensor_copy(c_sxr[:, :], pC[:, 322:323]))
    tok_scb = DO("dv", lambda: V.tensor_copy(c_s1[:, :], pC[:, 323:324]), "dv")
    W("dv", "dv", tok_scb)
    tok_b1s = DO("dv", lambda: V.tensor_scalar(c_b1s[:, :], c_b1r[:, :],
                                               c_s1[:, :], None, ALU.mult), "dv")

    tok_xqq = {}
    for tc in range(16):
        W("dv", "dv", tok_scb)
        W("dv", "dv", tok_xqq.get(tc - 1))
        W("dv", "ac", tok_tev[(tc, 1)])
        t1 = DO("dv", lambda tc=tc: V.tensor_scalar(
            t_qf16[:, :], t_xf5[tc][:, :, :].rearrange("p a b -> p (a b)"),
            c_sxr[:, :], MAGIC, ALU.mult, ALU.add), "dv")
        W("dv", "dv", t1)
        tok_xqq[tc] = DO("dv", lambda tc=tc: V.tensor_scalar(
            t_xq[:, :, tc * 128:(tc + 1) * 128],
            t_qf16[:, :].rearrange("p (kc n) -> p kc n", n=128),
            -MAGIC, None, ALU.add), "dv")
    tok_xq_all = tok_xqq[15]

    tok_gelu = {}
    tok_hst = {}
    tok_hred = {}
    tok_ffmm = {}
    fidx = 0
    for Hc in range(32):
        sl = Hc % 2
        w1sem = "sW10" if sl == 0 else "sW11"
        W("sp", "pe", tok_ffmm.get((Hc - 2, 3)))
        W("sp", "sWT0", tok_wt[14])
        W("sp", "sWT1", tok_wt[15])
        tok_w1 = DO("sp", lambda Hc=Hc, sl=sl: SP.dma_start(
            out=t_w1s[sl][:, :, :], in_=d_w1qT[Hc]), w1sem, 16)
        for tt in range(4):
            slot = fidx % 4
            ps = [pA[:, 0:512], pA[:, 512:1024], pA[:, 1024:1536], pB[:, 0:512]][slot]
            W("pe", w1sem, tok_w1)
            W("pe", "dv", tok_xqq[4 * tt + 3])
            W("pe", "ac", tok_gelu.get(fidx - 4))
            for kc in range(8):
                tk = DO("pe", lambda ps=ps, sl=sl, kc=kc, tt=tt: T.matmul(
                    ps, t_w1s[sl][:, kc, :], t_xq[:, kc, tt * 512:(tt + 1) * 512],
                    start=(kc == 0), stop=(kc == 7)), "pe" if kc == 7 else None)
            tok_ffmm[(Hc, tt)] = tk
            hsl = fidx % 3
            hsem = ("sH0", "sH1", "sH2")[hsl]
            W("ac", "pe", tok_ffmm[(Hc, tt)])
            W("ac", "dv", tok_b1s)
            W("ac", hsem, tok_hst.get(fidx - 3))
            W("ac", "dv", tok_hred.get(fidx - 3))
            tok_gelu[fidx] = DO("ac", lambda ps=ps, hsl=hsl, Hc=Hc: A.activation(
                t_ht[hsl][:, :], ps, AF.Gelu, bias=c_b1s[:, Hc:Hc + 1],
                scale=c_s1[:, :]), "ac")
            W("dv", "ac", tok_gelu[fidx])
            tok_hred[fidx] = DO("dv", lambda hsl=hsl, fidx=fidx: V.tensor_reduce(
                t_hm[:, fidx:fidx + 1], t_ht[hsl][:, :], AX.X, ALU.max,
                apply_absolute_value=True), "dv")
            W("gp", "ac", tok_gelu[fidx])
            tok_hst[fidx] = DO("gp", lambda Hc=Hc, tt=tt, hsl=hsl: GP.dma_start(
                out=d_hst[Hc, :, tt * 512:(tt + 1) * 512], in_=t_ht[hsl][:, :]),
                hsem, 16)
            fidx += 1

    W("dv", "dv", tok_hred[127])
    tok_hmc = DO("dv", lambda: V.tensor_reduce(c_wred[:, 1:2], t_hm[:, :],
                                               AX.X, ALU.max), "dv")
    W("pe", "dv", tok_hmc)
    W("pe", "dv", tok_scb)
    tok_hmT = DO("pe", lambda: T.transpose(pC[0:1, 64:192], c_wred[:, 1:2],
                                           c_idf[:, :]), "pe")
    W("dv", "pe", tok_hmT)
    tok_hmax = DO("dv", lambda: V.tensor_reduce(c_sc[0:1, 10:11], pC[0:1, 64:192],
                                                AX.X, ALU.max), "dv")
    W("dv", "dv", tok_hmax)
    tok_s8b = DO("dv", lambda: V.tensor_scalar(c_s8[0:1, :], c_1f[0:1, 0:8],
                                               c_sc[0:1, 10:11], None, ALU.mult),
                 "dv")
    W("gp", "dv", tok_s8b)
    tok_c2a = DO("gp", lambda: GP.dma_start(out=d_cc2i, in_=c_s8[0:1, :]), "sCC", 16)
    W("gp", "sCC", tok_c2a)
    tok_cc2 = DO("gp", lambda: GP.collective_compute(
        "AllReduce", ALU.max, replica_groups=[list(range(cc_ranks))],
        ins=[d_cc2i], outs=[d_cc2o]), "gc")

    HRSEM = ("sHR0", "sHR1", "sHR2", "sHR3")
    tok_hr = {}
    tok_hq1 = {}
    W("sp", "pe", tok_ffmm[(31, 3)])
    W("sp", "sWT0", w2x_st[6])
    W("sp", "sWT1", w2x_st[7])
    tok_w2sld = DO("sp", lambda: SP.dma_start(
        out=t_w2s[:, :, :], in_=d_w2qT.rearrange("hc p m -> p hc m")), "sW2", 16)
    W("sp", "sH0", tok_hst[126])
    W("sp", "sH1", tok_hst[127])
    W("sp", "sH2", tok_hst[125])
    W("gp", "pe", tok_ffmm[(31, 3)])
    for tc in range(4):
        q = "gp" if tc % 2 == 0 else "sp"
        ENQ = GP if tc % 2 == 0 else SP
        tok_hr[tc] = DO(q, lambda tc=tc, ENQ=ENQ: ENQ.dma_start(
            out=t_hf[tc][:, :, :],
            in_=d_hst[:, :, tc * 128:(tc + 1) * 128].rearrange("hc p d -> p hc d")),
            HRSEM[tc], 16)

    W("sp", "gc", tok_cc2)
    tok_c2b = DO("sp", lambda: SP.dma_start(out=c_sc[0:1, 10:11],
                                            in_=d_cc2o[0:1, 0:1]), "sC2", 16)
    W("dv", "sC2", tok_c2b)
    t = DO("dv", lambda: V.tensor_single_scalar(c_sc[0:1, 11:12], c_sc[0:1, 10:11],
                                                1e-6, ALU.max), "dv")
    W("ac", "dv", t)
    t = DO("ac", lambda: A.activation(c_rc[0:1, 1:2], c_sc[0:1, 11:12], AF.Ln,
                                      bias=0.0, scale=1.0), "ac")
    W("ac", "ac", t)
    t = DO("ac", lambda: A.activation(c_sc[0:1, 12:13], c_rc[0:1, 1:2], AF.Exp,
                                      bias=0.0, scale=-1.0), "ac")
    W("dv", "ac", t)
    DO("dv", lambda: V.tensor_single_scalar(c_sc[0:1, 13:14], c_sc[0:1, 12:13],
                                            127.0, ALU.mult))
    t = DO("dv", lambda: V.tensor_single_scalar(c_sc[0:1, 14:15], c_sc[0:1, 11:12],
                                                1.0 / 127.0, ALU.mult), "dv")
    W("dv", "dv", t)
    tok_s2s = DO("dv", lambda: V.tensor_tensor(c_sc[0:1, 15:16], c_sc[0:1, 14:15],
                                               c_sc[0:1, 9:10], ALU.mult), "dv")
    W("pe", "dv", tok_s2s)
    DO("pe", lambda: T.matmul(pC[:, 324:325], c_1f[0:1, :], c_sc[0:1, 13:14],
                              start=True, stop=True))
    tok_sbc2 = DO("pe", lambda: T.matmul(pC[:, 325:326], c_1f[0:1, :],
                                         c_sc[0:1, 15:16], start=True, stop=True),
                  "pe")
    W("dv", "pe", tok_sbc2)
    DO("dv", lambda: V.tensor_copy(c_shr[:, :], pC[:, 324:325]))
    tok_scb2 = DO("dv", lambda: V.tensor_copy(c_s2[:, :], pC[:, 325:326]), "dv")

    W("sp", "pe", tok_tr[(15, 1)])
    W("dv", "pe", tok_tr[(15, 1)])
    W("pe", "ac", tok_gelu[127])
    tok_hq = {}
    tok_ff2 = {}
    tok_sum6 = {}
    tok_yst = {}
    tok_pre6 = {}
    tok_post6 = {}

    def p6_post(tc):
        sl = tc % 2
        ysem = "sY0" if sl == 0 else "sY1"
        tok_post6[tc] = ln_post(t_out[sl][:, :], t_out[sl][:, :], t_bx6[sl],
                                tok_pre6[tc][0])
        W("sp", "dv", tok_post6[tc])
        tok_yst[tc] = DO("sp", lambda tc=tc, sl=sl: SP.dma_start(
            out=d_y[tc * 128:(tc + 1) * 128, :], in_=t_out[sl][:, :]), ysem, 16)

    def p6_quant(q1):
        sl4q = q1 % 4
        slq = q1 % 2
        W("ac", HRSEM[sl4q], tok_hr[q1])
        W("ac", "dv", tok_scb2)
        W("ac", "dv", tok_hq.get(q1 - 2))
        t1 = DO("ac", lambda sl4q=sl4q, slq=slq: A.activation(
            t_f16b[slq][:, 0:4096],
            t_hf[sl4q][:, :, :].rearrange("p a b -> p (a b)"),
            AF.Identity, bias=c_mag[:, :], scale=c_shr[:, :]), "ac")
        tok_hq1[q1] = t1
        W("dv", "ac", t1)
        W("dv", "pe", tok_ff2.get(q1 - 2))
        tok_hq[q1] = DO("dv", lambda slq=slq: V.tensor_scalar(
            t_hq[slq][:, :, :].rearrange("p a b -> p (a b)"),
            t_f16b[slq][:, 0:4096], -MAGIC, None, ALU.add), "dv")

    p6_quant(0)
    for tc in range(16):
        sl = tc % 2
        sl4 = tc % 4
        buf = pA if tc % 2 == 0 else pB
        xlsem = "sXL0" if sl == 0 else "sXL1"
        ysem = "sY0" if sl == 0 else "sY1"
        W("sp", "dv", tok_sum6.get(tc - 2))
        tok_xl6t = DO("sp", lambda tc=tc, sl=sl: SP.dma_start(
            out=t_xl6[sl][:, :], in_=d_xln[tc * 128:(tc + 1) * 128, :]), xlsem, 16)
        W("pe", "dv", tok_hq[tc])
        W("pe", "sW2", tok_w2sld)
        W("pe", "dv", tok_sum6.get(tc - 2))
        for dn in range(2):
            for hc in range(32):
                DO("pe", lambda buf=buf, dn=dn, hc=hc, sl=sl: T.matmul(
                    buf[:, dn * 512:(dn + 1) * 512], t_hq[sl][:, hc, :],
                    t_w2s[:, hc, dn * 512:(dn + 1) * 512],
                    start=(hc == 0), stop=False))
            tk = DO("pe", lambda buf=buf, dn=dn: T.matmul(
                buf[:, dn * 512:(dn + 1) * 512], c_1b[0:1, :],
                c_b2b[0:1, dn * 512:(dn + 1) * 512], start=False, stop=True),
                "pe" if dn == 1 else None)
        tok_ff2[tc] = tk
        if tc + 1 < 16:
            p6_quant(tc + 1)
        if tc + 4 < 16:
            q = "gp" if (tc + 4) % 2 == 0 else "sp"
            ENQ = GP if (tc + 4) % 2 == 0 else SP
            W(q, "ac", tok_hq1[tc])
            tok_hr[tc + 4] = DO(q, lambda tc=tc, sl4=sl4, ENQ=ENQ: ENQ.dma_start(
                out=t_hf[sl4][:, :, :],
                in_=d_hst[:, :, (tc + 4) * 128:(tc + 5) * 128].rearrange(
                    "hc p d -> p hc d")), HRSEM[sl4], 16)
        W("dv", "pe", tok_ff2[tc])
        W("dv", xlsem, tok_xl6t)
        W("dv", ysem, tok_yst.get(tc - 2))
        W("dv", "dv", tok_post6.get(tc - 2))
        tok_sum6[tc] = DO("dv", lambda buf=buf, sl=sl: V.scalar_tensor_tensor(
            t_out[sl][:, :], buf[:, 0:1024], c_s2[:, :], t_xl6[sl][:, :],
            ALU.mult, ALU.add), "dv")
        W("dv", "dv", tok_sum6[tc])
        tok_pre6[tc] = ln_pre(t_out[sl][:, :], t_bn6[sl], t_bx6[sl])
        if tc >= 1:
            p6_post(tc - 1)
    p6_post(15)

    W("sp", "sY0", tok_yst[14])
    W("sp", "sY1", tok_yst[15])

    sch.emit(block)
    es.close()
    return nc


_NC_CACHE = {}


def _get_nc(r0, cc_ranks=8):
    key = (r0, cc_ranks)
    if key not in _NC_CACHE:
        _NC_CACHE[key] = build_nc(r0, cc_ranks)
    return _NC_CACHE[key]


def kernel(L0_attn_x, L0_attn_Wq, L0_attn_bq, L0_attn_Wl, L0_attn_bl,
           L0_attn_Wk, L0_attn_bk, L0_attn_Wv, L0_attn_bv, L0_attn_Wo,
           L0_attn_bo, L0_ffn_w1, L0_ffn_b1, L0_ffn_w2, L0_ffn_b2):
    x = np.ascontiguousarray(np.asarray(L0_attn_x, dtype=np.float32))
    bcat = np.zeros((32, 128), np.float32)
    bcat[0:8] = np.asarray(L0_attn_bq, np.float32).reshape(8, 128)
    bcat[8:10] = np.asarray(L0_attn_bl, np.float32).reshape(2, 128)
    bcat[10:18] = np.asarray(L0_attn_bk, np.float32).reshape(8, 128)
    bcat[18:26] = np.asarray(L0_attn_bv, np.float32).reshape(8, 128)
    common = {
        "Wq": np.ascontiguousarray(np.asarray(L0_attn_Wq, np.float32)),
        "Wl": np.ascontiguousarray(np.asarray(L0_attn_Wl, np.float32)),
        "Wk": np.ascontiguousarray(np.asarray(L0_attn_Wk, np.float32)),
        "Wv": np.ascontiguousarray(np.asarray(L0_attn_Wv, np.float32)),
        "Wo": np.ascontiguousarray(np.asarray(L0_attn_Wo, np.float32)),
        "bcat": bcat,
        "bo": np.asarray(L0_attn_bo, np.float32).reshape(1, D),
        "w1": np.ascontiguousarray(np.asarray(L0_ffn_w1, np.float32)),
        "b1": np.asarray(L0_ffn_b1, np.float32).reshape(32, 128),
        "w2": np.ascontiguousarray(np.asarray(L0_ffn_w2, np.float32)),
        "b2": np.asarray(L0_ffn_b2, np.float32).reshape(1, D),
        "identb": np.eye(128, dtype=ml_dtypes.bfloat16),
        "identf": np.eye(128, dtype=np.float32),
    }
    in_maps = []
    for c in range(8):
        b, half = c // 2, c % 2
        m = dict(common)
        xb = x[b] if half == 0 else np.roll(x[b], -TO, axis=0)
        m["xfull"] = np.ascontiguousarray(xb)
        in_maps.append(m)
    nc = _get_nc(0, 8)
    res = run_bass_kernel_spmd(nc, in_maps, core_ids=list(range(8)))
    out = np.empty((4, S, D), np.float32)
    for c in range(8):
        b, half = c // 2, c % 2
        out[b, half * TO:(half + 1) * TO] = res.results[c]["y"]
    return out


# version 16
# speedup vs baseline: 1.0378x; 1.0070x over previous
import numpy as np
import ml_dtypes
import concourse.bass as bass
import concourse.mybir as mybir
from concourse.bass_utils import run_bass_kernel_spmd

F32 = mybir.dt.float32
F16 = mybir.dt.float16
BF16 = mybir.dt.bfloat16
AF = mybir.ActivationFunctionType
ALU = mybir.AluOpType
AX = mybir.AxisListType

D = 1024
S = 4096
TO = 2048
LATD = 256
HID = 4096
NI = 512
MAGIC = 1152.0
EPS = 1e-5
KB = 1024
ENGS = ("pe", "dv", "ac", "sp", "gp")

SEMS = ("gWA", "gWL", "gWB", "gX0", "gX1", "sB", "sWS0", "sWS1", "sWS2", "sWS3", "sWL0", "sWL1",
        "sWQ0", "sWQ1", "sXT0", "sXT1", "sWT0", "sWT1", "sX320", "sX321", "sXS0", "sXS1",
        "sH0", "sH1", "sH2", "sW10", "sW11", "sW2", "sXL0", "sXL1",
        "sHR0", "sHR1", "sHR2", "sHR3", "sY0", "sY1", "sCC", "sC2", "pe", "dv",
        "ac", "gc")


class Sch:
    def __init__(self, nc):
        self.nc = nc
        self.q = {e: [] for e in ENGS}
        self.sems = {}
        self.wm = {}

    def eng(self, e):
        nc = self.nc
        return {"pe": nc.tensor, "dv": nc.vector, "ac": nc.scalar,
                "sp": nc.sync, "gp": nc.gpsimd}[e]

    def mksem(self, name, es):
        self.sems[name] = [es.enter_context(self.nc.semaphore(name)), 0]

    def n(self, name):
        return self.sems[name][1]

    def wait(self, e, name, val):
        if val is None or val <= 0:
            return
        key = (e, name)
        if self.wm.get(key, 0) >= val:
            return
        self.wm[key] = val
        h = self.sems[name][0]
        eng = self.eng(e)
        self.q[e].append(lambda eng=eng, h=h, val=val: eng.wait_ge(h, val))

    def do(self, e, fn, inc=None, k=1):
        if inc is None:
            self.q[e].append(fn)
            return None
        rec = self.sems[inc]
        rec[1] += k
        h = rec[0]
        self.q[e].append(lambda fn=fn, h=h, k=k: fn().then_inc(h, k))
        return rec[1]

    def emit(self, block):
        for e in ENGS:
            dec = {"pe": block.tensor, "dv": block.vector, "ac": block.scalar,
                   "sp": block.sync, "gp": block.gpsimd}[e]
            ops = self.q[e]

            def body(_eng, ops=ops):
                for f in ops:
                    f()
            dec(body)


def build_nc(r0, cc_ranks=8):
    nc = bass.Bass("TRN2", target_bir_lowering=False)

    d_x = nc.dram_tensor("xfull", [S, D], F32, kind="ExternalInput").ap()
    d_Wq = nc.dram_tensor("Wq", [D, D], F32, kind="ExternalInput").ap()
    d_Wl = nc.dram_tensor("Wl", [D, LATD], F32, kind="ExternalInput").ap()
    d_Wk = nc.dram_tensor("Wk", [LATD, D], F32, kind="ExternalInput").ap()
    d_Wv = nc.dram_tensor("Wv", [LATD, D], F32, kind="ExternalInput").ap()
    d_Wo = nc.dram_tensor("Wo", [D, D], F32, kind="ExternalInput").ap()
    d_bcat = nc.dram_tensor("bcat", [32, 128], F32, kind="ExternalInput").ap()
    d_bo = nc.dram_tensor("bo", [1, D], F32, kind="ExternalInput").ap()
    d_w1 = nc.dram_tensor("w1", [HID, D], F32, kind="ExternalInput").ap()
    d_b1 = nc.dram_tensor("b1", [32, 128], F32, kind="ExternalInput").ap()
    d_w2 = nc.dram_tensor("w2", [D, HID], F32, kind="ExternalInput").ap()
    d_b2 = nc.dram_tensor("b2", [1, D], F32, kind="ExternalInput").ap()
    d_idb = nc.dram_tensor("identb", [128, 128], BF16, kind="ExternalInput").ap()
    d_idf = nc.dram_tensor("identf", [128, 128], F32, kind="ExternalInput").ap()
    d_y = nc.dram_tensor("y", [TO, D], F32, kind="ExternalOutput").ap()

    d_hst = nc.dram_tensor("h_stash", [32, 128, TO], F32).ap()
    d_xln = nc.dram_tensor("xln_stash", [TO, D], F32).ap()
    d_w1q = nc.dram_tensor("w1q_dram", [HID, D], BF16).ap()
    d_w2q = nc.dram_tensor("w2q_dram", [D, HID], BF16).ap()
    d_w1qT = nc.dram_tensor("w1qT_dram", [32, 128, 8, 128], BF16).ap()
    d_w2qT = nc.dram_tensor("w2qT_dram", [32, 128, D], BF16).ap()
    d_cc1i = nc.dram_tensor("cc1i", [1, 8], F32).ap()
    d_cc1o = nc.dram_tensor("cc1o", [1, 8], F32).ap()
    d_cc2i = nc.dram_tensor("cc2i", [1, 8], F32).ap()
    d_cc2o = nc.dram_tensor("cc2o", [1, 8], F32).ap()

    AT = nc.alloc_sbuf_tensor_at
    po = [(int(nc.sbuf_base) + 63) & ~63]

    def perm(name, shape, dtype):
        nbytes = int(np.prod(shape[1:])) * mybir.dt.size(dtype)
        t = AT(name, shape, dtype, offset=po[0])
        po[0] = (po[0] + nbytes + 31) & ~31
        return t

    c_idb = perm("c_idb", [128, 128], BF16)
    c_idf = perm("c_idf", [128, 128], F32)
    c_ball = perm("c_ball", [128, 32], F32)
    c_bvb = perm("c_bvb", [128, 8], BF16)
    c_b1r = perm("c_b1r", [128, 32], F32)
    c_b1s = perm("c_b1s", [128, 32], F32)
    c_bob = perm("c_bob", [1, D], BF16)
    c_b2b = perm("c_b2b", [1, D], BF16)
    c_1f = perm("c_1f", [1, 128], F32)
    c_sel = perm("c_sel", [128, 128], F32)
    c_1b = perm("c_1b", [1, 128], BF16)
    c_sc = perm("c_sc", [1, 16], F32)
    c_s8 = perm("c_s8", [1, 8], F32)
    c_wred = perm("c_wred", [128, 2], F32)
    c_wr1 = perm("c_wr1", [128, 1], F32)
    c_wr2 = perm("c_wr2", [128, 1], F32)
    c_sxr = perm("c_sxr", [128, 1], F32)
    c_s1 = perm("c_s1", [128, 1], F32)
    c_shr = perm("c_shr", [128, 1], F32)
    c_s2 = perm("c_s2", [128, 1], F32)
    c_eps = perm("c_eps", [128, 1], F32)
    c_mag = perm("c_mag", [128, 1], F32)
    c_rc = perm("c_rc", [1, 4], F32)
    c_stage = perm("c_stage", [32, 128], F32)

    PB = (po[0] + 63) & ~63

    def at(name, shape, dtype, off_kb):
        return AT(name, shape, dtype, offset=PB + (int(off_kb * KB) & ~31))

    t_lat = at("t_lat", [128, 2, S], BF16, 0)
    t_q = at("t_q", [128, 8, TO], BF16, 16)
    t_o = at("t_o", [128, 8, TO], BF16, 48)
    t_P = at("t_P", [128, 8, 2, NI], BF16, 80)
    t_kg = at("t_kg", [128, 2, S], BF16, 112)
    t_vg = at("t_vg", [128, 32, 4, 65], BF16, 128)
    t_Wk = at("t_Wk", [128, 2, D], BF16, 145)
    t_Wv = at("t_Wv", [128, 2, D], BF16, 149)
    t_Wo = at("t_Wo", [128, 8, D], BF16, 153)
    t_wld = [at(f"t_wld{i}", [128, D], F32, [169, 173, 179, 183][i])
             for i in range(4)]
    t_wf16 = at("t_wf16", [128, D], F16, 177)
    t_wqb = [at(f"t_wqb{i}", [128, D], BF16, 179 + 2 * i) for i in range(2)]
    t_wtr = [at(f"t_wtr{i}", [128, 2048], BF16, 183 + 4 * i) for i in range(2)]
    t_slr = at("t_slr", [128, NI], F32, 191)
    t_slb = at("t_slb", [128, NI], F32, 195)
    t_sll = at("t_sll", [128, NI], F32, 96)
    t_slx = at("t_slx", [128, NI], F32, 98)
    t_bo32 = at("t_bo32", [1, D], F32, 112)
    t_b2r = at("t_b2r", [1, D], F32, 116)

    t_Wq = at("t_Wq", [128, 8, D], BF16, 48)
    t_Wl = at("t_Wl", [128, 8, LATD], BF16, 64)
    t_x32b = [at(f"t_x32b{i}", [128, D], BF16, 68 + 4 * i) for i in range(2)]
    t_xb = [at(f"t_xb{i}", [128, D], BF16, 76 + 2 * i) for i in range(2)]
    t_xf = [at(f"t_xf{i}", [128, 8, 128], BF16, 80 + 2 * i) for i in range(2)]
    t_wm = at("t_wm", [128, 64], F32, 84)

    t_x32 = [at(f"t_x32{i}", [128, D], F32, 112 + 4 * i) for i in range(2)]
    t_sum = [at(f"t_sum{i}", [128, D], F32, 120 + 4 * i) for i in range(2)]
    t_bn = [at(f"t_bn{i}", [128, 12], F32, 128 + 0.1 * i) for i in range(2)]
    t_bnx = [at(f"t_bnx{i}", [128, 16], F32, 128.2 + 0.1 * i) for i in range(2)]
    t_rm = at("t_rm", [128, 16], F32, 128.75)

    t_xq = at("t_xq", [128, 8, TO], BF16, 0)
    t_w2s = at("t_w2s", [128, 32, D], BF16, 32)
    t_w1s = [at(f"t_w1s{i}", [128, 8, 128], BF16, 96 + 2 * i) for i in range(2)]
    t_ht = [at(f"t_ht{i}", [128, NI], F32, 100 + 2 * i) for i in range(3)]
    t_hm = at("t_hm", [128, 128], F32, 106)
    t_xl5 = [at(f"t_xl5{i}", [128, D], F32, 84 + 4 * i) for i in range(2)]
    t_qf16 = at("t_qf16", [128, D], F16, 115)
    t_f16b = [at(f"t_f16b{i}", [128, HID], F16, [117, 107][i]) for i in range(2)]

    t_xf5 = [at(f"t_xf5{i}", [128, 8, 128], F32, 125 + 4 * i) for i in range(16)]
    t_w2l = [at(f"t_w2l{i}", [128, HID], BF16, 32 + 8 * i) for i in range(2)]
    t_w2x = [at(f"t_w2x{i}", [128, 32, 128], BF16, [183, 191][i]) for i in range(2)]
    t_xr = [at(f"t_xr{i}", [128, D], F32, 129 + 4 * i) for i in range(2)]
    t_hq = [at(f"t_hq{i}", [128, 32, 128], BF16, 125 + 8 * i) for i in range(2)]
    t_hf = ([at(f"t_hf{i}", [128, 32, 128], F32, 141 + 16 * i) for i in range(2)]
            + [at(f"t_hfb{i}", [128, 32, 128], F32, 0 + 16 * i) for i in range(2)])
    t_xl6 = [at(f"t_xl6{i}", [128, D], F32, 173 + 4 * i) for i in range(2)]
    t_out = [at(f"t_out{i}", [128, D], F32, 181 + 4 * i) for i in range(2)]
    t_bn6 = [at(f"t_bn6{i}", [128, 12], F32, 189 + 0.1 * i) for i in range(2)]
    t_bx6 = [at(f"t_bx6{i}", [128, 16], F32, 189.2 + 0.1 * i) for i in range(2)]

    pA = nc.alloc_psum_tensor("pA", [128, 1536], F32)
    pB = nc.alloc_psum_tensor("pB", [128, 1536], F32)
    pC = nc.alloc_psum_tensor("pC", [128, 512], F32)
    pD = nc.alloc_psum_tensor("pD", [128, 512], F32)

    sch = Sch(nc)
    from contextlib import ExitStack
    es = ExitStack()
    for s in SEMS:
        sch.mksem(s, es)
    block = es.enter_context(nc.Block())

    W = sch.wait
    DO = sch.do
    T = nc.tensor
    V = nc.vector
    A = nc.scalar
    SP = nc.sync
    GP = nc.gpsimd

    DO("sp", lambda: SP.dma_start(out=c_idb[:, :], in_=d_idb), "sB", 16)
    DO("sp", lambda: SP.dma_start(out=c_idf[:, :], in_=d_idf), "sB", 16)
    DO("sp", lambda: SP.dma_start(out=c_stage[:, :], in_=d_bcat), "sB", 16)
    DO("sp", lambda: SP.dma_start(out=t_bo32[0:1, :], in_=d_bo), "sB", 16)
    tok_b0 = DO("sp", lambda: SP.dma_start(out=t_b2r[0:1, :], in_=d_b2), "sB", 16)

    DO("dv", lambda: V.memset(c_eps[:, :], EPS))
    DO("dv", lambda: V.memset(c_mag[:, :], MAGIC))
    DO("dv", lambda: V.memset(c_1f[0:1, :], 1.0))
    t_sel0 = DO("dv", lambda: V.memset(c_sel[:, :], 0.0), "dv")
    W("dv", "dv", t_sel0)
    DO("dv", lambda: V.memset(c_sel[0:1, 0:64], 1.0))
    DO("dv", lambda: V.memset(c_sel[64:65, 64:128], 1.0))
    DO("dv", lambda: V.memset(t_sll[:, :], 1.0))
    DO("dv", lambda: V.memset(c_1b[0:1, :], 1.0))
    tok_ones = DO("dv", lambda: V.memset(t_vg[:, :, :, 64], 1.0), "dv")

    tok_xld = {}
    for pf in range(2):
        tok_xld[pf] = DO("gp", lambda pf=pf: GP.dma_start(
            out=t_x32b[pf][:, :], in_=d_x[pf * 128:(pf + 1) * 128, :]),
            "gX0" if pf == 0 else "gX1", 16)
    tok_wL = DO("gp", lambda: GP.dma_start(
        out=t_Wl[:, :, :], in_=d_Wl.rearrange("(kc p) m -> p kc m", p=128)), "gWL", 16)
    tok_wA = DO("gp", lambda: GP.dma_start(
        out=t_Wq[:, :, :], in_=d_Wq.rearrange("(kc p) m -> p kc m", p=128)), "gWA", 16)

    W("pe", "sB", tok_b0)
    tok_bT = DO("pe", lambda: T.transpose(pC[:, 0:32], c_stage[0:32, :],
                                          c_idf[0:32, 0:32]), "pe")
    W("dv", "pe", tok_bT)
    tok_ballc = DO("dv", lambda: V.tensor_copy(c_ball[:, :], pC[:, 0:32]), "dv")
    W("dv", "dv", tok_ballc)
    tok_ball = DO("dv", lambda: V.tensor_copy(c_bvb[:, :], c_ball[:, 18:26]), "dv")

    W("sp", "pe", tok_bT)
    tok_b1ld = DO("sp", lambda: SP.dma_start(out=c_stage[:, :], in_=d_b1), "sB", 16)
    W("pe", "sB", tok_b1ld)
    W("pe", "dv", tok_ball)
    tok_b1T = DO("pe", lambda: T.transpose(pC[:, 32:64], c_stage[0:32, :],
                                           c_idf[0:32, 0:32]), "pe")
    W("ac", "pe", tok_b1T)
    tok_b1r = DO("ac", lambda: A.activation(c_b1r[:, :], pC[:, 32:64],
                                            AF.Identity, bias=0.0, scale=1.0),
                 "ac")

    W("ac", "dv", tok_ball)
    tok_wall = [None]
    tok_tr = {}
    tok_xf = {(-1, 1): tok_b1r}
    tok_latmm = {}
    tok_latev = {}
    tok_qmm = {}
    tok_qev = {}
    qcnt = 0
    tok_wsred = {}

    def emit_scan_job(i):
        sl = i % 4
        ssem = ("sWS0", "sWS1", "sWS2", "sWS3")[sl]
        q = "gp" if sl == 0 else "sp"
        ENQ = GP if sl == 0 else SP
        if i < 32:
            src = d_w1[i * 128:(i + 1) * 128, :]
        else:
            j = i - 32
            rc, cs = j // 4, j % 4
            src = d_w2[rc * 128:(rc + 1) * 128, cs * 1024:(cs + 1) * 1024]
        W(q, "dv", tok_wsred.get(i - 4))
        tok_ld = DO(q, lambda src=src, sl=sl, ENQ=ENQ: ENQ.dma_start(
            out=t_wld[sl][:, :], in_=src), ssem, 16)
        W("dv", ssem, tok_ld)
        tok_wsred[i] = DO("dv", lambda sl=sl, i=i: V.tensor_reduce(
            t_wm[:, i:i + 1], t_wld[sl][:, :], AX.X, ALU.max,
            apply_absolute_value=True), "dv")

    for tit in range(33):
        if tit < 32:
            ti = tit
            sl = ti % 2
            xsem = "gX0" if sl == 0 else "gX1"
            W("pe", xsem, tok_xld[ti])
            for half, pbank in ((0, pD), (1, pC)):
                W("pe", "ac", tok_xf.get((ti - 1, half)))
                for kc4 in range(4):
                    kc = half * 4 + kc4
                    last = DO("pe", lambda sl=sl, kc=kc, kc4=kc4, pbank=pbank:
                              T.matmul(pbank[:, kc4 * 128:(kc4 + 1) * 128],
                                       t_x32b[sl][:, kc * 128:(kc + 1) * 128],
                                       c_idb[:, :], start=True, stop=True),
                              "pe" if kc4 == 3 else None)
                tok_tr[(ti, half)] = last
                W("ac", "pe", tok_tr[(ti, half)])
                tok_xf[(ti, half)] = DO("ac", lambda sl=sl, half=half, pbank=pbank:
                                        A.activation(
                                            t_xf[sl][:, half * 4:half * 4 + 4, :]
                                            .rearrange("p a b -> p (a b)"),
                                            pbank[:, :], AF.Identity, bias=0.0,
                                            scale=1.0), "ac")
            tok_tr[ti] = tok_tr[(ti, 1)]
            tok_xf[ti] = tok_xf[(ti, 1)]
            if ti + 2 < 32:
                W("gp", "pe", tok_tr[ti])
                tok_xld[ti + 2] = DO("gp", lambda ti=ti, sl=sl: GP.dma_start(
                    out=t_x32b[sl][:, :],
                    in_=d_x[(ti + 2) * 128:(ti + 3) * 128, :]), xsem, 16)
            if ti == 8:
                DO("gp", lambda: GP.dma_start(
                    out=t_Wk[:, :, :],
                    in_=d_Wk.rearrange("(kc p) m -> p kc m", p=128)), "gWB", 16)
                DO("gp", lambda: GP.dma_start(
                    out=t_Wv[:, :, :],
                    in_=d_Wv.rearrange("(kc p) m -> p kc m", p=128)), "gWB", 16)
                tok_wall[0] = DO("gp", lambda: GP.dma_start(
                    out=t_Wo[:, :, :],
                    in_=d_Wo.rearrange("(kc p) m -> p kc m", p=128)), "gWB", 16)
        if tit < 1:
            continue
        ti = tit - 1
        sl = ti % 2
        W("pe", "ac", tok_xf[(ti, 0)])
        W("pe", "ac", tok_xf[(ti, 1)])
        W("pe", "gWL", tok_wL)
        W("pe", "ac", tok_latev.get(ti - 2))
        lo = 0 if ti % 2 == 0 else 512
        for mc in range(2):
            for kc in range(8):
                tk = DO("pe", lambda sl=sl, mc=mc, kc=kc, lo=lo: T.matmul(
                    pA[:, lo + mc * 128: lo + (mc + 1) * 128],
                    t_Wl[:, kc, mc * 128:(mc + 1) * 128],
                    t_xf[sl][:, kc, :], start=(kc == 0), stop=(kc == 7)),
                    "pe" if (mc == 1 and kc == 7) else None)
        tok_latmm[ti] = tk
        W("ac", "pe", tok_latmm[ti])
        for mc in range(2):
            tk = DO("ac", lambda ti=ti, mc=mc, lo=lo: A.activation(
                t_lat[:, mc, ti * 128:(ti + 1) * 128],
                pA[:, lo + mc * 128: lo + (mc + 1) * 128],
                AF.Identity, bias=c_ball[:, 8 + mc:9 + mc], scale=1.0), "ac")
        tok_latev[ti] = tk
        if ti < 16:
            tloc = ti * 128
            for half in range(2):
                qs = (qcnt % 2) * 512
                W("pe", "gWA", tok_wA)
                W("pe", "ac", tok_qev.get(qcnt - 2))
                for mc in range(4):
                    for kc in range(8):
                        tk = DO("pe", lambda sl=sl, half=half, mc=mc, kc=kc, qs=qs:
                                T.matmul(pB[:, qs + mc * 128: qs + (mc + 1) * 128],
                                         t_Wq[:, kc, (half * 4 + mc) * 128:
                                              (half * 4 + mc + 1) * 128],
                                         t_xf[sl][:, kc, :],
                                         start=(kc == 0), stop=(kc == 7)),
                                "pe" if (mc == 3 and kc == 7) else None)
                tok_qmm[qcnt] = tk
                W("ac", "pe", tok_qmm[qcnt])
                for mc in range(4):
                    tk = DO("ac", lambda half=half, mc=mc, qs=qs, tloc=tloc:
                            A.activation(
                                t_q[:, half * 4 + mc, tloc:tloc + 128],
                                pB[:, qs + mc * 128: qs + (mc + 1) * 128],
                                AF.Identity,
                                bias=c_ball[:, half * 4 + mc: half * 4 + mc + 1],
                                scale=1.0), "ac")
                tok_qev[qcnt] = tk
                qcnt += 1
        emit_scan_job(2 * ti)
        emit_scan_job(2 * ti + 1)

    tok_q_all = tok_qev[qcnt - 1]
    tok_lat_all = tok_latev[31]

    W("dv", "dv", tok_wsred[63])
    DO("dv", lambda: V.tensor_reduce(c_wred[:, 0:1], t_wm[:, 0:32], AX.X, ALU.max))
    tok_wred = DO("dv", lambda: V.tensor_reduce(
        c_wred[:, 1:2], t_wm[:, 32:64], AX.X, ALU.max), "dv")
    W("pe", "dv", tok_wred)
    W("pe", "ac", tok_b1r)
    DO("pe", lambda: T.transpose(pC[0:1, 64:192], c_wred[:, 0:1], c_idf[:, :]))
    tok_wT = DO("pe", lambda: T.transpose(pC[0:1, 192:320], c_wred[:, 1:2],
                                          c_idf[:, :]), "pe")
    W("dv", "pe", tok_wT)
    DO("dv", lambda: V.tensor_reduce(c_sc[0:1, 0:1], pC[0:1, 64:192], AX.X, ALU.max))
    tok_w12 = DO("dv", lambda: V.tensor_reduce(
        c_sc[0:1, 1:2], pC[0:1, 192:320], AX.X, ALU.max), "dv")
    W("dv", "dv", tok_w12)
    DO("dv", lambda: V.tensor_single_scalar(c_sc[0:1, 2:3], c_sc[0:1, 0:1], 1e-6, ALU.max))
    tok_m = DO("dv", lambda: V.tensor_single_scalar(
        c_sc[0:1, 3:4], c_sc[0:1, 1:2], 1e-6, ALU.max), "dv")
    W("dv", "dv", tok_m)
    DO("dv", lambda: V.reciprocal(c_sc[0:1, 4:5], c_sc[0:1, 2:3]))
    tok_r = DO("dv", lambda: V.reciprocal(c_sc[0:1, 5:6], c_sc[0:1, 3:4]), "dv")
    W("dv", "dv", tok_r)
    DO("dv", lambda: V.tensor_single_scalar(c_sc[0:1, 6:7], c_sc[0:1, 4:5], 127.0, ALU.mult))
    DO("dv", lambda: V.tensor_single_scalar(c_sc[0:1, 7:8], c_sc[0:1, 5:6], 127.0, ALU.mult))
    DO("dv", lambda: V.tensor_single_scalar(c_sc[0:1, 8:9], c_sc[0:1, 2:3], 1.0 / 127.0, ALU.mult))
    tok_sw = DO("dv", lambda: V.tensor_single_scalar(
        c_sc[0:1, 9:10], c_sc[0:1, 3:4], 1.0 / 127.0, ALU.mult), "dv")
    W("pe", "dv", tok_sw)
    DO("pe", lambda: T.matmul(pC[:, 320:321], c_1f[0:1, :], c_sc[0:1, 6:7],
                              start=True, stop=True))
    tok_wbc = DO("pe", lambda: T.matmul(pC[:, 321:322], c_1f[0:1, :], c_sc[0:1, 7:8],
                                        start=True, stop=True), "pe")
    W("dv", "pe", tok_wbc)
    DO("dv", lambda: V.tensor_copy(c_wr1[:, :], pC[:, 320:321]))
    tok_wrb = DO("dv", lambda: V.tensor_copy(c_wr2[:, :], pC[:, 321:322]), "dv")

    W("pe", "gWB", tok_wall[0])
    W("pe", "dv", tok_wrb)
    tok_bvE = None
    for part in range(2):
        W("pe", "dv", tok_bvE)
        for kc in range(8):
            tk = DO("pe", lambda part=part, kc=kc: T.matmul(
                pC[0:1, 0:512],
                c_bvb[:, kc:kc + 1], t_Wo[:, kc, part * 512:(part + 1) * 512],
                start=(kc == 0), stop=(kc == 7)), "pe" if kc == 7 else None)
        W("dv", "pe", tk)
        tok_bvE = DO("dv", lambda part=part: V.tensor_tensor(
            c_bob[0:1, part * 512:(part + 1) * 512], pC[0:1, 0:512],
            t_bo32[0:1, part * 512:(part + 1) * 512], ALU.add), "dv")
    tok_b2b = DO("dv", lambda: V.tensor_copy(c_b2b[0:1, :], t_b2r[0:1, :]), "dv")

    wprep_quant = [("w1", i) for i in range(32)] + [("w2", j) for j in range(32)]
    tok_wq2 = {}
    tok_wqst = {}
    tok_wld_d = {}

    def emit_wquant_load(idx):
        kind, i = wprep_quant[idx]
        sl = idx % 2
        lsem = "sWL0" if sl == 0 else "sWL1"
        if kind == "w1":
            src_ = d_w1[i * 128:(i + 1) * 128, :]
        else:
            rc, cs = i // 4, i % 4
            src_ = d_w2[rc * 128:(rc + 1) * 128, cs * 1024:(cs + 1) * 1024]
        if idx < 2:
            W("sp", "dv", tok_wsred[60 + idx])
        W("sp", "dv", tok_wq2.get(idx - 2))
        tok_wld_d[idx] = DO("sp", lambda src_=src_, sl=sl: SP.dma_start(
            out=t_wld[sl][:, :], in_=src_), lsem, 16)

    def emit_wquant_comp(idx):
        kind, i = wprep_quant[idx]
        sl = idx % 2
        lsem = "sWL0" if sl == 0 else "sWL1"
        qsem = "sWQ0" if sl == 0 else "sWQ1"
        if kind == "w1":
            dst = d_w1q[i * 128:(i + 1) * 128, :]
            wr = c_wr1
        else:
            rc, cs = i // 4, i % 4
            dst = d_w2q[rc * 128:(rc + 1) * 128, cs * 1024:(cs + 1) * 1024]
            wr = c_wr2
        W("dv", lsem, tok_wld_d[idx])
        W("dv", "dv", tok_wrb)
        W("dv", "dv", tok_wq2.get(idx - 1))
        W("dv", qsem, tok_wqst.get(idx - 2))
        st1 = DO("dv", lambda sl=sl, wr=wr: V.tensor_scalar(
            t_wf16[:, :], t_wld[sl][:, :], wr[:, :], MAGIC, ALU.mult, ALU.add), "dv")
        W("dv", "dv", st1)
        tok_wq2[idx] = DO("dv", lambda sl=sl: V.tensor_scalar(
            t_wqb[sl][:, :], t_wf16[:, :], -MAGIC, None, ALU.add), "dv")
        W("sp", "dv", tok_wq2[idx])
        tok_wqst[idx] = DO("sp", lambda dst=dst, sl=sl: SP.dma_start(
            out=dst, in_=t_wqb[sl][:, :]), qsem, 16)

    xbar_jobs = [("w1", c, half) for c in range(8) for half in range(2)]
    tok_wt = {}
    xbar_i = [0]

    def emit_xbar_job(idx):
        kind, a, b = xbar_jobs[idx]
        sl = idx % 2
        wt = t_wtr[sl]
        xtsem = "sXT0" if sl == 0 else "sXT1"
        wtsem = "sWT0" if sl == 0 else "sWT1"
        W("sp", wtsem, tok_wt.get(idx - 2))
        if kind == "w1":
            W("sp", "sWQ0", tok_wqst[30])
            W("sp", "sWQ1", tok_wqst[31])
            src_ = d_w1q[b * 2048:(b + 1) * 2048, a * 128:(a + 1) * 128]
            tok_x = DO("sp", lambda src_=src_, wt=wt: SP.dma_start(
                out=wt[:, :], in_=src_, transpose=True), xtsem, 16)
            W("sp", xtsem, tok_x)
            tok_wt[idx] = DO("sp", lambda a=a, b=b, wt=wt: SP.dma_start(
                out=d_w1qT[b * 16:(b + 1) * 16, :, a, :].rearrange(
                    "hc p m -> p hc m"),
                in_=wt[:, :].rearrange("p (hc m) -> p hc m", m=128)),
                wtsem, 16)
        else:
            W("sp", "sWQ0", tok_wqst[62])
            W("sp", "sWQ1", tok_wqst[63])
            src_ = d_w2q[:, a * 128:(a + 1) * 128]
            tok_x = DO("sp", lambda src_=src_, wt=wt: SP.dma_start(
                out=wt[:, 0:1024], in_=src_, transpose=True), xtsem, 16)
            W("sp", xtsem, tok_x)
            tok_wt[idx] = DO("sp", lambda a=a, wt=wt: SP.dma_start(
                out=d_w2qT[a, :, :], in_=wt[:, 0:1024]), wtsem, 16)

    tok_stt_prev = None
    tok_copy_prev = None
    tok_av_final_prev = None
    pend_ln = [None]
    tok_stt_pair = {0: None, 1: None}
    pend = [None]
    tok_bc_last = [None]
    tok_ln_last = [None]
    tok_exp = {}
    tok_av = {}
    last_scores_tok = None
    pit = 0
    wprep_i = 0

    def emit_pend_ln():
        if pend_ln[0] is None:
            return
        tok_lcp = pend_ln[0]
        pend_ln[0] = None
        W("ac", "dv", tok_lcp)
        tok_ln = DO("ac", lambda: A.activation(
            t_slx[:, :], t_sll[:, :], AF.Ln, bias=0.0, scale=1.0), "ac")
        W("ac", "ac", tok_ln)
        W("ac", "pe", tok_bc_last[0])
        tok_recip = DO("ac", lambda: A.activation(
            t_slr[:, :], t_slx[:, :], AF.Exp, bias=0.0, scale=-1.0), "ac")
        tok_ln_last[0] = tok_recip
        if pend[0] is not None:
            pend[0]["tok_recip"] = tok_recip

    def flush_tail(tok_pa_free):
        nonlocal tok_stt_prev, tok_copy_prev
        p = pend[0]
        if p is None:
            return
        pend[0] = None
        W("pe", "ac", p["tok_recip"])
        W("pe", "ac", tok_pa_free)
        W("pe", "dv", tok_copy_prev)
        tok_bc = DO("pe", lambda: T.matmul(pA[:, 0:512], c_sel[:, :],
                                           t_slr[:, :], start=True, stop=True),
                    "pe")
        tok_bc_last[0] = tok_bc
        W("dv", "pe", tok_bc)
        W("dv", "dv", tok_stt_prev)
        tok_copy = DO("dv", lambda: V.tensor_copy(t_slb[:, :], pA[:, 0:512]), "dv")
        W("dv", "dv", tok_copy)
        avA, avB = p["avA"], p["avB"]
        DO("dv", lambda avA=avA, p=p: V.scalar_tensor_tensor(
            t_o[0:64, p["qsub"], p["it"] * 512:(p["it"] + 1) * 512],
            avA[0:64, :], 1.0, t_slb[0:64, :], ALU.mult, ALU.mult), "dv")
        tok_stt = DO("dv", lambda avB=avB, p=p: V.scalar_tensor_tensor(
            t_o[64:128, p["qsub"], p["it"] * 512:(p["it"] + 1) * 512],
            avB[0:64, :], 1.0, t_slb[64:128, :], ALU.mult, ALU.mult), "dv")
        tok_stt_pair[p["parity"]] = ("dv", tok_stt)
        tok_stt_prev = tok_stt
        tok_copy_prev = tok_copy

    tok_stt_pair[0] = ("ac", tok_xf[(31, 1)])
    tok_stt_pair[1] = ("ac", tok_xf[(31, 0)])
    for g in range(4):
        emit_pend_ln()
        flush_tail(tok_exp.get((pit - 1, 30)))
        kv_par = 0
        tok_kv_ev = {}
        for r in range(2):
            for jt in range(8):
                ps = pC[:, 0:512] if kv_par % 2 == 0 else pB[:, 0:512]
                W("pe", "dv", tok_stt_prev)
                W("pe", "ac", tok_exp.get((pit - 1, 31)))
                W("pe", "dv", tok_copy_prev)
                W("pe", "gWB", tok_wall[0])
                W("pe", "ac", tok_lat_all)
                W("pe", "dv", tok_b2b)
                W("pe", "dv", tok_kv_ev.get(kv_par - 2))
                for lc in range(2):
                    tk = DO("pe", lambda g=g, r=r, jt=jt, lc=lc, ps=ps: T.matmul(
                        ps, t_Wk[:, lc, g * 256 + r * 128: g * 256 + r * 128 + 128],
                        t_lat[:, lc, jt * 512:(jt + 1) * 512],
                        start=(lc == 0), stop=(lc == 1)), "pe" if lc == 1 else None)
                W("dv", "pe", tk)
                W("dv", "pe", last_scores_tok)
                tok_kv_ev[kv_par] = DO("dv", lambda g=g, r=r, jt=jt, ps=ps:
                                       V.tensor_scalar(
                                           t_kg[:, r, jt * 512:(jt + 1) * 512], ps,
                                           c_ball[:, 10 + 2 * g + r: 11 + 2 * g + r],
                                           None, ALU.add), "dv")
                kv_par += 1
        for jc in range(32):
            ps = pC[:, 0:256] if kv_par % 2 == 0 else pB[:, 0:256]
            W("pe", "dv", tok_kv_ev.get(kv_par - 2))
            for lc in range(2):
                tk = DO("pe", lambda g=g, jc=jc, lc=lc, ps=ps: T.matmul(
                    ps, t_lat[:, lc, jc * 128:(jc + 1) * 128],
                    t_Wv[:, lc, g * 256:(g + 1) * 256],
                    start=(lc == 0), stop=(lc == 1)), "pe" if lc == 1 else None)
            W("dv", "pe", tk)
            W("dv", "pe", tok_av_final_prev)
            W("dv", "dv", tok_ones)
            tok_kv_ev[kv_par] = DO("dv", lambda jc=jc, ps=ps: V.tensor_copy(
                t_vg[:, jc, :, 0:64],
                ps.rearrange("p (hh d) -> p hh d", d=64)), "dv")
            kv_par += 1
        tok_kv_all = tok_kv_ev[kv_par - 1]

        for hp in range(2):
            ksub = hp
            qsub = 2 * g + hp
            hhA, hhB = 2 * hp, 2 * hp + 1
            for it in range(4):
                tok_sc = {}
                parity = pit % 2
                avA = pC[:, 0:512] if parity == 0 else pB[:, 1024:1536]
                avB = pA[:, 1024:1536] if parity == 0 else pD[:, 0:512]
                for gr in range(34):
                    if gr == 2:
                        emit_pend_ln()
                    if gr == 4:
                        flush_tail(tok_exp[(pit, 2)])
                    if gr < 32:
                        sb = pA if gr % 2 == 0 else pB
                        W("pe", "ac", tok_exp.get((pit, gr - 2)))
                        if gr == 4:
                            W("pe", "dv", tok_copy_prev)
                        if gr <= 1:
                            W("pe", "ac", tok_exp.get((pit - 1, 30 + gr)))
                            W("pe", "dv", tok_kv_all)
                            W("pe", "ac", tok_q_all)
                        DO("pe", lambda sb=sb, gr=gr, ksub=ksub, qsub=qsub, it=it:
                           T.matmul(sb[:, 0:512],
                                    t_kg[0:64, ksub, gr * 128:(gr + 1) * 128],
                                    t_q[0:64, qsub, it * 512:(it + 1) * 512],
                                    start=True, stop=True))
                        tok_sc[gr] = DO("pe", lambda sb=sb, gr=gr, ksub=ksub,
                                        qsub=qsub, it=it: T.matmul(
                                            sb[:, 512:1024],
                                            t_kg[64:128, ksub, gr * 128:(gr + 1) * 128],
                                            t_q[64:128, qsub, it * 512:(it + 1) * 512],
                                            start=True, stop=True), "pe")
                        W("ac", "pe", tok_sc[gr])
                        ring = tok_av.get((pit, gr - 8 + ((gr - 8) % 2 == 0))) \
                            if gr >= 8 else tok_av.get((pit - 1, gr + 24 + ((gr + 24) % 2 == 0)))
                        W("ac", "pe", ring)
                        tok_exp[(pit, gr)] = DO(
                            "ac", lambda sb=sb, gr=gr: A.activation(
                                t_P[:, gr % 8, :, :].rearrange("p c n -> p (c n)"),
                                sb[:, 0:1024], AF.Exp, bias=0.0, scale=0.125), "ac")
                    if gr >= 2:
                        ga = gr - 2
                        W("pe", "ac", tok_exp[(pit, ga)])
                        if ga == 0:
                            sp_e, sp_t = tok_stt_pair[parity]
                            W("pe", sp_e, sp_t)
                        DO("pe", lambda ga=ga, hhA=hhA, avA=avA: T.matmul(
                            avA[0:65, :], t_vg[:, ga, hhA, :], t_P[:, ga % 8, 0, :],
                            start=(ga == 0), stop=(ga == 31)))
                        tk = DO("pe", lambda ga=ga, hhB=hhB, avB=avB: T.matmul(
                            avB[0:65, :], t_vg[:, ga, hhB, :], t_P[:, ga % 8, 1, :],
                            start=(ga == 0), stop=(ga == 31)),
                            "pe" if ga % 2 == 1 else None)
                        if ga % 2 == 1:
                            tok_av[(pit, ga)] = tk
                tok_av_final = tok_av[(pit, 31)]
                last_scores_tok = tok_sc[31]
                W("dv", "pe", tok_av_final)
                W("dv", "ac", tok_ln_last[0])
                DO("dv", lambda avA=avA: V.tensor_copy(t_sll[0:1, :],
                                                       avA[64:65, :]))
                tok_lcp = DO("dv", lambda avB=avB: V.tensor_copy(
                    t_sll[64:65, :], avB[64:65, :]), "dv")
                pend_ln[0] = tok_lcp
                pend[0] = {"avA": avA, "avB": avB, "parity": parity,
                           "qsub": qsub, "it": it, "tok_recip": None}
                tok_av_final_prev = tok_av_final
                for j in (2 * wprep_i - 2, 2 * wprep_i - 1):
                    if 0 <= j < 64:
                        emit_wquant_comp(j)
                for j in (2 * wprep_i, 2 * wprep_i + 1):
                    if j < 64:
                        emit_wquant_load(j)
                if wprep_i >= 16:
                    for _ in range(2):
                        if xbar_i[0] < 16:
                            emit_xbar_job(xbar_i[0])
                            xbar_i[0] += 1
                wprep_i += 1
                pit += 1

    tok_womm = {}
    tok_sumev = {}
    tok_rmred = {}
    tok_x32 = {}
    tok_xst = {}

    def ln_pre(sum_ap, bn, bnx):
        bn_m = bn[:, 0:12].rearrange("p (a b) -> p a b", b=3)[:, :, 1]
        bn_cv = bn[:, 0:12].rearrange("p (a b) -> p a b", b=3)[:, :, 2]
        DO("dv", lambda: V.bn_stats(bn[:, 0:6], sum_ap[:, 0:512]))
        t1 = DO("dv", lambda: V.bn_stats(bn[:, 6:12], sum_ap[:, 512:1024]), "dv")
        W("dv", "dv", t1)
        DO("dv", lambda: V.tensor_reduce(bnx[:, 0:1], bn_m, AX.X, ALU.add))
        t2 = DO("dv", lambda: V.tensor_tensor(bnx[:, 8:12], bn_m, bn_m,
                                              ALU.mult), "dv")
        W("dv", "dv", t2)
        DO("dv", lambda: V.scalar_tensor_tensor(bnx[:, 12:16], bnx[:, 8:12], 256.0,
                                                bn_cv, ALU.mult, ALU.add))
        t3 = DO("dv", lambda: V.tensor_single_scalar(bnx[:, 2:3], bnx[:, 0:1],
                                                     0.25, ALU.mult), "dv")
        W("dv", "dv", t3)
        DO("dv", lambda: V.tensor_reduce(bnx[:, 1:2], bnx[:, 12:16], AX.X, ALU.add))
        t4 = DO("dv", lambda: V.tensor_tensor(bnx[:, 3:4], bnx[:, 2:3], bnx[:, 2:3],
                                              ALU.mult), "dv")
        W("dv", "dv", t4)
        t5 = DO("dv", lambda: V.scalar_tensor_tensor(bnx[:, 4:5], bnx[:, 1:2],
                                                     1.0 / 1024.0, bnx[:, 3:4],
                                                     ALU.mult, ALU.subtract), "dv")
        W("ac", "dv", t5)
        t6 = DO("ac", lambda: A.activation(bnx[:, 5:6], bnx[:, 4:5], AF.Ln,
                                           bias=c_eps[:, :], scale=1.0), "ac")
        W("ac", "ac", t6)
        t7 = DO("ac", lambda: A.activation(bnx[:, 6:7], bnx[:, 5:6], AF.Exp,
                                           bias=0.0, scale=-0.5), "ac")
        return t7, t5

    def ln_post(sum_ap, out_ap, bnx, tok_rstd):
        W("dv", "ac", tok_rstd)
        t8 = DO("dv", lambda: V.scalar_tensor_tensor(bnx[:, 7:8], bnx[:, 2:3], -1.0,
                                                     bnx[:, 6:7], ALU.mult,
                                                     ALU.mult), "dv")
        W("dv", "dv", t8)
        t9 = DO("dv", lambda: V.tensor_scalar(out_ap, sum_ap, bnx[:, 6:7],
                                              bnx[:, 7:8], ALU.mult, ALU.add), "dv")
        return t9

    emit_pend_ln()
    flush_tail(tok_exp.get((pit - 1, 30)))
    emit_wquant_comp(62)
    emit_wquant_comp(63)
    W("sp", "pe", tok_av_final_prev)
    W("sp", "dv", tok_stt_prev)
    W("gp", "pe", tok_av_final_prev)
    W("gp", "dv", tok_stt_prev)
    tok_pre = {}

    def p3_post(tc):
        sl = tc % 2
        W("dv", "sXS0" if sl == 0 else "sXS1", tok_xst.get(tc - 2))
        tok_x32[tc] = ln_post(t_sum[sl][:, :], t_x32[sl][:, :], t_bnx[sl],
                              tok_pre[tc][0])
        W("dv", "dv", tok_x32[tc])
        tok_rmred[tc] = DO("dv", lambda sl=sl, tc=tc: V.tensor_reduce(
            t_rm[:, tc:tc + 1], t_x32[sl][:, :], AX.X, ALU.max,
            apply_absolute_value=True), "dv")
        W("gp", "dv", tok_x32[tc])
        xssem = "sXS0" if sl == 0 else "sXS1"
        tok_xst[tc] = DO("gp", lambda sl=sl, tc=tc: GP.dma_start(
            out=d_xln[tc * 128:(tc + 1) * 128, :], in_=t_x32[sl][:, :]), xssem, 16)

    w2x_tr = {}
    w2x_ev = {}
    w2x_st = {}

    w2x_ld = {}

    def emit_w2x_load(rc):
        sl = rc % 2
        xtsem = "sXT0" if sl == 0 else "sXT1"
        if rc == 0:
            W("sp", "pe", tok_av_final_prev)
        if rc >= 2:
            W("sp", "pe", w2x_tr[(rc - 2, 7)])
        W("sp", "sWQ0", tok_wqst[34 + 4 * rc])
        W("sp", "sWQ1", tok_wqst[35 + 4 * rc])
        w2x_ld[rc] = DO("sp", lambda rc=rc, sl=sl: SP.dma_start(
            out=t_w2l[sl][:, :], in_=d_w2q[rc * 128:(rc + 1) * 128, :]), xtsem, 16)

    def emit_w2x_comp(rc):
        sl = rc % 2
        xtsem = "sXT0" if sl == 0 else "sXT1"
        wtsem = "sWT0" if sl == 0 else "sWT1"
        if rc == 0:
            W("ac", "sWT0", tok_wt[14])
            W("ac", "sWT1", tok_wt[15])
        for g in range(8):
            pbank = pD if g % 2 == 0 else pC
            W("pe", xtsem, w2x_ld[rc])
            W("pe", "ac", w2x_ev.get((rc, g - 2)))
            if rc > 0 and g < 2:
                W("pe", "ac", w2x_ev.get((rc - 1, 6 + g)))
            for hh in range(4):
                h = g * 4 + hh
                tk = DO("pe", lambda sl=sl, h=h, hh=hh, pbank=pbank: T.matmul(
                    pbank[:, hh * 128:(hh + 1) * 128],
                    t_w2l[sl][:, h * 128:(h + 1) * 128],
                    c_idb[:, :], start=True, stop=True), "pe" if hh == 3 else None)
            w2x_tr[(rc, g)] = tk
            W("ac", "pe", tk)
            W("ac", wtsem, w2x_st.get(rc - 2))
            w2x_ev[(rc, g)] = DO("ac", lambda sl=sl, g=g, pbank=pbank:
                                 A.activation(
                                     t_w2x[sl][:, g * 4:(g + 1) * 4, :].rearrange(
                                         "p a b -> p (a b)"),
                                     pbank[:, :], AF.Identity, bias=0.0,
                                     scale=1.0), "ac")
        W("sp", "ac", w2x_ev[(rc, 7)])
        w2x_st[rc] = DO("sp", lambda rc=rc, sl=sl: SP.dma_start(
            out=d_w2qT[:, :, rc * 128:(rc + 1) * 128].rearrange("a p b -> p a b"),
            in_=t_w2x[sl][:, :, :]), wtsem, 16)

    tok_x3 = {}
    for pf in range(2):
        tok_x3[pf] = DO("gp", lambda pf=pf: GP.dma_start(
            out=t_xr[pf][:, :], in_=d_x[pf * 128:(pf + 1) * 128, :]),
            "sX320" if pf == 0 else "sX321", 16)
    for tc in range(16):
        sl = tc % 2
        buf = pA if tc % 2 == 0 else pB
        x3sem = "sX320" if sl == 0 else "sX321"
        xssem = "sXS0" if sl == 0 else "sXS1"
        W("pe", "dv", tok_stt_prev)
        W("pe", "dv", tok_b2b)
        W("pe", "dv", tok_sumev.get(tc - 2))
        for dn in range(2):
            for kc in range(8):
                DO("pe", lambda buf=buf, dn=dn, kc=kc, tc=tc: T.matmul(
                    buf[:, dn * 512:(dn + 1) * 512],
                    t_o[:, kc, tc * 128:(tc + 1) * 128],
                    t_Wo[:, kc, dn * 512:(dn + 1) * 512],
                    start=(kc == 0), stop=False))
            tk = DO("pe", lambda buf=buf, dn=dn: T.matmul(
                buf[:, dn * 512:(dn + 1) * 512], c_1b[0:1, :],
                c_bob[0:1, dn * 512:(dn + 1) * 512], start=False, stop=True),
                "pe" if dn == 1 else None)
        tok_womm[tc] = tk
        W("dv", "pe", tok_womm[tc])
        W("dv", x3sem, tok_x3[tc])
        W("dv", "dv", tok_x32.get(tc - 2))
        tok_sumev[tc] = DO("dv", lambda buf=buf, sl=sl: V.tensor_tensor(
            t_sum[sl][:, :], buf[:, 0:1024], t_xr[sl][:, :], ALU.add), "dv")
        W("dv", "dv", tok_sumev[tc])
        if tc + 2 < 16:
            W("gp", "dv", tok_sumev[tc])
            tok_x3[tc + 2] = DO("gp", lambda tc=tc, sl=sl: GP.dma_start(
                out=t_xr[sl][:, :], in_=d_x[(tc + 2) * 128:(tc + 3) * 128, :]),
                x3sem, 16)
        if tc % 2 == 1:
            k = (tc - 1) // 2
            if k == 0:
                emit_w2x_load(0)
            if k + 1 < 8:
                emit_w2x_load(k + 1)
            emit_w2x_comp(k)
        tok_pre[tc] = ln_pre(t_sum[sl][:, :], t_bn[sl], t_bnx[sl])
        if tc >= 1:
            p3_post(tc - 1)
        if xbar_i[0] < len(xbar_jobs):
            emit_xbar_job(xbar_i[0])
            xbar_i[0] += 1
    p3_post(15)

    W("dv", "dv", tok_rmred[15])
    tok_rmc = DO("dv", lambda: V.tensor_reduce(c_wred[:, 0:1], t_rm[:, 0:16],
                                               AX.X, ALU.max), "dv")
    W("pe", "dv", tok_rmc)
    tok_rmT = DO("pe", lambda: T.transpose(pB[0:1, 64:192], c_wred[:, 0:1],
                                           c_idf[:, :]), "pe")
    W("dv", "pe", tok_rmT)
    tok_lmax = DO("dv", lambda: V.tensor_reduce(c_sc[0:1, 10:11], pB[0:1, 64:192],
                                                AX.X, ALU.max), "dv")
    W("dv", "dv", tok_lmax)
    tok_s8 = DO("dv", lambda: V.tensor_scalar(c_s8[0:1, :], c_1f[0:1, 0:8],
                                              c_sc[0:1, 10:11], None, ALU.mult),
                "dv")
    W("gp", "dv", tok_s8)
    tok_c1a = DO("gp", lambda: GP.dma_start(out=d_cc1i, in_=c_s8[0:1, :]), "sCC", 16)
    W("gp", "sCC", tok_c1a)
    tok_cc1 = DO("gp", lambda: GP.collective_compute(
        "AllReduce", ALU.max, replica_groups=[list(range(cc_ranks))],
        ins=[d_cc1i], outs=[d_cc1o]), "gc")

    p3_pe = tok_womm[15]
    W("dv", "sWQ0", tok_wqst[62])
    W("dv", "sWQ1", tok_wqst[63])
    W("pe", "sWQ0", tok_wqst[62])
    W("pe", "sWQ1", tok_wqst[63])
    tok_tr = {}
    tok_tev = {(-1, 0): w2x_ev[(7, 6)], (-1, 1): w2x_ev[(7, 7)]}
    tok_xl5t = {}
    for tc in range(16):
        sl = tc % 2
        xlsem = "sXL0" if sl == 0 else "sXL1"
        xssem = "sXS0" if tc % 2 == 0 else "sXS1"
        if tc >= 2:
            W("sp", "pe", tok_tr[(tc - 2, 1)])
        W("sp", xssem, tok_xst[tc])
        tok_xl5t[tc] = DO("sp", lambda tc=tc, sl=sl: SP.dma_start(
            out=t_xl5[sl][:, :], in_=d_xln[tc * 128:(tc + 1) * 128, :]), xlsem, 16)
        if tc == 14:
            W("ac", "sWT0", w2x_st[6])
            W("ac", "sWT1", w2x_st[7])
        for half, pbank in ((0, pD), (1, pC)):
            W("pe", "ac", tok_tev.get((tc - 1, half)))
            W("pe", xlsem, tok_xl5t[tc])
            for kc4 in range(4):
                kc = half * 4 + kc4
                tk = DO("pe", lambda sl=sl, kc=kc, kc4=kc4, pbank=pbank: T.transpose(
                    pbank[:, kc4 * 128:(kc4 + 1) * 128],
                    t_xl5[sl][:, kc * 128:(kc + 1) * 128],
                    c_idf[:, :]), "pe" if kc4 == 3 else None)
            tok_tr[(tc, half)] = tk
            W("ac", "pe", tok_tr[(tc, half)])
            tok_tev[(tc, half)] = DO("ac", lambda tc=tc, half=half, pbank=pbank:
                                     A.activation(
                                         t_xf5[tc][:, half * 4:half * 4 + 4, :]
                                         .rearrange("p a b -> p (a b)"),
                                         pbank[:, :], AF.Identity, bias=0.0,
                                         scale=1.0), "ac")

    W("gp", "gc", tok_cc1)
    tok_c1b = DO("gp", lambda: GP.dma_start(out=c_sc[0:1, 10:11],
                                            in_=d_cc1o[0:1, 0:1]), "sCC", 16)
    W("dv", "sCC", tok_c1b)
    t = DO("dv", lambda: V.tensor_single_scalar(c_sc[0:1, 11:12], c_sc[0:1, 10:11],
                                                1e-6, ALU.max), "dv")
    W("ac", "dv", t)
    t = DO("ac", lambda: A.activation(c_rc[0:1, 0:1], c_sc[0:1, 11:12], AF.Ln,
                                      bias=0.0, scale=1.0), "ac")
    W("ac", "ac", t)
    t = DO("ac", lambda: A.activation(c_sc[0:1, 12:13], c_rc[0:1, 0:1], AF.Exp,
                                      bias=0.0, scale=-1.0), "ac")
    W("dv", "ac", t)
    DO("dv", lambda: V.tensor_single_scalar(c_sc[0:1, 13:14], c_sc[0:1, 12:13],
                                            127.0, ALU.mult))
    t = DO("dv", lambda: V.tensor_single_scalar(c_sc[0:1, 14:15], c_sc[0:1, 11:12],
                                                1.0 / 127.0, ALU.mult), "dv")
    W("dv", "dv", t)
    tok_s1s = DO("dv", lambda: V.tensor_tensor(c_sc[0:1, 15:16], c_sc[0:1, 14:15],
                                               c_sc[0:1, 8:9], ALU.mult), "dv")
    W("pe", "dv", tok_s1s)
    W("pe", "ac", tok_tev[(15, 1)])
    DO("pe", lambda: T.matmul(pC[:, 322:323], c_1f[0:1, :], c_sc[0:1, 13:14],
                              start=True, stop=True))
    tok_sbc = DO("pe", lambda: T.matmul(pC[:, 323:324], c_1f[0:1, :],
                                        c_sc[0:1, 15:16], start=True, stop=True),
                 "pe")
    W("dv", "pe", tok_sbc)
    DO("dv", lambda: V.tensor_copy(c_sxr[:, :], pC[:, 322:323]))
    tok_scb = DO("dv", lambda: V.tensor_copy(c_s1[:, :], pC[:, 323:324]), "dv")
    W("dv", "dv", tok_scb)
    tok_b1s = DO("dv", lambda: V.tensor_scalar(c_b1s[:, :], c_b1r[:, :],
                                               c_s1[:, :], None, ALU.mult), "dv")

    tok_xqq = {}
    for tc in range(16):
        W("dv", "dv", tok_scb)
        W("dv", "dv", tok_xqq.get(tc - 1))
        W("dv", "ac", tok_tev[(tc, 1)])
        t1 = DO("dv", lambda tc=tc: V.tensor_scalar(
            t_qf16[:, :], t_xf5[tc][:, :, :].rearrange("p a b -> p (a b)"),
            c_sxr[:, :], MAGIC, ALU.mult, ALU.add), "dv")
        W("dv", "dv", t1)
        tok_xqq[tc] = DO("dv", lambda tc=tc: V.tensor_scalar(
            t_xq[:, :, tc * 128:(tc + 1) * 128],
            t_qf16[:, :].rearrange("p (kc n) -> p kc n", n=128),
            -MAGIC, None, ALU.add), "dv")
    tok_xq_all = tok_xqq[15]

    tok_gelu = {}
    tok_hst = {}
    tok_hred = {}
    tok_ffmm = {}
    fidx = 0
    for Hc in range(32):
        sl = Hc % 2
        w1sem = "sW10" if sl == 0 else "sW11"
        W("sp", "pe", tok_ffmm.get((Hc - 2, 3)))
        W("sp", "sWT0", tok_wt[14])
        W("sp", "sWT1", tok_wt[15])
        tok_w1 = DO("sp", lambda Hc=Hc, sl=sl: SP.dma_start(
            out=t_w1s[sl][:, :, :], in_=d_w1qT[Hc]), w1sem, 16)
        for tt in range(4):
            slot = fidx % 4
            ps = [pA[:, 0:512], pA[:, 512:1024], pA[:, 1024:1536], pB[:, 0:512]][slot]
            W("pe", w1sem, tok_w1)
            W("pe", "dv", tok_xqq[4 * tt + 3])
            W("pe", "ac", tok_gelu.get(fidx - 4))
            for kc in range(8):
                tk = DO("pe", lambda ps=ps, sl=sl, kc=kc, tt=tt: T.matmul(
                    ps, t_w1s[sl][:, kc, :], t_xq[:, kc, tt * 512:(tt + 1) * 512],
                    start=(kc == 0), stop=(kc == 7)), "pe" if kc == 7 else None)
            tok_ffmm[(Hc, tt)] = tk
            hsl = fidx % 3
            hsem = ("sH0", "sH1", "sH2")[hsl]
            W("ac", "pe", tok_ffmm[(Hc, tt)])
            W("ac", "dv", tok_b1s)
            W("ac", hsem, tok_hst.get(fidx - 3))
            W("ac", "dv", tok_hred.get(fidx - 3))
            tok_gelu[fidx] = DO("ac", lambda ps=ps, hsl=hsl, Hc=Hc: A.activation(
                t_ht[hsl][:, :], ps, AF.Gelu, bias=c_b1s[:, Hc:Hc + 1],
                scale=c_s1[:, :]), "ac")
            W("dv", "ac", tok_gelu[fidx])
            tok_hred[fidx] = DO("dv", lambda hsl=hsl, fidx=fidx: V.tensor_reduce(
                t_hm[:, fidx:fidx + 1], t_ht[hsl][:, :], AX.X, ALU.max,
                apply_absolute_value=True), "dv")
            W("gp", "ac", tok_gelu[fidx])
            tok_hst[fidx] = DO("gp", lambda Hc=Hc, tt=tt, hsl=hsl: GP.dma_start(
                out=d_hst[Hc, :, tt * 512:(tt + 1) * 512], in_=t_ht[hsl][:, :]),
                hsem, 16)
            fidx += 1

    W("dv", "dv", tok_hred[127])
    tok_hmc = DO("dv", lambda: V.tensor_reduce(c_wred[:, 1:2], t_hm[:, :],
                                               AX.X, ALU.max), "dv")
    W("pe", "dv", tok_hmc)
    W("pe", "dv", tok_scb)
    tok_hmT = DO("pe", lambda: T.transpose(pC[0:1, 64:192], c_wred[:, 1:2],
                                           c_idf[:, :]), "pe")
    W("dv", "pe", tok_hmT)
    tok_hmax = DO("dv", lambda: V.tensor_reduce(c_sc[0:1, 10:11], pC[0:1, 64:192],
                                                AX.X, ALU.max), "dv")
    W("dv", "dv", tok_hmax)
    tok_s8b = DO("dv", lambda: V.tensor_scalar(c_s8[0:1, :], c_1f[0:1, 0:8],
                                               c_sc[0:1, 10:11], None, ALU.mult),
                 "dv")
    W("gp", "dv", tok_s8b)
    tok_c2a = DO("gp", lambda: GP.dma_start(out=d_cc2i, in_=c_s8[0:1, :]), "sCC", 16)
    W("gp", "sCC", tok_c2a)
    tok_cc2 = DO("gp", lambda: GP.collective_compute(
        "AllReduce", ALU.max, replica_groups=[list(range(cc_ranks))],
        ins=[d_cc2i], outs=[d_cc2o]), "gc")

    HRSEM = ("sHR0", "sHR1", "sHR2", "sHR3")
    tok_hr = {}
    tok_hq1 = {}
    W("sp", "pe", tok_ffmm[(31, 3)])
    W("sp", "sWT0", w2x_st[6])
    W("sp", "sWT1", w2x_st[7])
    tok_w2sld = DO("sp", lambda: SP.dma_start(
        out=t_w2s[:, :, :], in_=d_w2qT.rearrange("hc p m -> p hc m")), "sW2", 16)
    W("sp", "sH0", tok_hst[126])
    W("sp", "sH1", tok_hst[127])
    W("sp", "sH2", tok_hst[125])
    W("gp", "pe", tok_ffmm[(31, 3)])
    for tc in range(4):
        q = "gp" if tc % 2 == 0 else "sp"
        ENQ = GP if tc % 2 == 0 else SP
        tok_hr[tc] = DO(q, lambda tc=tc, ENQ=ENQ: ENQ.dma_start(
            out=t_hf[tc][:, :, :],
            in_=d_hst[:, :, tc * 128:(tc + 1) * 128].rearrange("hc p d -> p hc d")),
            HRSEM[tc], 16)

    W("sp", "gc", tok_cc2)
    tok_c2b = DO("sp", lambda: SP.dma_start(out=c_sc[0:1, 10:11],
                                            in_=d_cc2o[0:1, 0:1]), "sC2", 16)
    W("dv", "sC2", tok_c2b)
    t = DO("dv", lambda: V.tensor_single_scalar(c_sc[0:1, 11:12], c_sc[0:1, 10:11],
                                                1e-6, ALU.max), "dv")
    W("ac", "dv", t)
    t = DO("ac", lambda: A.activation(c_rc[0:1, 1:2], c_sc[0:1, 11:12], AF.Ln,
                                      bias=0.0, scale=1.0), "ac")
    W("ac", "ac", t)
    t = DO("ac", lambda: A.activation(c_sc[0:1, 12:13], c_rc[0:1, 1:2], AF.Exp,
                                      bias=0.0, scale=-1.0), "ac")
    W("dv", "ac", t)
    DO("dv", lambda: V.tensor_single_scalar(c_sc[0:1, 13:14], c_sc[0:1, 12:13],
                                            127.0, ALU.mult))
    t = DO("dv", lambda: V.tensor_single_scalar(c_sc[0:1, 14:15], c_sc[0:1, 11:12],
                                                1.0 / 127.0, ALU.mult), "dv")
    W("dv", "dv", t)
    tok_s2s = DO("dv", lambda: V.tensor_tensor(c_sc[0:1, 15:16], c_sc[0:1, 14:15],
                                               c_sc[0:1, 9:10], ALU.mult), "dv")
    W("pe", "dv", tok_s2s)
    DO("pe", lambda: T.matmul(pC[:, 324:325], c_1f[0:1, :], c_sc[0:1, 13:14],
                              start=True, stop=True))
    tok_sbc2 = DO("pe", lambda: T.matmul(pC[:, 325:326], c_1f[0:1, :],
                                         c_sc[0:1, 15:16], start=True, stop=True),
                  "pe")
    W("dv", "pe", tok_sbc2)
    DO("dv", lambda: V.tensor_copy(c_shr[:, :], pC[:, 324:325]))
    tok_scb2 = DO("dv", lambda: V.tensor_copy(c_s2[:, :], pC[:, 325:326]), "dv")

    W("sp", "pe", tok_tr[(15, 1)])
    W("dv", "pe", tok_tr[(15, 1)])
    W("pe", "ac", tok_gelu[127])
    tok_hq = {}
    tok_ff2 = {}
    tok_sum6 = {}
    tok_yst = {}
    tok_pre6 = {}
    tok_post6 = {}

    def p6_post(tc):
        sl = tc % 2
        ysem = "sY0" if sl == 0 else "sY1"
        tok_post6[tc] = ln_post(t_out[sl][:, :], t_out[sl][:, :], t_bx6[sl],
                                tok_pre6[tc][0])
        W("sp", "dv", tok_post6[tc])
        tok_yst[tc] = DO("sp", lambda tc=tc, sl=sl: SP.dma_start(
            out=d_y[tc * 128:(tc + 1) * 128, :], in_=t_out[sl][:, :]), ysem, 16)

    def p6_quant(q1):
        sl4q = q1 % 4
        slq = q1 % 2
        W("ac", HRSEM[sl4q], tok_hr[q1])
        W("ac", "dv", tok_scb2)
        W("ac", "dv", tok_hq.get(q1 - 2))
        t1 = DO("ac", lambda sl4q=sl4q, slq=slq: A.activation(
            t_f16b[slq][:, 0:4096],
            t_hf[sl4q][:, :, :].rearrange("p a b -> p (a b)"),
            AF.Identity, bias=c_mag[:, :], scale=c_shr[:, :]), "ac")
        tok_hq1[q1] = t1
        W("dv", "ac", t1)
        W("dv", "pe", tok_ff2.get(q1 - 2))
        tok_hq[q1] = DO("dv", lambda slq=slq: V.tensor_scalar(
            t_hq[slq][:, :, :].rearrange("p a b -> p (a b)"),
            t_f16b[slq][:, 0:4096], -MAGIC, None, ALU.add), "dv")

    p6_quant(0)
    for tc in range(16):
        sl = tc % 2
        sl4 = tc % 4
        buf = pA if tc % 2 == 0 else pB
        xlsem = "sXL0" if sl == 0 else "sXL1"
        ysem = "sY0" if sl == 0 else "sY1"
        W("sp", "dv", tok_sum6.get(tc - 2))
        tok_xl6t = DO("sp", lambda tc=tc, sl=sl: SP.dma_start(
            out=t_xl6[sl][:, :], in_=d_xln[tc * 128:(tc + 1) * 128, :]), xlsem, 16)
        W("pe", "dv", tok_hq[tc])
        W("pe", "sW2", tok_w2sld)
        W("pe", "dv", tok_sum6.get(tc - 2))
        for dn in range(2):
            for hc in range(32):
                DO("pe", lambda buf=buf, dn=dn, hc=hc, sl=sl: T.matmul(
                    buf[:, dn * 512:(dn + 1) * 512], t_hq[sl][:, hc, :],
                    t_w2s[:, hc, dn * 512:(dn + 1) * 512],
                    start=(hc == 0), stop=False))
            tk = DO("pe", lambda buf=buf, dn=dn: T.matmul(
                buf[:, dn * 512:(dn + 1) * 512], c_1b[0:1, :],
                c_b2b[0:1, dn * 512:(dn + 1) * 512], start=False, stop=True),
                "pe" if dn == 1 else None)
        tok_ff2[tc] = tk
        if tc + 1 < 16:
            p6_quant(tc + 1)
        if tc + 4 < 16:
            q = "gp" if (tc + 4) % 2 == 0 else "sp"
            ENQ = GP if (tc + 4) % 2 == 0 else SP
            W(q, "ac", tok_hq1[tc])
            tok_hr[tc + 4] = DO(q, lambda tc=tc, sl4=sl4, ENQ=ENQ: ENQ.dma_start(
                out=t_hf[sl4][:, :, :],
                in_=d_hst[:, :, (tc + 4) * 128:(tc + 5) * 128].rearrange(
                    "hc p d -> p hc d")), HRSEM[sl4], 16)
        W("dv", "pe", tok_ff2[tc])
        W("dv", xlsem, tok_xl6t)
        W("dv", ysem, tok_yst.get(tc - 2))
        W("dv", "dv", tok_post6.get(tc - 2))
        tok_sum6[tc] = DO("dv", lambda buf=buf, sl=sl: V.scalar_tensor_tensor(
            t_out[sl][:, :], buf[:, 0:1024], c_s2[:, :], t_xl6[sl][:, :],
            ALU.mult, ALU.add), "dv")
        W("dv", "dv", tok_sum6[tc])
        tok_pre6[tc] = ln_pre(t_out[sl][:, :], t_bn6[sl], t_bx6[sl])
        if tc >= 1:
            p6_post(tc - 1)
    p6_post(15)

    W("sp", "sY0", tok_yst[14])
    W("sp", "sY1", tok_yst[15])

    sch.emit(block)
    es.close()
    return nc


_NC_CACHE = {}


def _get_nc(r0, cc_ranks=8):
    key = (r0, cc_ranks)
    if key not in _NC_CACHE:
        _NC_CACHE[key] = build_nc(r0, cc_ranks)
    return _NC_CACHE[key]


def kernel(L0_attn_x, L0_attn_Wq, L0_attn_bq, L0_attn_Wl, L0_attn_bl,
           L0_attn_Wk, L0_attn_bk, L0_attn_Wv, L0_attn_bv, L0_attn_Wo,
           L0_attn_bo, L0_ffn_w1, L0_ffn_b1, L0_ffn_w2, L0_ffn_b2):
    x = np.ascontiguousarray(np.asarray(L0_attn_x, dtype=np.float32))
    bcat = np.zeros((32, 128), np.float32)
    bcat[0:8] = np.asarray(L0_attn_bq, np.float32).reshape(8, 128)
    bcat[8:10] = np.asarray(L0_attn_bl, np.float32).reshape(2, 128)
    bcat[10:18] = np.asarray(L0_attn_bk, np.float32).reshape(8, 128)
    bcat[18:26] = np.asarray(L0_attn_bv, np.float32).reshape(8, 128)
    common = {
        "Wq": np.ascontiguousarray(np.asarray(L0_attn_Wq, np.float32)),
        "Wl": np.ascontiguousarray(np.asarray(L0_attn_Wl, np.float32)),
        "Wk": np.ascontiguousarray(np.asarray(L0_attn_Wk, np.float32)),
        "Wv": np.ascontiguousarray(np.asarray(L0_attn_Wv, np.float32)),
        "Wo": np.ascontiguousarray(np.asarray(L0_attn_Wo, np.float32)),
        "bcat": bcat,
        "bo": np.asarray(L0_attn_bo, np.float32).reshape(1, D),
        "w1": np.ascontiguousarray(np.asarray(L0_ffn_w1, np.float32)),
        "b1": np.asarray(L0_ffn_b1, np.float32).reshape(32, 128),
        "w2": np.ascontiguousarray(np.asarray(L0_ffn_w2, np.float32)),
        "b2": np.asarray(L0_ffn_b2, np.float32).reshape(1, D),
        "identb": np.eye(128, dtype=ml_dtypes.bfloat16),
        "identf": np.eye(128, dtype=np.float32),
    }
    in_maps = []
    for c in range(8):
        b, half = c // 2, c % 2
        m = dict(common)
        xb = x[b] if half == 0 else np.roll(x[b], -TO, axis=0)
        m["xfull"] = np.ascontiguousarray(xb)
        in_maps.append(m)
    nc = _get_nc(0, 8)
    res = run_bass_kernel_spmd(nc, in_maps, core_ids=list(range(8)))
    out = np.empty((4, S, D), np.float32)
    for c in range(8):
        b, half = c // 2, c % 2
        out[b, half * TO:(half + 1) * TO] = res.results[c]["y"]
    return out
